# version 10
# baseline (speedup 1.0000x reference)
import math
import numpy as np
import ml_dtypes
import concourse.bass as bass
import concourse.mybir as mybir
from concourse.bass_core_rust import IndirectOffsetOnAxis
from concourse.bass_utils import run_bass_kernel_spmd

F32 = mybir.dt.float32
BF16 = mybir.dt.bfloat16
U32 = mybir.dt.uint32
AF = mybir.ActivationFunctionType
ALU = mybir.AluOpType
AXX = mybir.AxisListType.X

SEQ = 2048
DM = 2048
NF = 16
NT = 16
HLOC = 8
HD = 128
NCH = 16
DFFH = 4096
KSEL = 256
OWN = 1024
TRASH = 256
EPS = 1e-5

ENGINES = ("tensor", "vector", "scalar", "sync", "gpsimd")


def _slopes(n_heads):
    start = 2.0 ** (-(2.0 ** (-(math.log2(n_heads) - 3))))
    return np.array([start ** (i + 1) for i in range(n_heads)], dtype=np.float64)


class Prog:
    def __init__(self):
        self.ops = {e: [] for e in ENGINES}

    def t(self, fn):
        self.ops["tensor"].append(fn)

    def v(self, fn):
        self.ops["vector"].append(fn)

    def s(self, fn):
        self.ops["scalar"].append(fn)

    def y(self, fn):
        self.ops["sync"].append(fn)

    def g(self, fn):
        self.ops["gpsimd"].append(fn)


class Sem:
    def __init__(self, nc, name):
        self.h = nc.alloc_semaphore(name=name)
        self.n = 0

    def inc(self, k=1):
        self.n += k
        return self.n


_CHAIN_OPS = frozenset((
    "tensor_copy", "tensor_tensor", "tensor_scalar", "tensor_single_scalar",
    "tensor_add", "tensor_sub", "tensor_mul", "tensor_max", "tensor_relu",
    "tensor_scalar_add", "tensor_scalar_sub", "tensor_scalar_mul",
    "tensor_reduce", "reduce_sum", "reduce_max", "scalar_tensor_tensor",
    "tensor_tensor_scan", "memset", "memzero", "reciprocal", "select",
    "copy_predicated", "activation", "copy", "add", "mul", "sqrt", "square", "sign",
))


class ChainIns:

    def __init__(self, chain, n):
        self._c = chain
        self._n = n

    def then_inc(self, sem, val=1):
        self._c._e.wait_ge(self._c._sem.h, self._n)
        self._c._e.sem_inc(sem, val)
        return self


class Chained:

    def __init__(self, eng, sem):
        self._e = eng
        self._sem = sem

    def __getattr__(self, name):
        attr = getattr(self._e, name)
        if name in _CHAIN_OPS:
            def wrapped(*a, **k):
                self._e.wait_ge(self._sem.h, self._sem.n)
                ins = attr(*a, **k)
                self._sem.inc()
                ins.then_inc(self._sem.h, 1)
                return ChainIns(self, self._sem.n)
            return wrapped
        return attr


def build_program(upto=99, debug=()):
    nc = bass.Bass("TRN2", target_bir_lowering=False)
    P = Prog()

    def edge(eng, ins, sem):
        k = sem.inc()
        ins.then_inc(sem.h, 1)
        eng.wait_ge(sem.h, k)

    dx = nc.dram_tensor("x", [SEQ, DM], F32, kind="ExternalInput").ap()
    dwqkv = nc.dram_tensor("wqkvT", [24, 128, NF, 128], F32, kind="ExternalInput").ap()
    dbqkv = nc.dram_tensor("bqkv", [128, 24], F32, kind="ExternalInput").ap()
    dwout = nc.dram_tensor("woutT", [4, 128, NF, 512], F32, kind="ExternalInput").ap()
    dW1 = nc.dram_tensor("W1t", [32, 128, NF, 128], F32, kind="ExternalInput").ap()
    db1 = nc.dram_tensor("b1h", [128, 32], F32, kind="ExternalInput").ap()
    dW2 = nc.dram_tensor("W2cm", [16, 128, 32, 128], F32, kind="ExternalInput").ap()
    db2 = nc.dram_tensor("b2c", [128, 16], F32, kind="ExternalInput").ap()
    dWr = nc.dram_tensor("Wrc", [128, NF], F32, kind="ExternalInput").ap()
    dbrow = nc.dram_tensor("boutr", [1, DM], F32, kind="ExternalInput").ap()
    donesr = nc.dram_tensor("onesr", [1, 256], F32, kind="ExternalInput").ap()
    dident = nc.dram_tensor("identc", [128, 128], F32, kind="ExternalInput").ap()
    dtri = nc.dram_tensor("tric", [128, 128], F32, kind="ExternalInput").ap()
    donesm = nc.dram_tensor("onesmc", [128, 128], F32, kind="ExternalInput").ap()
    diob = nc.dram_tensor("iobc", [128, 256], F32, kind="ExternalInput").ap()
    diop = nc.dram_tensor("iopc", [128, 1], F32, kind="ExternalInput").ap()
    donescol = nc.dram_tensor("onescolc", [128, 1], F32, kind="ExternalInput").ap()
    deps = nc.dram_tensor("epscolc", [128, 1], F32, kind="ExternalInput").ap()
    ddmask = nc.dram_tensor("dmaskc", [128, HLOC, 128], F32, kind="ExternalInput").ap()
    ddqb = nc.dram_tensor("dqbc", [128, HLOC, 128], F32, kind="ExternalInput").ap()
    ddk = nc.dram_tensor("dkc", [128, HLOC], F32, kind="ExternalInput").ap()
    dlamc = nc.dram_tensor("lamcc", [128, HLOC], F32, kind="ExternalInput").ap()
    dcc1i = nc.dram_tensor("cc1idx", [128, 16], F32, kind="ExternalInput").ap()
    dcc2i = nc.dram_tensor("cc2idx", [128, 4], F32, kind="ExternalInput").ap()
    dhofs = nc.dram_tensor("hofsc", [1, 1], F32, kind="ExternalInput").ap()
    dxown = nc.dram_tensor("xown", [OWN, DM], F32, kind="ExternalInput").ap()
    dxni = nc.dram_tensor("xnidx", [128, 32], F32, kind="ExternalInput").ap()

    dy = nc.dram_tensor("y", [OWN + TRASH, DM], F32, kind="ExternalOutput").ap()

    dxn = nc.dram_tensor("xn_dram", [SEQ, DM], F32).ap()
    dlrow = nc.dram_tensor("lrow_dram", [1, SEQ], F32).ap()
    dhbuf = nc.dram_tensor("h_dram", [32, 128, KSEL], BF16).ap()
    dysc = nc.dram_tensor("ysc_dram", [KSEL, DM], F32).ap()
    dW1bf = nc.dram_tensor("w1bf", [32, 128, NF, 128], BF16).ap()
    dW2bf = nc.dram_tensor("w2bf", [64, 128, 8, 128], BF16).ap()
    dwoutbf = nc.dram_tensor("woutbf", [16, 128, 4, 512], BF16).ap()
    dcc1inA = nc.dram_tensor("cc1_inA", [512, SEQ], BF16).ap()
    dcc1inB = nc.dram_tensor("cc1_inB", [512, SEQ], BF16).ap()
    dcc1outA = nc.dram_tensor("cc1_outA", [4096, SEQ], BF16).ap()
    dcc1outB = nc.dram_tensor("cc1_outB", [4096, SEQ], BF16).ap()
    dcc2inH = [nc.dram_tensor("cc2_in%d" % i, [KSEL, DM // 2], F32).ap() for i in range(2)]
    dcc2outH = [nc.dram_tensor("cc2_out%d" % i, [8 * KSEL, DM // 2], F32).ap() for i in range(2)]

    dbg = {}
    if "xnfm" in debug:
        dbg["d_xnfm"] = nc.dram_tensor("d_xnfm", [128, NF * SEQ], F32, kind="ExternalOutput").ap()
    if "router" in debug:
        dbg["d_ltm"] = nc.dram_tensor("d_ltm", [128, 16], F32, kind="ExternalOutput").ap()
        dbg["d_rank"] = nc.dram_tensor("d_rank", [128, 16], F32, kind="ExternalOutput").ap()
        dbg["d_slott"] = nc.dram_tensor("d_slott", [128, 2], F32, kind="ExternalOutput").ap()
        dbg["d_rw"] = nc.dram_tensor("d_rw", [128, 2], F32, kind="ExternalOutput").ap()
        dbg["d_sci"] = nc.dram_tensor("d_sci", [128, 2], F32, kind="ExternalOutput").ap()
        dbg["d_fxfm"] = nc.dram_tensor("d_fxfm", [128, NF * KSEL], F32, kind="ExternalOutput").ap()
    if "ffn" in debug:
        dbg["d_h"] = nc.dram_tensor("d_h", [128, 32 * KSEL], BF16, kind="ExternalOutput").ap()
        dbg["d_ysc"] = nc.dram_tensor("d_ysc", [KSEL, DM], F32, kind="ExternalOutput").ap()
        dbg["d_aog"] = nc.dram_tensor("d_aog", [128, 16 * 1024], BF16, kind="ExternalOutput").ap()
    if "attn" in debug:
        dbg["d_attn"] = nc.dram_tensor("d_attn", [128, HLOC * 1024], BF16, kind="ExternalOutput").ap()
        dbg["d_cc1"] = nc.dram_tensor("d_cc1", [1024, SEQ], BF16, kind="ExternalOutput").ap()
        dbg["d_qf"] = nc.dram_tensor("d_qf", [128, 4 * SEQ], BF16, kind="ExternalOutput").ap()
        dbg["d_kf"] = nc.dram_tensor("d_kf", [128, 4 * SEQ], BF16, kind="ExternalOutput").ap()
        dbg["d_vf"] = nc.dram_tensor("d_vf", [128, 4 * SEQ], BF16, kind="ExternalOutput").ap()

    S = {}
    for name in (
        "cst", "xld", "vmean", "ssq", "vxn", "xnst", "tp", "etp",
        "lgm", "lge", "rkm", "rke", "fxg", "fxt",
        "wqld", "qkm", "qke", "w1ld", "hm", "he", "hst",
        "scn", "sce", "scs", "cc1st", "cc",
        "w2ld", "hld", "ym", "ye", "yst", "cc2st", "ccrb", "ypost",
        "wold", "opm", "asml", "ast", "sca", "fin", "vch", "sch",
        "xld0", "xld1", "xnst0", "xnst1", "wqld0", "wqld1", "w1ld0", "w1ld1", "ccr2",
        "w2ld0", "w2ld1", "cc1st0", "cc1st1", "hst0", "hst1", "hld0", "hld1",
        "ast0", "ast1", "fxg0", "fxg1", "lrb", "yrl",
        "pd0", "pd1", "pd2", "pd3", "ps0", "ps1", "ps2", "ps3",
        "cc2st0", "cc2st1",
    ):
        S[name] = Sem(nc, "s_" + name)

    ctxs = []

    def alloc(name, shape, dtype):
        cm = nc.sbuf_tensor(name, shape, dtype)
        t = cm.__enter__()
        ctxs.append(cm)
        return t

    A = alloc("arenaA", [128, NF, SEQ], BF16)
    Bq = alloc("arenaBq", [128, 4, SEQ], BF16)
    Bk = alloc("arenaBk", [128, 4, SEQ], BF16)
    Bv = alloc("arenaBv", [128, 4, SEQ], BF16)
    C0 = alloc("arenaC0", [128, DM], F32)
    C1 = alloc("arenaC1", [128, DM], F32)
    C2 = alloc("arenaC2", [128, DM], F32)

    asmb = alloc("asmb", [128, 4, 1024], F32)
    fx_fm = alloc("fx_fm", [128, NF, KSEL], BF16)
    pstage = alloc("pstage", [128, 2, 4, 128], BF16)

    wqkvb = alloc("wqkvb", [128, 2, NF, 128], BF16)
    w1b = alloc("w1b", [128, 2, NF, 128], BF16)
    w2b = alloc("w2b", [128, 2, 8, 128], BF16)

    brow = alloc("brows", [1, DM], F32)
    onesr = alloc("onesrs", [1, 256], F32)
    ident = alloc("idents", [128, 128], BF16)
    identf = alloc("identfs", [128, 128], F32)
    tri = alloc("tris", [128, 128], BF16)
    onesm = alloc("onesms", [128, 128], BF16)
    iob = alloc("iobs", [128, 256], BF16)
    iop = alloc("iops", [128, 1], F32)
    onescol = alloc("onescols", [128, 1], F32)
    epscol = alloc("epscols", [128, 1], F32)
    dmask = alloc("dmasks", [128, HLOC, 128], BF16)
    dqb = alloc("dqbs", [128, HLOC, 128], BF16)
    dkc = alloc("dks", [128, HLOC], F32)
    lamc = alloc("lamcs", [128, HLOC], F32)
    bqkvc = alloc("bqkvs", [128, 24], F32)
    b1c = alloc("b1s", [128, 32], F32)
    b2c = alloc("b2s", [128, 16], F32)
    wrc = alloc("wrs", [128, NF], BF16)
    cc1i_u = alloc("cc1i_u", [128, 16], U32)
    cc2i_u = alloc("cc2i_u", [128, 4], U32)
    xni_u = alloc("xni_u", [128, 32], U32)
    hofs = alloc("hofss", [1, 1], F32)

    ltm = alloc("ltm", [128, 16], F32)
    lrow = brow
    lrow16 = alloc("lrow16", [16, 128], F32)
    rank = alloc("ranks", [128, 16], F32)
    flag = alloc("flags", [128, 16], F32)
    flagb = alloc("flagbs", [128, 16], BF16)
    etm = alloc("etms", [128, 16], F32)
    posp = alloc("posps", [128, 16], F32)
    mcol = alloc("mcols", [128, 1], F32)
    negmx = alloc("negms", [128, 1], F32)
    zcol = alloc("zcols", [128, 1], F32)
    izval = alloc("izvals", [1, 1], F32)
    izb = alloc("izbs", [128, 1], F32)
    slott = alloc("slotts", [128, 2], F32)
    slote = alloc("slotes", [128, 2], F32)
    rw = alloc("rws", [128, 2], F32)
    tokc = alloc("tokcs", [128, 1], F32)
    fxi_u = alloc("fxi_u", [128, 2], U32)
    sci_f = alloc("sci_f", [128, 2], F32)
    sci_u = alloc("sci_u", [128, 2], U32)

    S_f = alloc("S_f", [128, 4, 128], F32)
    S_b = alloc("S_b", [128, 4, 128], BF16)
    kT_s = alloc("kT_s", [128, 2, 4, 128], BF16)
    vT_s = alloc("vT_s", [128, 2, 4, 128], BF16)
    sc_s = alloc("sc_s", [128, 2, 4, 128], BF16)
    lnm = alloc("lnm", [128, 2], F32)
    lnr = alloc("lnr", [128, 2], F32)

    xbuf = [C0, C1]
    lbuf = C2
    yfmt = C2[:, 0:256]
    junkb = Bv[:, 0, :]
    xnbf = [Bq[:, 0, :], Bq[:, 1, :]]
    fxtm = [Bk[:, 0, :], Bk[:, 1, :]]

    pb = []
    for i in range(8):
        cm = nc.psum_tensor(f"pb{i}", [128, 512], F32)
        pb.append(cm.__enter__())
        ctxs.append(cm)
    pbh = [pb[7][:, 0:32], pb[7][:, 32:64]]

    def p0():
        pairs = [
            (onesr, donesr), (ident, dident), (identf, dident),
            (tri, dtri), (onesm, donesm), (iob, diob), (iop, diop), (onescol, donescol), (epscol, deps), (lamc, dlamc),
            (dmask, ddmask), (dqb, ddqb), (dkc, ddk), (bqkvc, dbqkv),
            (b1c, db1), (b2c, db2), (wrc, dWr), (hofs, dhofs),
        ]

        def fn(g, pairs=pairs):
            for dst, src in pairs:
                g.dma_start(out=dst[...], in_=src[...]).then_inc(S["cst"].h, 16)
            g.dma_start(out=cc1i_u[:, :], in_=dcc1i[:, :]).then_inc(S["cst"].h, 16)
            g.dma_start(out=cc2i_u[:, :], in_=dcc2i[:, :]).then_inc(S["cst"].h, 16)
            g.dma_start(out=xni_u[:, :], in_=dxni[:, :]).then_inc(S["cst"].h, 16)
        for _ in range(len(pairs) + 3):
            S["cst"].inc(16)
        P.g(fn)

    def p1():
        stop_ln = "stop_ln" in debug
        stop_router = "stop_router" in debug
        stop_slots = "stop_slots" in debug
        cst_all = S["cst"].n
        for tc in range(NT):
            par = tc % 2
            xw = S["xld%d" % par].inc(16)

            def f_x(y, tc=tc, par=par):
                if tc >= 2:
                    y.wait_ge(S["xnst%d" % par].h, (tc // 2) * 16)
                y.dma_start(out=xbuf[par][:, :], in_=dx[tc * 128:(tc + 1) * 128, :]).then_inc(S["xld%d" % par].h, 16)
            P.y(f_x)

            vm = S["vmean"].inc()

            def f_mean(v, par=par, xw=xw, cst_all=cst_all):
                v.wait_ge(S["cst"].h, cst_all)
                v.wait_ge(S["xld%d" % par].h, xw)
                edge(v, v.reduce_sum(lnm[:, par:par + 1], xbuf[par][:, :], axis=AXX), S["vch"])
                v.tensor_scalar(lnm[:, par:par + 1], lnm[:, par:par + 1], -1.0 / DM, None,
                                ALU.mult).then_inc(S["vmean"].h, 1)
            P.v(f_mean)

            sq = S["ssq"].inc()

            def f_sq(s, par=par, vm=vm):
                s.wait_ge(S["vmean"].h, vm)
                edge(s, s.activation(junkb[:, :], xbuf[par][:, :], AF.Square, bias=lnm[:, par:par + 1],
                                     scale=1.0, accum_out=lnr[:, par:par + 1]), S["sch"])
                edge(s, s.activation(lnr[:, par:par + 1], lnr[:, par:par + 1], AF.Ln,
                                     bias=epscol[:, :], scale=1.0 / DM), S["sch"])
                s.activation(lnr[:, par:par + 1], lnr[:, par:par + 1], AF.Exp,
                             scale=-0.5).then_inc(S["ssq"].h, 1)
            P.s(f_sq)

            vx = S["vxn"].inc()

            def f_norm(v, par=par, sq=sq, tc=tc):
                v.wait_ge(S["ssq"].h, sq)
                edge(v, v.tensor_scalar(xbuf[par][:, :], xbuf[par][:, :], lnm[:, par:par + 1],
                                        lnr[:, par:par + 1], ALU.add, ALU.mult), S["vch"])
                if tc >= 2:
                    v.wait_ge(S["tp"].h, tc - 1)
                v.tensor_copy(xnbf[par][:, :], xbuf[par][:, :]).then_inc(S["vxn"].h, 1)
            P.v(f_norm)

            S["xnst%d" % par].inc(16)

            def f_store(s, tc=tc, par=par, vx=vx):
                s.wait_ge(S["vxn"].h, vx)
                s.dma_start(out=dxn[tc * 128:(tc + 1) * 128, :], in_=xbuf[par][:, :]).then_inc(S["xnst%d" % par].h, 16)
            P.s(f_store)

            tpv = S["tp"].inc()

            def f_tp(t, tc=tc, par=par, vx=vx, tpv=tpv, cst_all=cst_all):
                if tc == 0:
                    t.wait_ge(S["cst"].h, cst_all)
                t.wait_ge(S["vxn"].h, vx)
                if tpv > 1:
                    t.wait_ge(S["etp"].h, tpv - 1)
                for fc in range(NF):
                    bank = fc // 4
                    col = fc % 4
                    ins = t.matmul(pb[bank][:, col * 128:(col + 1) * 128], xnbf[par][:, fc * 128:(fc + 1) * 128],
                                   ident[:, :], start=True, stop=True)
                ins.then_inc(S["tp"].h, 1)
            P.t(f_tp)

            ev = S["etp"].inc()

            def f_etp(s, tc=tc, tpv=tpv):
                s.wait_ge(S["tp"].h, tpv)
                for bank in range(4):
                    ins = s.copy(A[:, bank * 4:(bank + 1) * 4, tc * 128:(tc + 1) * 128],
                                 pb[bank][:, :].rearrange("p (f c) -> p f c", f=4))
                    if bank == 3:
                        ins.then_inc(S["etp"].h, 1)
            P.s(f_etp)

            S["lgm"].inc()

            def f_lg(t, tc=tc, ev=ev, cst_all=cst_all):
                if tc == 0:
                    t.matmul(pbh[0][:, 19:20], onesr[0:1, 0:128], onesr[0:1, 128:129],
                             start=True, stop=True)
                t.wait_ge(S["etp"].h, ev)
                for fc in range(NF):
                    ins = t.matmul(pbh[0][:, tc:tc + 1], A[:, fc, tc * 128:(tc + 1) * 128], wrc[:, fc:fc + 1],
                                   start=(fc == 0), stop=(fc == NF - 1))
                ins.then_inc(S["lgm"].h, 1)
            P.t(f_lg)

        if stop_ln:
            return
        lgall = S["lgm"].n
        lge1 = S["lge"].inc()

        def f_lev(v, lgall=lgall):
            v.wait_ge(S["lgm"].h, lgall)
            v.tensor_scalar(ltm[:, :], pbh[0][:, 0:16], pbh[0][:, 19:20], None, ALU.add).then_inc(S["lge"].h, 1)
        P.v(f_lev)

        rk1 = S["rkm"].inc()

        def f_lt(t, lge1=lge1):
            t.wait_ge(S["lge"].h, lge1)
            t.matmul(pb[4][0:16, 0:128], ltm[:, :], identf[:, :], start=True,
                     stop=True).then_inc(S["rkm"].h, 1)
        P.t(f_lt)

        lge2 = S["lge"].inc()

        def f_lte(v, rk1=rk1):
            v.wait_ge(S["rkm"].h, rk1)
            v.tensor_copy(lrow16[:, :], pb[4][0:16, 0:128]).then_inc(S["lge"].h, 1)
        P.v(f_lte)

        st1 = S["lrb"].inc(16)

        def f_lrd(y, lge2=lge2):
            y.wait_ge(S["lge"].h, lge2)
            y.dma_start(out=dlrow.rearrange("o (a b) -> (o a) b", a=16), in_=lrow16[:, :]).then_inc(S["lrb"].h, 16)
        P.y(f_lrd)
        st2 = S["lrb"].inc(16)

        def f_lrl(y, st1=st1):
            y.wait_ge(S["lrb"].h, st1)
            y.dma_start(out=lrow[:, :], in_=dlrow[:, :]).then_inc(S["lrb"].h, 16)
        P.y(f_lrl)

        rk2 = S["rkm"].inc()

        def f_lb(t, st2=st2):
            t.wait_ge(S["lrb"].h, st2)
            for tc in range(NT):
                bank = tc // 4
                col = tc % 4
                ins = t.matmul(pb[bank][:, col * 128:(col + 1) * 128], onesr[0:1, 0:128],
                               lrow[0:1, tc * 128:(tc + 1) * 128], start=True, stop=True)
            ins.then_inc(S["rkm"].h, 1)
        P.t(f_lb)

        lge3 = S["lge"].inc()

        def f_lbe(v, rk2=rk2):
            v.wait_ge(S["rkm"].h, rk2)
            for bank in range(4):
                ins = v.tensor_copy(lbuf[:, bank * 512:(bank + 1) * 512], pb[bank][:, :])
                edge(v, ins, S["vch"])
            edge(v, v.reduce_max(mcol[:, :], lbuf[:, :], axis=AXX), S["vch"])
            v.tensor_scalar(negmx[:, :], mcol[:, :], -1.0, None, ALU.mult).then_inc(S["lge"].h, 1)
        P.v(f_lbe)

        lge4 = S["lge"].inc()

        def f_rank(v, lge3=lge3):
            v.wait_ge(S["lge"].h, lge3)
            for tc in range(NT):
                ins = v.tensor_scalar(junkb[:, :], lbuf[:, :], ltm[:, tc:tc + 1], 0.0, ALU.is_gt,
                                      ALU.add, accum_out=rank[:, tc:tc + 1])
                edge(v, ins, S["vch"])
            edge(v, v.tensor_scalar(flag[:, :], rank[:, :], 255.5, None, ALU.is_lt), S["vch"])
            v.tensor_copy(flagb[:, :], flag[:, :]).then_inc(S["lge"].h, 1)
        P.v(f_rank)

        se1 = S["ssq"].inc()

        def f_e(s, lge4=lge4):
            s.wait_ge(S["lge"].h, lge4)
            s.activation(etm[:, :], ltm[:, :], AF.Exp, bias=negmx[:, :], scale=1.0).then_inc(S["ssq"].h, 1)
        P.s(f_e)

        rk3 = S["rkm"].inc()

        def f_pf(t, lge4=lge4):
            t.wait_ge(S["lge"].h, lge4)
            for tc in range(NT):
                for tp_ in range(tc):
                    t.matmul(pbh[1][:, tc:tc + 1], onesm[:, :], flagb[:, tp_:tp_ + 1],
                             start=(tp_ == 0), stop=False)
                ins = t.matmul(pbh[1][:, tc:tc + 1], tri[:, :], flagb[:, tc:tc + 1],
                               start=(tc == 0), stop=True)
            ins.then_inc(S["rkm"].h, 1)
        P.t(f_pf)

        lge5 = S["lge"].inc()

        def f_pp(v, rk3=rk3, se1=se1):
            v.wait_ge(S["rkm"].h, rk3)
            v.wait_ge(S["ssq"].h, se1)
            edge(v, v.scalar_tensor_tensor(posp[:, :], flag[:, :], -999.0, pbh[1][:, 0:16],
                                           ALU.mult, ALU.add), S["vch"])
            v.tensor_scalar(posp[:, :], posp[:, :], 999.0, None, ALU.add)
            edge(v, v.tensor_mul(etm[:, :], etm[:, :], flag[:, :]), S["vch"])
            v.reduce_sum(zcol[:, :], etm[:, :], axis=AXX).then_inc(S["lge"].h, 1)
        P.v(f_pp)

        rk4 = S["rkm"].inc()

        def f_z(t, lge5=lge5):
            t.wait_ge(S["lge"].h, lge5)
            t.matmul(pbh[0][0:1, 16:17], zcol[:, :], onescol[:, :], start=True, stop=True)
            t.matmul(pbh[0][:, 18:19], onesr[0:1, 0:128], hofs[:, :], start=True,
                     stop=True).then_inc(S["rkm"].h, 1)
        P.t(f_z)

        lge6 = S["lge"].inc()

        def f_iz(v, rk4=rk4):
            v.wait_ge(S["rkm"].h, rk4)
            v.reciprocal(izval[:, :], pbh[0][0:1, 16:17]).then_inc(S["lge"].h, 1)
        P.v(f_iz)

        rk5 = S["rkm"].inc()

        def f_izb(t, lge6=lge6):
            t.wait_ge(S["lge"].h, lge6)
            t.matmul(pbh[0][:, 17:18], onesr[0:1, 0:128], izval[:, :], start=True,
                     stop=True).then_inc(S["rkm"].h, 1)
        P.t(f_izb)

        if stop_router:
            return
        for sh in range(2):
            for tc in range(NT):
                par = tc % 2
                pv = S["rke"].inc()

                def f_ptile(v, tc=tc, par=par, sh=sh, lge5=lge5, pv=pv):
                    if tc == 0 and sh == 0:
                        v.wait_ge(S["lge"].h, lge5)
                        v.wait_ge(S["tp"].h, NT)
                        v.wait_ge(S["xnst0"].h, 16 * 8)
                        v.wait_ge(S["xnst1"].h, 16 * 8)
                    if sh * NT + tc >= 2:
                        v.wait_ge(S["scn"].h, sh * NT + tc - 1)
                    v.tensor_scalar(tokc[:, :], iop[:, :], float(128 * tc), None, ALU.add)
                    v.tensor_scalar(xbuf[par][:, 0:128], iob[:, sh * 128:(sh + 1) * 128],
                                    posp[:, tc:tc + 1], None, ALU.is_equal)
                    v.tensor_copy(xbuf[par][:, 256:257], tokc[:, :])
                    v.tensor_copy(xbuf[par][:, 257:258], etm[:, tc:tc + 1]).then_inc(S["rke"].h, 1)
                P.v(f_ptile)

                S["scn"].inc()

                def f_pmm(t, tc=tc, par=par, sh=sh, pv=pv):
                    t.wait_ge(S["rke"].h, pv)
                    ins = t.matmul(pbh[1][:, 18 + sh * 2:20 + sh * 2],
                                   xbuf[par][:, 0:128], xbuf[par][:, 256:258],
                                   start=(tc == 0), stop=(tc == NT - 1))
                    ins.then_inc(S["scn"].h, 1)
                P.t(f_pmm)

        lge7 = S["lge"].inc()

        def f_slote(v, rk5=rk5):
            v.wait_ge(S["scn"].h, 2 * NT)
            edge(v, v.tensor_copy(izb[:, :], pbh[0][:, 17:18]), S["vch"])
            edge(v, v.tensor_copy(mcol[:, :], pbh[0][:, 18:19]), S["vch"])
            for sh in range(2):
                v.tensor_copy(slott[:, sh:sh + 1], pbh[1][:, 18 + sh * 2:19 + sh * 2])
                ins = v.tensor_copy(slote[:, sh:sh + 1], pbh[1][:, 19 + sh * 2:20 + sh * 2])
            edge(v, ins, S["vch"])
            v.tensor_scalar(rw[:, :], slote[:, :], izb[:, 0:1], None, ALU.mult)
            v.tensor_copy(fxi_u[:, :], slott[:, :])
            edge(v, v.tensor_scalar(sci_f[:, :], slott[:, :], mcol[:, 0:1], None, ALU.subtract), S["vch"])
            edge(v, v.tensor_scalar(lnm[:, 0:2], sci_f[:, :], -0.5, None, ALU.is_gt), S["vch"])
            edge(v, v.tensor_scalar(lnr[:, 0:2], sci_f[:, :], 1023.5, None, ALU.is_lt), S["vch"])
            edge(v, v.tensor_mul(lnm[:, 0:2], lnm[:, 0:2], lnr[:, 0:2]), S["vch"])
            edge(v, v.tensor_scalar(lnr[:, 0:1], iop[:, :], 1024.0, None, ALU.add), S["vch"])
            edge(v, v.tensor_scalar(lnr[:, 1:2], iop[:, :], 1024.0 + 128.0, None, ALU.add), S["vch"])
            edge(v, v.tensor_sub(sci_f[:, :], sci_f[:, :], lnr[:, 0:2]), S["vch"])
            edge(v, v.tensor_mul(sci_f[:, :], sci_f[:, :], lnm[:, 0:2]), S["vch"])
            edge(v, v.tensor_add(sci_f[:, :], sci_f[:, :], lnr[:, 0:2]), S["vch"])
            v.tensor_copy(sci_u[:, :], sci_f[:, :]).then_inc(S["lge"].h, 1)
        P.v(f_slote)

        if stop_slots:
            return
        rke_fx0 = None
        for sh in range(2):
            fg = S["fxg%d" % sh].inc(16)

            def f_fxg(g, sh=sh, lge7=lge7):
                g.wait_ge(S["lge"].h, lge7)
                g.wait_ge(S["xnst0"].h, 16 * 8)
                g.wait_ge(S["xnst1"].h, 16 * 8)
                g.indirect_dma_start(
                    out=fxtm[sh][:, :], out_offset=None,
                    in_=dxn[:, :], in_offset=IndirectOffsetOnAxis(fxi_u[:, sh:sh + 1], 0),
                ).then_inc(S["fxg%d" % sh].h, 16)
            P.g(f_fxg)

            ft = S["fxt"].inc()

            def f_fxt(t, sh=sh, fg=fg, ft=ft, rke_fx0=rke_fx0):
                t.wait_ge(S["fxg%d" % sh].h, fg)
                if ft > 1:
                    t.wait_ge(S["rke"].h, rke_fx0)
                for fc in range(NF):
                    bank = 3 + fc // 4
                    col = fc % 4
                    ins = t.matmul(pb[bank][:, col * 128:(col + 1) * 128], fxtm[sh][:, fc * 128:(fc + 1) * 128],
                                   ident[:, :], start=True, stop=True)
                ins.then_inc(S["fxt"].h, 1)
            P.t(f_fxt)

            S["rke"].inc(4)

            def f_fxe(s, sh=sh, ft=ft):
                s.wait_ge(S["fxt"].h, ft)
                for bank in range(4):
                    ins = s.copy(fx_fm[:, bank * 4:(bank + 1) * 4, sh * 128:(sh + 1) * 128],
                                 pb[3 + bank][:, :].rearrange("p (f c) -> p f c", f=4))
                    if bank == 3:
                        ins.then_inc(S["rke"].h, 4)
            P.s(f_fxe)
            rke_fx0 = S["rke"].n

        if "xnfm" in debug and not stop_ln:
            etp_all = S["etp"].n

            def f_dbg1(g, etp_all=etp_all):
                g.wait_ge(S["etp"].h, etp_all)
                for fc in range(NF):
                    g.dma_start(out=dbg["d_xnfm"][:, fc * SEQ:(fc + 1) * SEQ],
                                in_=A[:, fc, :]).then_inc(S["fin"].h, 16)
            for fc in range(NF):
                S["fin"].inc(16)
            P.g(f_dbg1)
        if "router" in debug and not (stop_ln or stop_router or stop_slots):
            lge_all = S["lge"].n
            rke_all = S["rke"].n

            def f_dbg2(g, lge_all=lge_all, rke_all=rke_all):
                g.wait_ge(S["lge"].h, lge_all)
                g.wait_ge(S["rke"].h, rke_all)
                g.dma_start(out=dbg["d_ltm"][:, :], in_=ltm[:, :]).then_inc(S["fin"].h, 16)
                g.dma_start(out=dbg["d_rank"][:, :], in_=rank[:, :]).then_inc(S["fin"].h, 16)
                g.dma_start(out=dbg["d_slott"][:, :], in_=slott[:, :]).then_inc(S["fin"].h, 16)
                g.dma_start(out=dbg["d_rw"][:, :], in_=rw[:, :]).then_inc(S["fin"].h, 16)
                g.dma_start(out=dbg["d_sci"][:, :], in_=sci_f[:, :]).then_inc(S["fin"].h, 16)
                for fc in range(NF):
                    g.dma_start(out=dbg["d_fxfm"][:, fc * KSEL:(fc + 1) * KSEL],
                                in_=fx_fm[:, fc, :]).then_inc(S["fin"].h, 16)
            for _ in range(5 + NF):
                S["fin"].inc(16)
            P.g(f_dbg2)


    shared = {}

    def p2():
        stop_qkv = "stop_qkv" in debug
        S_bv = {}
        SCw = {k: {} for k in ("ke", "ve", "me", "om", "dm", "su", "ae")}
        qarena = [Bq, Bk, Bv]
        rke_all_p1 = S["rke"].n
        lge_all_p1 = S["lge"].n
        vch_p1 = S["vch"].n
        pd_pieces = []
        for ob in range(32):
            pd_pieces.append((dW1[ob, :, :, :], dW1bf[ob, :, :, :], 2048))
        for i in range(64):
            pd_pieces.append((dW2[i // 4, :, (i % 4) * 8:(i % 4 + 1) * 8, :], dW2bf[i, :, :, :], 1024))
        for i in range(16):
            pd_pieces.append((dwout[i // 4, :, (i % 4) * 4:(i % 4 + 1) * 4, :], dwoutbf[i, :, :, :], 2048))
        bnc = [asmb[:, j, :].bitcast(BF16) for j in range(4)]

        def emit_pd(i):
            sl = i % 4
            S["pd%d" % sl].inc(16)

            def f_pdl(g, i=i, sl=sl, psv=16 * (i // 4)):
                if 32 <= i < 96:
                    g.wait_ge(S["qkm"].h, 12)
                elif i >= 96:
                    g.wait_ge(S["qkm"].h, 18)
                if i >= 4:
                    g.wait_ge(S["ps%d" % sl].h, psv)
                src_ap, dst_ap, n = pd_pieces[i]
                g.dma_start(out=bnc[sl][:, 0:n], in_=src_ap).then_inc(S["pd%d" % sl].h, 16)
            P.g(f_pdl)
            S["ps%d" % sl].inc(16)

            def f_pds(y, i=i, sl=sl, pdv=16 * (i // 4 + 1)):
                y.wait_ge(S["pd%d" % sl].h, pdv)
                src_ap, dst_ap, n = pd_pieces[i]
                y.dma_start(out=dst_ap, in_=bnc[sl][:, 0:n]).then_inc(S["ps%d" % sl].h, 16)
            P.y(f_pds)

        pd_emitted = 0
        for gb in range(24):
            wq = S["wqld%d" % (gb % 2)].inc(16)

            def f_wdma(g, gb=gb):
                if gb >= 2:
                    g.wait_ge(S["qkm"].h, gb - 1)
                g.dma_start(out=wqkvb[:, gb % 2, :, :], in_=dwqkv[gb, :, :, :]).then_inc(S["wqld%d" % (gb % 2)].h, 16)
            P.g(f_wdma)
            if gb < 12:
                target = min(32, (gb + 1) * 3)
            else:
                target = 32 + (gb - 11) * 4
            while pd_emitted < target:
                emit_pd(pd_emitted)
                pd_emitted += 1
        while pd_emitted < len(pd_pieces):
            emit_pd(pd_emitted)
            pd_emitted += 1
        shared["pd_done"] = [(S["ps%d" % j].h, S["ps%d" % j].n) for j in range(4)]

        for hg in range(2):
            if hg == 1:
                for ob in range(2):
                    S["w1ld%d" % ob].inc(16)

                    def f_w1e(s, ob=ob):
                        s.wait_ge(S["ps%d" % ob].h, 16)
                        s.dma_start(out=w1b[:, ob, :, :],
                                    in_=dW1bf[ob, :, :, :]).then_inc(S["w1ld%d" % ob].h, 16)
                    P.s(f_w1e)
            for bl in range(12):
                gb = hg * 12 + bl
                i, o = bl // 3, bl % 3
                base = 0 if gb % 2 == 0 else 4
                qk = S["qkm"].inc()

                def f_qmm(t, gb=gb, i=i, o=o, base=base, rke_all_p1=rke_all_p1, lge_all_p1=lge_all_p1,
                          sce_pg=S["sce"].n, scs_pg=S["scs"].n):
                    t.wait_ge(S["wqld%d" % (gb % 2)].h, 16 * (gb // 2 + 1))
                    if gb in (12, 13):
                        t.wait_ge(S["sce"].h, sce_pg)
                        t.wait_ge(S["scs"].h, scs_pg)
                    if gb == 0:
                        t.wait_ge(S["rke"].h, rke_all_p1)
                    if gb == 1:
                        t.wait_ge(S["lge"].h, lge_all_p1)
                    if gb >= 2:
                        t.wait_ge(S["qke"].h, gb - 1)
                    for f in range(NF):
                        for tt in range(4):
                            ins = t.matmul(pb[base + tt][:, :], wqkvb[:, gb % 2, f, :],
                                           A[:, f, tt * 512:(tt + 1) * 512],
                                           start=(f == 0), stop=(f == NF - 1))
                    ins.then_inc(S["qkm"].h, 1)
                P.t(f_qmm)

                qe = S["qke"].inc()
                dest = qarena[o]

                def f_qev(v, gb=gb, i=i, o=o, base=base, dest=dest, qk=qk,
                          rke_all_p1=rke_all_p1, lge_all_p1=lge_all_p1, vch_p1=vch_p1):
                    v.wait_ge(S["qkm"].h, qk)
                    if gb <= 2:
                        v.wait_ge(S["tp"].h, NT)
                        v.wait_ge(S["fxt"].h, 2)
                        v.wait_ge(S["lge"].h, lge_all_p1)
                        v.wait_ge(S["vch"].h, vch_p1)
                        v.wait_ge(S["rke"].h, rke_all_p1)
                    for tt in range(4):
                        ins = v.tensor_scalar(dest[:, i, tt * 512:(tt + 1) * 512], pb[base + tt][:, :],
                                              bqkvc[:, gb:gb + 1], None, ALU.add)
                    ins.then_inc(S["qke"].h, 1)
                P.v(f_qev)

            if stop_qkv and hg == 0:
                break

            sv = S["sce"].inc()

            def f_srst(v, hg=hg, qe_all=S["qke"].n, pvsb=S_bv.get(hg * NCH - 1)):
                v.wait_ge(S["qke"].h, qe_all)
                if pvsb is not None:
                    v.wait_ge(S["scs"].h, pvsb)
                v.memset(S_f[:, :, :], 0.0).then_inc(S["sce"].h, 1)
            P.v(f_srst)

            for c in range(NCH):
                par = c % 2
                gc = hg * NCH + c
                bK = [0, 5][par]
                bV = [1, 6][par]
                bS = [2, 7][par]

                kt = S["scn"].inc()
                vt = S["scn"].inc()

                def f_kt(t, hg=hg, c=c, par=par, bK=bK, bV=bV, qe_all=S["qke"].n,
                         w_ke=SCw["ke"].get(hg * NCH + c - 2), w_ve=SCw["ve"].get(hg * NCH + c - 2)):
                    if c <= 1:
                        t.wait_ge(S["qke"].h, qe_all)
                    if w_ke is not None:
                        t.wait_ge(S["sce"].h, w_ke)
                    if w_ve is not None:
                        t.wait_ge(S["scs"].h, w_ve)
                    for i in range(4):
                        ins = t.matmul(pb[bK][:, i * 128:(i + 1) * 128], Bk[:, i, c * 128:(c + 1) * 128],
                                       ident[:, :], start=True, stop=True)
                    for i in range(4):
                        ins2 = t.matmul(pb[bV][:, i * 128:(i + 1) * 128], Bv[:, i, c * 128:(c + 1) * 128],
                                        ident[:, :], start=True, stop=True)
                    ins.then_inc(S["scn"].h, 1)
                    ins2.then_inc(S["scn"].h, 1)
                P.t(f_kt)

                ke = S["sce"].inc()
                SCw["ke"][gc] = ke

                def f_ke(v, hg=hg, c=c, par=par, bK=bK, kt=kt, w_dm=SCw["dm"].get(hg * NCH + c - 2)):
                    v.wait_ge(S["scn"].h, kt)
                    if w_dm is not None:
                        v.wait_ge(S["scn"].h, w_dm)
                    for i in range(4):
                        ins = v.tensor_scalar(kT_s[:, par, i, :], pb[bK][:, i * 128:(i + 1) * 128],
                                              dkc[:, 4 * hg + i:4 * hg + i + 1], None, ALU.mult)
                    ins.then_inc(S["sce"].h, 1)
                P.v(f_ke)

                ve = S["scs"].inc()
                SCw["ve"][gc] = ve

                def f_ve(s, c=c, par=par, bV=bV, vt=vt, w_dm=SCw["dm"].get(hg * NCH + c - 2),
                         w_om=SCw["om"].get(hg * NCH + c - 2)):
                    s.wait_ge(S["scn"].h, vt)
                    if w_dm is not None:
                        s.wait_ge(S["scn"].h, w_dm)
                    if w_om is not None:
                        s.wait_ge(S["scn"].h, w_om)
                    s.copy(vT_s[:, par, :, :], pb[bV][:, :].rearrange("p (i t) -> p i t", i=4)).then_inc(S["scs"].h, 1)
                P.s(f_ve)

                sc_mm = S["scn"].inc()

                def f_sc(t, hg=hg, c=c, bS=bS, w_me=SCw["me"].get(hg * NCH + c - 2)):
                    if w_me is not None:
                        t.wait_ge(S["sce"].h, w_me)
                    for i in range(4):
                        ins = t.matmul(pb[bS][:, i * 128:(i + 1) * 128], Bk[:, i, c * 128:(c + 1) * 128],
                                       Bq[:, i, c * 128:(c + 1) * 128], start=True, stop=True)
                    ins.then_inc(S["scn"].h, 1)
                P.t(f_sc)

                me = S["sce"].inc()
                SCw["me"][gc] = me

                def f_me(v, hg=hg, c=c, par=par, bS=bS, sc_mm=sc_mm, w_om=SCw["om"].get(hg * NCH + c - 2)):
                    v.wait_ge(S["scn"].h, sc_mm)
                    if w_om is not None:
                        v.wait_ge(S["scn"].h, w_om)
                    v.tensor_tensor(sc_s[:, par, :, :], pb[bS][:, :].rearrange("p (i t) -> p i t", i=4),
                                    dmask[:, 4 * hg:4 * hg + 4, :], ALU.mult).then_inc(S["sce"].h, 1)
                P.v(f_me)

                qs = S["sce"].inc()

                def f_qs(v, hg=hg, c=c, sc_mm=sc_mm):
                    v.wait_ge(S["scn"].h, sc_mm)
                    v.tensor_tensor(Bq[:, :, c * 128:(c + 1) * 128], Bq[:, :, c * 128:(c + 1) * 128],
                                    dqb[:, 4 * hg:4 * hg + 4, :], ALU.mult).then_inc(S["sce"].h, 1)
                P.v(f_qs)

                om = S["scn"].inc()
                SCw["om"][gc] = om

                def f_om(t, hg=hg, c=c, par=par, me=me, qs=qs, ve=ve, gc=gc,
                         w_ae=SCw["ae"].get(hg * NCH + c - 1)):
                    t.wait_ge(S["sce"].h, me)
                    t.wait_ge(S["scs"].h, ve)
                    if w_ae is not None:
                        t.wait_ge(S["scs"].h, w_ae)
                    if c > 0:
                        t.wait_ge(S["sce"].h, qs)
                        t.wait_ge(S["scs"].h, S_bv[gc - 1])
                    for i in range(4):
                        ins = t.matmul(pb[3][:, i * 128:(i + 1) * 128], vT_s[:, par, i, :], sc_s[:, par, i, :],
                                       start=True, stop=(c == 0))
                        if c > 0:
                            ins = t.matmul(pb[3][:, i * 128:(i + 1) * 128], S_b[:, i, :],
                                           Bq[:, i, c * 128:(c + 1) * 128], start=False, stop=True)
                    ins.then_inc(S["scn"].h, 1)
                P.t(f_om)

                dm = S["scn"].inc()
                SCw["dm"][gc] = dm

                def f_dm(t, hg=hg, c=c, par=par, ke=ke, ve=ve, w_su=SCw["su"].get(hg * NCH + c - 1)):
                    t.wait_ge(S["sce"].h, ke)
                    t.wait_ge(S["scs"].h, ve)
                    if w_su is not None:
                        t.wait_ge(S["sce"].h, w_su)
                    for i in range(4):
                        ins = t.matmul(pb[4][:, i * 128:(i + 1) * 128], kT_s[:, par, i, :], vT_s[:, par, i, :],
                                       start=True, stop=True)
                    ins.then_inc(S["scn"].h, 1)
                P.t(f_dm)

                su = S["sce"].inc()
                SCw["su"][gc] = su

                def f_su(v, hg=hg, c=c, dm=dm, pvsb=S_bv.get(hg * NCH + c - 1)):
                    v.wait_ge(S["scn"].h, dm)
                    if pvsb is not None:
                        v.wait_ge(S["scs"].h, pvsb)
                    for i in range(4):
                        ins = v.scalar_tensor_tensor(S_f[:, i, :], S_f[:, i, :],
                                                     lamc[:, 4 * hg + i:4 * hg + i + 1],
                                                     pb[4][:, i * 128:(i + 1) * 128], ALU.mult, ALU.add)
                    ins.then_inc(S["sce"].h, 1)
                P.v(f_su)

                sb = S["scs"].inc()
                S_bv[gc] = sb

                def f_sb(s, c=c, su=su):
                    s.wait_ge(S["sce"].h, su)
                    s.copy(S_b[:, :, :], S_f[:, :, :]).then_inc(S["scs"].h, 1)
                P.s(f_sb)

                ae = S["scs"].inc()
                SCw["ae"][gc] = ae

                def f_ae(s, c=c, par=par, om=om, gc=gc):
                    s.wait_ge(S["scn"].h, om)
                    if gc >= 2:
                        s.wait_ge(S["cc1st%d" % par].h, (gc // 2) * 16)
                    s.copy(pstage[:, par, :, :], pb[3][:, :].rearrange("p (i t) -> p i t", i=4)).then_inc(S["scs"].h, 1)
                P.s(f_ae)

                S["cc1st%d" % par].inc(16)

                def f_cst(s, hg=hg, c=c, par=par, ae=ae):
                    s.wait_ge(S["scs"].h, ae)
                    dst = [dcc1inA, dcc1inB][hg]
                    s.dma_start(
                        out=dst[0:512, c * 128:(c + 1) * 128].rearrange("(i d) t -> d i t", i=4),
                        in_=pstage[:, par, :, :]).then_inc(S["cc1st%d" % par].h, 16)
                P.s(f_cst)

            S["cc"].inc()

            def f_ccg(g, hg=hg, p0=S["cc1st0"].n, p1=S["cc1st1"].n):
                g.wait_ge(S["cc1st0"].h, p0)
                g.wait_ge(S["cc1st1"].h, p1)
                g.collective_compute(
                    "AllGather", ALU.bypass, replica_groups=[list(range(8))],
                    ins=[[dcc1inA, dcc1inB][hg][:, :]], outs=[[dcc1outA, dcc1outB][hg][:, :]],
                ).then_inc(S["cc"].h, 1)
            P.g(f_ccg)

        if "attn" in debug and not stop_qkv:
            cc1_all = S["cc1st0"].n
            qke_all = S["qke"].n

            def f_dbga(g, cc1_all=cc1_all, qke_all=qke_all):
                g.wait_ge(S["cc1st0"].h, cc1_all)
                g.wait_ge(S["cc1st1"].h, S["cc1st1"].n)
                g.wait_ge(S["qke"].h, qke_all)
                g.dma_start(out=dbg["d_cc1"][:, :], in_=dcc1in[:, :]).then_inc(S["fin"].h, 16)
                for i in range(4):
                    g.dma_start(out=dbg["d_qf"][:, i * SEQ:(i + 1) * SEQ], in_=Bq[:, i, :]).then_inc(S["fin"].h, 16)
                    g.dma_start(out=dbg["d_kf"][:, i * SEQ:(i + 1) * SEQ], in_=Bk[:, i, :]).then_inc(S["fin"].h, 16)
                    g.dma_start(out=dbg["d_vf"][:, i * SEQ:(i + 1) * SEQ], in_=Bv[:, i, :]).then_inc(S["fin"].h, 16)
            for _ in range(13):
                S["fin"].inc(16)
            P.g(f_dbga)

    def p3():
        rkm_p1 = S["rkm"].n
        qkm_all = S["qkm"].n
        attn_g = A[:, :, 0:1024]
        h_reg = A[:, :, 1024:1536]
        wout_reg = A[:, :, 1536:2048]

        def hslice(dc):
            return A[:, dc % 16, 1024 + (dc // 16) * 256: 1024 + (dc // 16) * 256 + 256]


        S["cst"].inc(16)

        def f_brow(g, rkm_p1=rkm_p1):
            g.wait_ge(S["rkm"].h, rkm_p1)
            g.dma_start(out=brow[:, :], in_=dbrow[:, :]).then_inc(S["cst"].h, 16)
        P.g(f_brow)

        viewA = dcc1outA.rearrange("r (a c) -> (r a) c", a=2)
        viewB = dcc1outB.rearrange("r (a c) -> (r a) c", a=2)
        order = [g_ for g_ in range(16) if g_ % 8 < 4] + [g_ for g_ in range(16) if g_ % 8 >= 4]
        first_b = True
        for idx, g_ in enumerate(order):
            S["ccrb"].inc(16)
            isB = g_ % 8 >= 4

            def f_rb1(g, g_=g_, idx=idx, isB=isB, qkm_all=qkm_all, first_b=first_b):
                if idx == 0:
                    g.wait_ge(S["cc"].h, 1)
                    g.wait_ge(S["qkm"].h, qkm_all)
                if isB and first_b:
                    g.wait_ge(S["cc"].h, 2)
                g.indirect_dma_start(
                    out=attn_g[:, g_, :], out_offset=None,
                    in_=viewB if isB else viewA, in_offset=IndirectOffsetOnAxis(cc1i_u[:, g_:g_ + 1], 0),
                ).then_inc(S["ccrb"].h, 16)
            if isB and first_b:
                first_b = False
            P.g(f_rb1)
        ccrb1 = S["ccrb"].n

        for w2i in range(64):
            S["w2ld%d" % (w2i % 2)].inc(16)

            def f_w2(y, w2i=w2i, pv=16 * ((32 + w2i) // 4 + 1)):
                y.wait_ge(S["ps%d" % ((32 + w2i) % 4)].h, pv)
                if w2i >= 2:
                    y.wait_ge(S["ym"].h, w2i - 1)
                y.dma_start(out=w2b[:, w2i % 2, :, :],
                            in_=dW2bf[w2i, :, :, :]).then_inc(S["w2ld%d" % (w2i % 2)].h, 16)
            P.y(f_w2)

        cc2v = []
        for half in range(2):
            cc2v.append(S["cc"].inc())

            def f_cc2(g, half=half, stv=S["cc2st%d" % half].n + 32):
                g.wait_ge(S["cc2st%d" % half].h, stv)
                g.collective_compute(
                    "AllGather", ALU.bypass, replica_groups=[list(range(8))],
                    ins=[dcc2inH[half][:, :]], outs=[dcc2outH[half][:, :]],
                ).then_inc(S["cc"].h, 1)
            P.g(f_cc2)

        fxe_all = S["rke"].n
        for ob in range(32):
            S["hm"].inc()

            def f_hmm(t, ob=ob, fxe_all=fxe_all):
                t.wait_ge(S["w1ld%d" % (ob % 2)].h, 16 * (ob // 2 + 1))
                if ob >= 2:
                    t.wait_ge(S["he"].h, ob - 1)
                for f in range(NF):
                    ins = t.matmul(pb[5 + ob % 2][:, 0:256], w1b[:, ob % 2, f, :], fx_fm[:, f, :],
                                   start=(f == 0), stop=(f == NF - 1))
                ins.then_inc(S["hm"].h, 1)
            P.t(f_hmm)

            S["he"].inc()

            def f_gel(s, ob=ob, hv=S["hm"].n, qkm_all=qkm_all):
                s.wait_ge(S["hm"].h, hv)
                if ob == 0:
                    s.wait_ge(S["qkm"].h, qkm_all)
                s.activation(hslice(ob), pb[5 + ob % 2][:, 0:256], AF.Gelu,
                             bias=b1c[:, ob:ob + 1], scale=1.0).then_inc(S["he"].h, 1)
            P.s(f_gel)

            if ob + 2 < 32:
                ob2 = ob + 2
                S["w1ld%d" % (ob2 % 2)].inc(16)

                def f_w1(s, ob2=ob2, pv=16 * (ob2 // 4 + 1)):
                    s.wait_ge(S["ps%d" % (ob2 % 4)].h, pv)
                    s.wait_ge(S["hm"].h, ob2 - 1)
                    s.dma_start(out=w1b[:, ob2 % 2, :, :],
                                in_=dW1bf[ob2, :, :, :]).then_inc(S["w1ld%d" % (ob2 % 2)].h, 16)
                P.s(f_w1)

        he_all = S["he"].n
        for mb in range(16):
            for sg in range(4):
                w2i = mb * 4 + sg
                S["ym"].inc()

                def f_ymm(t, mb=mb, sg=sg, w2i=w2i, he_all=he_all):
                    t.wait_ge(S["w2ld%d" % (w2i % 2)].h, 16 * (w2i // 2 + 1))
                    if mb == 0 and sg == 0:
                        t.wait_ge(S["he"].h, he_all)
                    if sg == 0 and mb >= 2:
                        t.wait_ge(S["ye"].h, mb - 1)
                    for d8 in range(8):
                        dc = sg * 8 + d8
                        ins = t.matmul(pb[mb % 2][:, 0:256], w2b[:, w2i % 2, d8, :],
                                       hslice(dc), start=(dc == 0), stop=(dc == 31))
                    ins.then_inc(S["ym"].h, 1)
                P.t(f_ymm)

            S["ye"].inc()

            def f_yfe(v, mb=mb, ymv=S["ym"].n, trv_prev=S["opm"].n):
                v.wait_ge(S["ym"].h, ymv)
                if mb >= 1:
                    v.wait_ge(S["opm"].h, trv_prev)
                ins = v.tensor_scalar(yfmt[:, :], pb[mb % 2][:, 0:256], b2c[:, mb:mb + 1], None, ALU.add)
                ins.then_inc(S["ye"].h, 1)
            P.v(f_yfe)

            S["opm"].inc()

            def f_ytr(t, mb=mb, yev=S["ye"].n):
                t.wait_ge(S["ye"].h, yev)
                if mb >= 2:
                    t.wait_ge(S["asml"].h, 2 * (mb - 1))
                for sh in range(2):
                    ins = t.matmul(pb[2 + mb % 2][:, sh * 128:(sh + 1) * 128],
                                   yfmt[:, sh * 128:(sh + 1) * 128], identf[:, :],
                                   start=True, stop=True)
                ins.then_inc(S["opm"].h, 1)
            P.t(f_ytr)

            S["asml"].inc(2)

            def f_yte(v, mb=mb, trv=S["opm"].n, scn_p1=S["scn"].n if False else NT, lge_p=S["lge"].n):
                v.wait_ge(S["opm"].h, trv)
                if mb == 0:
                    v.wait_ge(S["scn"].h, 2 * NT)
                    v.wait_ge(S["lge"].h, lge_p)
                for sh in range(2):
                    ins = v.tensor_copy([C0, C1][sh][:, mb * 128:(mb + 1) * 128],
                                        pb[2 + mb % 2][:, sh * 128:(sh + 1) * 128])
                ins.then_inc(S["asml"].h, 2)
            P.v(f_yte)

        asml_y = S["asml"].n
        for half in range(2):
            for sh in range(2):
                S["cc2st%d" % half].inc(16)

                def f_c2st(y, half=half, sh=sh, av=asml_y - 32 + 16 * (half + 1)):
                    y.wait_ge(S["asml"].h, av)
                    y.dma_start(out=dcc2inH[half][sh * 128:(sh + 1) * 128, :],
                                in_=[C0, C1][sh][:, half * 1024:(half + 1) * 1024]
                                ).then_inc(S["cc2st%d" % half].h, 16)
                P.y(f_c2st)

        xn_view = dxn.rearrange("t (a c) -> (t a) c", a=4)
        sce_scan = S["sce"].n
        scs_scan = S["scs"].n
        pd_done = shared["pd_done"]
        for mt in range(4):
            for q in range(4):
                pi = 96 + mt * 4 + q
                S["wold"].inc(16)

                def f_wo(s, mt=mt, q=q, pi=pi, qkm_all=qkm_all, op_prev=S["opm"].n,
                         ppv=16 * (pi // 4 + 1)):
                    s.wait_ge(S["ps%d" % (pi % 4)].h, ppv)
                    if q == 0:
                        if mt == 0:
                            s.wait_ge(S["qkm"].h, qkm_all)
                        else:
                            s.wait_ge(S["opm"].h, op_prev)
                    s.dma_start(out=wout_reg[:, q * 4:(q + 1) * 4, :],
                                in_=dwoutbf[mt * 4 + q, :, :, :]).then_inc(S["wold"].h, 16)
                P.s(f_wo)

            for tc in range(8):
                j = mt * 8 + tc
                pj = j % 2
                xj = pj * 2
                xv = S["hst%d" % pj].inc(16)

                def f_xl(y, mt=mt, tc=tc, xj=xj, j=j, pj=pj, stv=S["ast%d" % pj].n,
                         pdd=pd_done):
                    if j < 2:
                        for h_, n_ in pdd:
                            y.wait_ge(h_, n_)
                    if j >= 2:
                        y.wait_ge(S["ast%d" % pj].h, stv)
                    y.dma_start(out=asmb[:, xj, 0:512],
                                in_=dxown[tc * 128:(tc + 1) * 128, mt * 512:(mt + 1) * 512]
                                ).then_inc(S["hst%d" % pj].h, 16)
                P.y(f_xl)

                xnv = S["hld%d" % pj].inc(16)

                def f_xnl(g, mt=mt, tc=tc, xj=xj, j=j, pj=pj, stv=S["ast%d" % pj].n,
                          pdd=pd_done):
                    if j == 0:
                        g.wait_ge(S["xnst0"].h, 16 * 8)
                        g.wait_ge(S["xnst1"].h, 16 * 8)
                    if j < 2:
                        for h_, n_ in pdd:
                            g.wait_ge(h_, n_)
                    if j >= 2:
                        g.wait_ge(S["ast%d" % pj].h, stv)
                    g.indirect_dma_start(
                        out=asmb[:, xj + 1, 0:512], out_offset=None,
                        in_=xn_view, in_offset=IndirectOffsetOnAxis(xni_u[:, tc * 4 + mt:tc * 4 + mt + 1], 0),
                    ).then_inc(S["hld%d" % pj].h, 16)
                P.g(f_xnl)

                opv = S["opm"].inc()

                def f_op(t, mt=mt, tc=tc, pj=pj, cst_b=S["cst"].n, wo_v=S["wold"].n,
                         asml_prev=S["asml"].n, sce_scan=sce_scan, scs_scan=scs_scan):
                    t.wait_ge(S["wold"].h, wo_v)
                    if mt == 0 and tc == 0:
                        t.wait_ge(S["cst"].h, cst_b)
                        t.wait_ge(S["ccrb"].h, ccrb1)
                        t.wait_ge(S["sce"].h, sce_scan)
                        t.wait_ge(S["scs"].h, scs_scan)
                    if mt * 8 + tc >= 2:
                        t.wait_ge(S["asml"].h, asml_prev - 1)
                    bank = 4 + pj
                    for g_ in range(16):
                        ins = t.matmul(pb[bank][:, :], attn_g[:, g_, tc * 128:(tc + 1) * 128],
                                       wout_reg[:, g_, :], start=(g_ == 0), stop=False)
                    t.matmul(pb[bank][:, :], onesr[0:1, 0:128],
                             brow[0:1, mt * 512:(mt + 1) * 512],
                             start=False, stop=True).then_inc(S["opm"].h, 1)
                P.t(f_op)

                av = S["asml"].inc()

                def f_asm(v, mt=mt, tc=tc, xj=xj, pj=pj, xv=xv, xnv=xnv, opv2=S["opm"].n):
                    v.wait_ge(S["hst%d" % pj].h, xv)
                    v.wait_ge(S["hld%d" % pj].h, xnv)
                    v.wait_ge(S["opm"].h, opv2)
                    v.tensor_add(asmb[:, xj, 0:512], asmb[:, xj, 0:512], asmb[:, xj + 1, 0:512])
                    v.tensor_add(asmb[:, xj, 0:512], asmb[:, xj, 0:512],
                                 pb[4 + pj][:, :]).then_inc(S["asml"].h, 1)
                P.v(f_asm)

                S["ast%d" % pj].inc(16)

                def f_st(y, mt=mt, tc=tc, xj=xj, pj=pj, av=av):
                    y.wait_ge(S["asml"].h, av)
                    y.dma_start(out=dy[tc * 128:(tc + 1) * 128, mt * 512:(mt + 1) * 512],
                                in_=asmb[:, xj, 0:512]).then_inc(S["ast%d" % pj].h, 16)
                P.y(f_st)

        opm_y = S["opm"].n - 32
        for sh in range(2):
            for half in range(2):
                S["ccr2"].inc(16)

                def f_rb2(g, sh=sh, half=half, ypv=S["ypost"].n, lge_p=S["lge"].n,
                          ccv=cc2v[half], opm_y=opm_y):
                    if sh == 0:
                        g.wait_ge(S["lge"].h, lge_p)
                        if half == 0:
                            g.wait_ge(S["opm"].h, opm_y)
                    elif half == 0:
                        g.wait_ge(S["ypost"].h, ypv)
                    g.wait_ge(S["cc"].h, ccv)
                    g.indirect_dma_start(
                        out=C2[:, half * 1024:(half + 1) * 1024], out_offset=None,
                        in_=dcc2outH[half][:, :],
                        in_offset=IndirectOffsetOnAxis(cc2i_u[:, 2 + sh:3 + sh], 0),
                    ).then_inc(S["ccr2"].h, 16)
                P.g(f_rb2)

            S["ypost"].inc()

            def f_yadd(v, sh=sh, rbv=S["ccr2"].n):
                v.wait_ge(S["ccr2"].h, rbv)
                Csh = [C0, C1][sh]
                edge(v, v.tensor_add(Csh[:, :], Csh[:, :], C2[:, :]), S["vch"])
                v.tensor_scalar(Csh[:, :], Csh[:, :], rw[:, sh:sh + 1], None, ALU.mult).then_inc(S["ypost"].h, 1)
            P.v(f_yadd)


        ast_all0 = S["ast0"].n
        ast_all1 = S["ast1"].n
        ypost_all = S["ypost"].n
        for sh in range(2):
            S["sca"].inc(16)

            def f_scat(g, sh=sh, ast_all0=ast_all0, ast_all1=ast_all1, ypost_all=ypost_all):
                if sh == 0:
                    g.wait_ge(S["ast0"].h, ast_all0)
                    g.wait_ge(S["ast1"].h, ast_all1)
                    g.wait_ge(S["ypost"].h, ypost_all)
                g.indirect_dma_start(
                    out=dy[:, :], out_offset=IndirectOffsetOnAxis(sci_u[:, sh:sh + 1], 0),
                    in_=[C0, C1][sh][:, :], in_offset=None,
                    compute_op=ALU.add,
                ).then_inc(S["sca"].h, 16)
            P.g(f_scat)

        if "ffn" in debug:
            he_a = S["he"].n
            ysp = S["yst"].n
            rb_a = S["ccrb"].n

            def f_dbgf(g, he_a=he_a, ysp=ysp, rb_a=rb_a):
                g.wait_ge(S["he"].h, he_a)
                g.wait_ge(S["yst"].h, ysp)
                g.wait_ge(S["ccrb"].h, rb_a)
                for dc in range(32):
                    g.dma_start(out=dbg["d_h"][:, dc * KSEL:(dc + 1) * KSEL], in_=hslice(dc)).then_inc(S["fin"].h, 16)
                g.dma_start(out=dbg["d_ysc"][:, :], in_=dysc[:, :]).then_inc(S["fin"].h, 16)
                for g_ in range(16):
                    g.dma_start(out=dbg["d_aog"][:, g_ * 1024:(g_ + 1) * 1024], in_=attn_g[:, g_, :]).then_inc(S["fin"].h, 16)
            for _ in range(32 + 1 + 16):
                S["fin"].inc(16)
            P.g(f_dbgf)

    p0()
    if upto >= 1:
        p1()
    if upto >= 2:
        p2()
    if upto >= 3:
        p3()

    sem_totals = [(sem.h, sem.n) for sem in S.values() if sem.n > 0]

    def f_fin(g, sem_totals=sem_totals):
        for h, n in sem_totals:
            g.wait_ge(h, n)
    P.g(f_fin)

    with nc.Block() as block:
        @block.tensor
        def _(t):
            for fn in P.ops["tensor"]:
                fn(t)

        @block.vector
        def _(v):
            vv = Chained(v, S["vch"])
            for fn in P.ops["vector"]:
                fn(vv)

        @block.scalar
        def _(s):
            ss = Chained(s, S["sch"])
            for fn in P.ops["scalar"]:
                fn(ss)

        @block.sync
        def _(y):
            for fn in P.ops["sync"]:
                fn(y)

        @block.gpsimd
        def _(g):
            for fn in P.ops["gpsimd"]:
                fn(g)

    return nc, dict(S=S)



def _prep_in_maps(L0_attn_x, L0_attn_w_qkv, L0_attn_b_qkv, L0_attn_w_out, L0_attn_b_out,
                  L0_ffn_Wr, L0_ffn_br, L0_ffn_W1, L0_ffn_b1, L0_ffn_W2, L0_ffn_b2):
    f32 = np.float32
    x = np.asarray(L0_attn_x, f32)
    wqkv = np.asarray(L0_attn_w_qkv, f32)
    bqkv = np.asarray(L0_attn_b_qkv, f32)
    wout = np.asarray(L0_attn_w_out, f32)
    bout = np.asarray(L0_attn_b_out, f32)
    Wr = np.asarray(L0_ffn_Wr, f32)
    br = np.asarray(L0_ffn_br, f32)
    W1 = np.asarray(L0_ffn_W1, f32)
    b1 = np.asarray(L0_ffn_b1, f32)
    W2 = np.asarray(L0_ffn_W2, f32)
    b2 = np.asarray(L0_ffn_b2, f32)

    lam = np.exp(-_slopes(16))
    pos = np.arange(128, dtype=np.float64)
    ident = np.eye(128, dtype=f32)
    tri = (np.arange(128)[:, None] < np.arange(128)[None, :]).astype(f32)
    onesm = np.ones((128, 128), f32)
    iob = np.broadcast_to(np.arange(256, dtype=f32)[None, :], (128, 256)).copy()
    iop = np.arange(128, dtype=f32).reshape(128, 1).copy()
    browv = bout.reshape(1, DM).copy()
    onesrv = np.zeros((1, 256), f32)
    onesrv[0, 0:128] = 1.0
    onesrv[0, 128] = br[0]

    wT = wout.T.copy()
    woutT = np.ascontiguousarray(wT.reshape(16, 128, 4, 512).transpose(2, 1, 0, 3))

    in_maps = []
    for c in range(8):
        s, h = c // 2, c % 2
        heads = np.arange(8 * h, 8 * h + 8)
        blocks = np.empty((24, 128, NF, 128), f32)
        bq = np.empty((128, 24), f32)
        for hh in range(8):
            g = heads[hh]
            for o, base in enumerate((0, DM, 2 * DM)):
                Wsub = wqkv[base + g * 128: base + (g + 1) * 128]
                blocks[hh * 3 + o] = Wsub.reshape(128, NF, 128).transpose(2, 1, 0)
                bq[:, hh * 3 + o] = bqkv[base + g * 128: base + (g + 1) * 128]
        W1sub = W1[:, h * DFFH:(h + 1) * DFFH]
        W1t = np.empty((32, 128, NF, 128), f32)
        for b in range(32):
            W1t[b] = W1sub[:, b * 128:(b + 1) * 128].reshape(NF, 128, 128).transpose(1, 0, 2)
        b1h = np.ascontiguousarray(b1[h * DFFH:(h + 1) * DFFH].reshape(32, 128).T)
        W2sub = W2[h * DFFH:(h + 1) * DFFH, :]
        W2cm = np.ascontiguousarray(W2sub.reshape(32, 128, 16, 128).transpose(2, 1, 0, 3))
        b2cv = np.ascontiguousarray(b2.reshape(16, 128).T) if h == 0 else np.zeros((128, 16), f32)
        wrcv = np.ascontiguousarray(Wr[:, 0].reshape(NF, 128).T)

        lam_h = lam[heads]
        lamcv = np.broadcast_to((lam_h ** 128.0).astype(f32)[None, :], (128, HLOC)).copy()
        dmaskT = np.zeros((128, HLOC, 128), f32)
        dqbv = np.zeros((128, HLOC, 128), f32)
        dkv = np.zeros((128, HLOC), f32)
        for hh in range(8):
            lh = lam_h[hh]
            diff = pos[None, :] - pos[:, None]
            m = np.where(diff >= 0, lh ** np.maximum(diff, 0.0), 0.0)
            dmaskT[:, hh, :] = m.astype(f32)
            dqbv[:, hh, :] = np.broadcast_to((lh ** (pos + 1.0)).astype(f32)[None, :], (128, 128))
            dkv[:, hh] = (lh ** (127.0 - pos)).astype(f32)

        pair = s * 2 + (1 - h)
        cc1idx = np.zeros((128, 16), f32)
        for g in range(16):
            owner = s * 2 + (g // 8)
            lh = g % 8
            r = owner * 512 + (lh % 4) * 128 + np.arange(128)
            cc1idx[:, g] = r * 2 + h
        cc2idx = np.zeros((128, 4), f32)
        for sh in range(2):
            cc2idx[:, sh] = c * KSEL + sh * 128 + np.arange(128)
            cc2idx[:, 2 + sh] = pair * KSEL + sh * 128 + np.arange(128)

        in_maps.append(dict(
            x=np.ascontiguousarray(x[s]),
            wqkvT=blocks, bqkv=bq, woutT=woutT,
            W1t=W1t, b1h=b1h, W2cm=W2cm, b2c=b2cv, Wrc=wrcv,
            boutr=browv, onesr=onesrv,
            identc=ident, tric=tri, onesmc=onesm, iobc=iob, iopc=iop,
            onescolc=np.ones((128, 1), f32),
            epscolc=np.full((128, 1), EPS, f32),
            dmaskc=dmaskT, dqbc=dqbv, dkc=dkv, lamcc=lamcv,
            cc1idx=cc1idx, cc2idx=cc2idx,
            hofsc=np.array([[h * 1024.0]], f32),
            xown=np.ascontiguousarray(x[s, h * OWN:(h + 1) * OWN]),
            xnidx=np.ascontiguousarray(
                ((h * OWN + (np.arange(32)[None, :] // 4) * 128 + np.arange(128)[:, None]) * 4
                 + (np.arange(32)[None, :] % 4)).astype(f32)),
        ))
    return in_maps


_BUILD_CACHE = {}


def _get_program(upto=99, debug=()):
    key = (upto, tuple(debug))
    if key not in _BUILD_CACHE:
        _BUILD_CACHE[key] = build_program(upto=upto, debug=debug)
    return _BUILD_CACHE[key]


def kernel(**inputs):
    in_maps = _prep_in_maps(**inputs)
    nc, meta = _get_program()
    res = run_bass_kernel_spmd(nc, in_maps, core_ids=list(range(8)))
    out = np.empty((4, SEQ, DM), np.float32)
    for c in range(8):
        s, h = c // 2, c % 2
        out[s, h * OWN:(h + 1) * OWN] = res.results[c]["y"][0:OWN]
    return out


# version 11
# speedup vs baseline: 1.0235x; 1.0235x over previous
import math
import numpy as np
import ml_dtypes
import concourse.bass as bass
import concourse.mybir as mybir
from concourse.bass_core_rust import IndirectOffsetOnAxis
from concourse.bass_utils import run_bass_kernel_spmd

F32 = mybir.dt.float32
BF16 = mybir.dt.bfloat16
U32 = mybir.dt.uint32
AF = mybir.ActivationFunctionType
ALU = mybir.AluOpType
AXX = mybir.AxisListType.X

SEQ = 2048
DM = 2048
NF = 16
NT = 16
HLOC = 8
HD = 128
NCH = 16
DFFH = 4096
KSEL = 256
OWN = 1024
TRASH = 256
EPS = 1e-5

ENGINES = ("tensor", "vector", "scalar", "sync", "gpsimd")


def _slopes(n_heads):
    start = 2.0 ** (-(2.0 ** (-(math.log2(n_heads) - 3))))
    return np.array([start ** (i + 1) for i in range(n_heads)], dtype=np.float64)


class Prog:
    def __init__(self):
        self.ops = {e: [] for e in ENGINES}

    def t(self, fn):
        self.ops["tensor"].append(fn)

    def v(self, fn):
        self.ops["vector"].append(fn)

    def s(self, fn):
        self.ops["scalar"].append(fn)

    def y(self, fn):
        self.ops["sync"].append(fn)

    def g(self, fn):
        self.ops["gpsimd"].append(fn)


class Sem:
    def __init__(self, nc, name):
        self.h = nc.alloc_semaphore(name=name)
        self.n = 0

    def inc(self, k=1):
        self.n += k
        return self.n


_CHAIN_OPS = frozenset((
    "tensor_copy", "tensor_tensor", "tensor_scalar", "tensor_single_scalar",
    "tensor_add", "tensor_sub", "tensor_mul", "tensor_max", "tensor_relu",
    "tensor_scalar_add", "tensor_scalar_sub", "tensor_scalar_mul",
    "tensor_reduce", "reduce_sum", "reduce_max", "scalar_tensor_tensor",
    "tensor_tensor_scan", "memset", "memzero", "reciprocal", "select",
    "copy_predicated", "activation", "copy", "add", "mul", "sqrt", "square", "sign",
))


class ChainIns:

    def __init__(self, chain, n):
        self._c = chain
        self._n = n

    def then_inc(self, sem, val=1):
        self._c._e.wait_ge(self._c._sem.h, self._n)
        self._c._e.sem_inc(sem, val)
        return self


class Chained:

    def __init__(self, eng, sem):
        self._e = eng
        self._sem = sem

    def __getattr__(self, name):
        attr = getattr(self._e, name)
        if name in _CHAIN_OPS:
            def wrapped(*a, **k):
                self._e.wait_ge(self._sem.h, self._sem.n)
                ins = attr(*a, **k)
                self._sem.inc()
                ins.then_inc(self._sem.h, 1)
                return ChainIns(self, self._sem.n)
            return wrapped
        return attr


def build_program(upto=99, debug=()):
    nc = bass.Bass("TRN2", target_bir_lowering=False)
    P = Prog()

    def edge(eng, ins, sem):
        k = sem.inc()
        ins.then_inc(sem.h, 1)
        eng.wait_ge(sem.h, k)

    dx = nc.dram_tensor("x", [SEQ, DM], F32, kind="ExternalInput").ap()
    dwqkv = nc.dram_tensor("wqkvT", [24, 128, NF, 128], F32, kind="ExternalInput").ap()
    dbqkv = nc.dram_tensor("bqkv", [128, 24], F32, kind="ExternalInput").ap()
    dwout = nc.dram_tensor("woutT", [4, 128, NF, 512], F32, kind="ExternalInput").ap()
    dW1 = nc.dram_tensor("W1t", [32, 128, NF, 128], F32, kind="ExternalInput").ap()
    db1 = nc.dram_tensor("b1h", [128, 32], F32, kind="ExternalInput").ap()
    dW2 = nc.dram_tensor("W2cm", [16, 128, 32, 128], F32, kind="ExternalInput").ap()
    db2 = nc.dram_tensor("b2c", [128, 16], F32, kind="ExternalInput").ap()
    dWr = nc.dram_tensor("Wrc", [128, NF], F32, kind="ExternalInput").ap()
    dbrow = nc.dram_tensor("boutr", [1, DM], F32, kind="ExternalInput").ap()
    donesr = nc.dram_tensor("onesr", [1, 256], F32, kind="ExternalInput").ap()
    dident = nc.dram_tensor("identc", [128, 128], F32, kind="ExternalInput").ap()
    dtri = nc.dram_tensor("tric", [128, 128], F32, kind="ExternalInput").ap()
    donesm = nc.dram_tensor("onesmc", [128, 128], F32, kind="ExternalInput").ap()
    diob = nc.dram_tensor("iobc", [128, 256], F32, kind="ExternalInput").ap()
    diop = nc.dram_tensor("iopc", [128, 1], F32, kind="ExternalInput").ap()
    donescol = nc.dram_tensor("onescolc", [128, 1], F32, kind="ExternalInput").ap()
    deps = nc.dram_tensor("epscolc", [128, 1], F32, kind="ExternalInput").ap()
    ddmask = nc.dram_tensor("dmaskc", [128, HLOC, 128], F32, kind="ExternalInput").ap()
    ddqb = nc.dram_tensor("dqbc", [128, HLOC, 128], F32, kind="ExternalInput").ap()
    ddk = nc.dram_tensor("dkc", [128, HLOC], F32, kind="ExternalInput").ap()
    dlamc = nc.dram_tensor("lamcc", [128, HLOC], F32, kind="ExternalInput").ap()
    dcc1i = nc.dram_tensor("cc1idx", [128, 16], F32, kind="ExternalInput").ap()
    dcc2i = nc.dram_tensor("cc2idx", [128, 4], F32, kind="ExternalInput").ap()
    dhofs = nc.dram_tensor("hofsc", [1, 1], F32, kind="ExternalInput").ap()
    dxown = nc.dram_tensor("xown", [OWN, DM], F32, kind="ExternalInput").ap()
    dxni = nc.dram_tensor("xnidx", [128, 32], F32, kind="ExternalInput").ap()

    dy = nc.dram_tensor("y", [OWN + TRASH, DM], F32, kind="ExternalOutput").ap()

    dxn = nc.dram_tensor("xn_dram", [SEQ, DM], F32).ap()
    dlrow = nc.dram_tensor("lrow_dram", [1, SEQ], F32).ap()
    dhbuf = nc.dram_tensor("h_dram", [32, 128, KSEL], BF16).ap()
    dysc = nc.dram_tensor("ysc_dram", [KSEL, DM], F32).ap()
    dW1bf = nc.dram_tensor("w1bf", [32, 128, NF, 128], BF16).ap()
    dW2bf = nc.dram_tensor("w2bf", [64, 128, 8, 128], BF16).ap()
    dwoutbf = nc.dram_tensor("woutbf", [16, 128, 4, 512], BF16).ap()
    dcc1inA = nc.dram_tensor("cc1_inA", [512, SEQ], BF16).ap()
    dcc1inB = nc.dram_tensor("cc1_inB", [512, SEQ], BF16).ap()
    dcc1outA = nc.dram_tensor("cc1_outA", [4096, SEQ], BF16).ap()
    dcc1outB = nc.dram_tensor("cc1_outB", [4096, SEQ], BF16).ap()
    dcc2inH = [nc.dram_tensor("cc2_in%d" % i, [KSEL, DM // 2], F32).ap() for i in range(2)]
    dcc2outH = [nc.dram_tensor("cc2_out%d" % i, [8 * KSEL, DM // 2], F32).ap() for i in range(2)]

    dbg = {}
    if "xnfm" in debug:
        dbg["d_xnfm"] = nc.dram_tensor("d_xnfm", [128, NF * SEQ], F32, kind="ExternalOutput").ap()
    if "router" in debug:
        dbg["d_ltm"] = nc.dram_tensor("d_ltm", [128, 16], F32, kind="ExternalOutput").ap()
        dbg["d_rank"] = nc.dram_tensor("d_rank", [128, 16], F32, kind="ExternalOutput").ap()
        dbg["d_slott"] = nc.dram_tensor("d_slott", [128, 2], F32, kind="ExternalOutput").ap()
        dbg["d_rw"] = nc.dram_tensor("d_rw", [128, 2], F32, kind="ExternalOutput").ap()
        dbg["d_sci"] = nc.dram_tensor("d_sci", [128, 2], F32, kind="ExternalOutput").ap()
        dbg["d_fxfm"] = nc.dram_tensor("d_fxfm", [128, NF * KSEL], F32, kind="ExternalOutput").ap()
    if "ffn" in debug:
        dbg["d_h"] = nc.dram_tensor("d_h", [128, 32 * KSEL], BF16, kind="ExternalOutput").ap()
        dbg["d_ysc"] = nc.dram_tensor("d_ysc", [KSEL, DM], F32, kind="ExternalOutput").ap()
        dbg["d_aog"] = nc.dram_tensor("d_aog", [128, 16 * 1024], BF16, kind="ExternalOutput").ap()
    if "attn" in debug:
        dbg["d_attn"] = nc.dram_tensor("d_attn", [128, HLOC * 1024], BF16, kind="ExternalOutput").ap()
        dbg["d_cc1"] = nc.dram_tensor("d_cc1", [1024, SEQ], BF16, kind="ExternalOutput").ap()
        dbg["d_qf"] = nc.dram_tensor("d_qf", [128, 4 * SEQ], BF16, kind="ExternalOutput").ap()
        dbg["d_kf"] = nc.dram_tensor("d_kf", [128, 4 * SEQ], BF16, kind="ExternalOutput").ap()
        dbg["d_vf"] = nc.dram_tensor("d_vf", [128, 4 * SEQ], BF16, kind="ExternalOutput").ap()

    S = {}
    for name in (
        "cst", "xld", "vmean", "ssq", "vxn", "xnst", "tp", "etp",
        "lgm", "lge", "rkm", "rke", "fxg", "fxt",
        "wqld", "qkm", "qke", "w1ld", "hm", "he", "hst",
        "scn", "sce", "scs", "cc1st", "cc",
        "w2ld", "hld", "ym", "ye", "yst", "cc2st", "ccrb", "ypost",
        "wold", "opm", "asml", "ast", "sca", "fin", "vch", "sch",
        "xld0", "xld1", "xnst0", "xnst1", "wqld0", "wqld1", "w1ld0", "w1ld1", "ccr2",
        "w2ld0", "w2ld1", "cc1st0", "cc1st1", "hst0", "hst1", "hld0", "hld1",
        "ast0", "ast1", "fxg0", "fxg1", "lrb", "yrl",
        "pd0", "pd1", "pd2", "pd3", "ps0", "ps1", "ps2", "ps3",
        "cc2st0", "cc2st1", "w1ld2", "w1ld3",
    ):
        S[name] = Sem(nc, "s_" + name)

    ctxs = []

    def alloc(name, shape, dtype):
        cm = nc.sbuf_tensor(name, shape, dtype)
        t = cm.__enter__()
        ctxs.append(cm)
        return t

    A = alloc("arenaA", [128, NF, SEQ], BF16)
    Bq = alloc("arenaBq", [128, 4, SEQ], BF16)
    Bk = alloc("arenaBk", [128, 4, SEQ], BF16)
    Bv = alloc("arenaBv", [128, 4, SEQ], BF16)
    C0 = alloc("arenaC0", [128, DM], F32)
    C1 = alloc("arenaC1", [128, DM], F32)
    C2 = alloc("arenaC2", [128, DM], F32)

    asmb = alloc("asmb", [128, 4, 1024], F32)
    fx_fm = alloc("fx_fm", [128, NF, KSEL], BF16)
    pstage = alloc("pstage", [128, 2, 4, 128], BF16)

    wqkvb = alloc("wqkvb", [128, 2, NF, 128], BF16)
    w1b = alloc("w1b", [128, 2, NF, 128], BF16)
    w2b = alloc("w2b", [128, 2, 8, 128], BF16)

    brow = alloc("brows", [1, DM], F32)
    onesr = alloc("onesrs", [1, 256], F32)
    ident = alloc("idents", [128, 128], BF16)
    identf = alloc("identfs", [128, 128], F32)
    tri = alloc("tris", [128, 128], BF16)
    onesm = alloc("onesms", [128, 128], BF16)
    iob = alloc("iobs", [128, 256], BF16)
    iop = alloc("iops", [128, 1], F32)
    onescol = alloc("onescols", [128, 1], F32)
    epscol = alloc("epscols", [128, 1], F32)
    dmask = alloc("dmasks", [128, HLOC, 128], BF16)
    dqb = alloc("dqbs", [128, HLOC, 128], BF16)
    dkc = alloc("dks", [128, HLOC], F32)
    lamc = alloc("lamcs", [128, HLOC], F32)
    bqkvc = alloc("bqkvs", [128, 24], F32)
    b1c = alloc("b1s", [128, 32], F32)
    b2c = alloc("b2s", [128, 16], F32)
    wrc = alloc("wrs", [128, NF], BF16)
    cc1i_u = alloc("cc1i_u", [128, 16], U32)
    cc2i_u = alloc("cc2i_u", [128, 4], U32)
    xni_u = alloc("xni_u", [128, 32], U32)
    hofs = alloc("hofss", [1, 1], F32)

    ltm = alloc("ltm", [128, 16], F32)
    lrow = brow
    lrow16 = alloc("lrow16", [16, 128], F32)
    rank = alloc("ranks", [128, 16], F32)
    flag = alloc("flags", [128, 16], F32)
    flagb = alloc("flagbs", [128, 16], BF16)
    etm = alloc("etms", [128, 16], F32)
    posp = alloc("posps", [128, 16], F32)
    mcol = alloc("mcols", [128, 1], F32)
    negmx = alloc("negms", [128, 1], F32)
    zcol = alloc("zcols", [128, 1], F32)
    izval = alloc("izvals", [1, 1], F32)
    izb = alloc("izbs", [128, 1], F32)
    slott = alloc("slotts", [128, 2], F32)
    slote = alloc("slotes", [128, 2], F32)
    rw = alloc("rws", [128, 2], F32)
    tokc = alloc("tokcs", [128, 1], F32)
    fxi_u = alloc("fxi_u", [128, 2], U32)
    sci_f = alloc("sci_f", [128, 2], F32)
    sci_u = alloc("sci_u", [128, 2], U32)

    S_f = alloc("S_f", [128, 4, 128], F32)
    S_b = alloc("S_b", [128, 4, 128], BF16)
    kT_s = alloc("kT_s", [128, 2, 4, 128], BF16)
    vT_s = alloc("vT_s", [128, 2, 4, 128], BF16)
    sc_s = alloc("sc_s", [128, 2, 4, 128], BF16)
    lnm = alloc("lnm", [128, 2], F32)
    lnr = alloc("lnr", [128, 2], F32)

    xbuf = [C0, C1]
    lbuf = C2
    yfmt = C2[:, 0:256]
    junkb = Bv[:, 0, :]
    xnbf = [Bq[:, 0, :], Bq[:, 1, :]]
    fxtm = [Bk[:, 0, :], Bk[:, 1, :]]

    pb = []
    for i in range(8):
        cm = nc.psum_tensor(f"pb{i}", [128, 512], F32)
        pb.append(cm.__enter__())
        ctxs.append(cm)
    pbh = [pb[7][:, 0:32], pb[7][:, 32:64]]

    def p0():
        pairs = [
            (onesr, donesr), (ident, dident), (identf, dident),
            (tri, dtri), (onesm, donesm), (iob, diob), (iop, diop), (onescol, donescol), (epscol, deps), (lamc, dlamc),
            (dmask, ddmask), (dqb, ddqb), (dkc, ddk), (bqkvc, dbqkv),
            (b1c, db1), (b2c, db2), (wrc, dWr), (hofs, dhofs),
        ]

        def fn(g, pairs=pairs):
            for dst, src in pairs:
                g.dma_start(out=dst[...], in_=src[...]).then_inc(S["cst"].h, 16)
            g.dma_start(out=cc1i_u[:, :], in_=dcc1i[:, :]).then_inc(S["cst"].h, 16)
            g.dma_start(out=cc2i_u[:, :], in_=dcc2i[:, :]).then_inc(S["cst"].h, 16)
            g.dma_start(out=xni_u[:, :], in_=dxni[:, :]).then_inc(S["cst"].h, 16)
        for _ in range(len(pairs) + 3):
            S["cst"].inc(16)
        P.g(fn)

    def p1():
        stop_ln = "stop_ln" in debug
        stop_router = "stop_router" in debug
        stop_slots = "stop_slots" in debug
        cst_all = S["cst"].n
        for tc in range(NT):
            par = tc % 2
            xw = S["xld%d" % par].inc(16)

            def f_x(y, tc=tc, par=par):
                if tc >= 2:
                    y.wait_ge(S["xnst%d" % par].h, (tc // 2) * 16)
                y.dma_start(out=xbuf[par][:, :], in_=dx[tc * 128:(tc + 1) * 128, :]).then_inc(S["xld%d" % par].h, 16)
            P.y(f_x)

            vm = S["vmean"].inc()

            def f_mean(v, par=par, xw=xw, cst_all=cst_all):
                v.wait_ge(S["cst"].h, cst_all)
                v.wait_ge(S["xld%d" % par].h, xw)
                edge(v, v.reduce_sum(lnm[:, par:par + 1], xbuf[par][:, :], axis=AXX), S["vch"])
                v.tensor_scalar(lnm[:, par:par + 1], lnm[:, par:par + 1], -1.0 / DM, None,
                                ALU.mult).then_inc(S["vmean"].h, 1)
            P.v(f_mean)

            sq = S["ssq"].inc()

            def f_sq(s, par=par, vm=vm):
                s.wait_ge(S["vmean"].h, vm)
                edge(s, s.activation(junkb[:, :], xbuf[par][:, :], AF.Square, bias=lnm[:, par:par + 1],
                                     scale=1.0, accum_out=lnr[:, par:par + 1]), S["sch"])
                edge(s, s.activation(lnr[:, par:par + 1], lnr[:, par:par + 1], AF.Ln,
                                     bias=epscol[:, :], scale=1.0 / DM), S["sch"])
                s.activation(lnr[:, par:par + 1], lnr[:, par:par + 1], AF.Exp,
                             scale=-0.5).then_inc(S["ssq"].h, 1)
            P.s(f_sq)

            vx = S["vxn"].inc()

            def f_norm(v, par=par, sq=sq, tc=tc):
                v.wait_ge(S["ssq"].h, sq)
                edge(v, v.tensor_scalar(xbuf[par][:, :], xbuf[par][:, :], lnm[:, par:par + 1],
                                        lnr[:, par:par + 1], ALU.add, ALU.mult), S["vch"])
                if tc >= 2:
                    v.wait_ge(S["tp"].h, tc - 1)
                v.tensor_copy(xnbf[par][:, :], xbuf[par][:, :]).then_inc(S["vxn"].h, 1)
            P.v(f_norm)

            S["xnst%d" % par].inc(16)

            def f_store(s, tc=tc, par=par, vx=vx):
                s.wait_ge(S["vxn"].h, vx)
                s.dma_start(out=dxn[tc * 128:(tc + 1) * 128, :], in_=xbuf[par][:, :]).then_inc(S["xnst%d" % par].h, 16)
            P.s(f_store)

            tpv = S["tp"].inc()

            def f_tp(t, tc=tc, par=par, vx=vx, tpv=tpv, cst_all=cst_all):
                if tc == 0:
                    t.wait_ge(S["cst"].h, cst_all)
                t.wait_ge(S["vxn"].h, vx)
                if tpv > 1:
                    t.wait_ge(S["etp"].h, tpv - 1)
                for fc in range(NF):
                    bank = fc // 4
                    col = fc % 4
                    ins = t.matmul(pb[bank][:, col * 128:(col + 1) * 128], xnbf[par][:, fc * 128:(fc + 1) * 128],
                                   ident[:, :], start=True, stop=True)
                ins.then_inc(S["tp"].h, 1)
            P.t(f_tp)

            ev = S["etp"].inc()

            def f_etp(s, tc=tc, tpv=tpv):
                s.wait_ge(S["tp"].h, tpv)
                for bank in range(4):
                    ins = s.copy(A[:, bank * 4:(bank + 1) * 4, tc * 128:(tc + 1) * 128],
                                 pb[bank][:, :].rearrange("p (f c) -> p f c", f=4))
                    if bank == 3:
                        ins.then_inc(S["etp"].h, 1)
            P.s(f_etp)

            S["lgm"].inc()

            def f_lg(t, tc=tc, ev=ev, cst_all=cst_all):
                if tc == 0:
                    t.matmul(pbh[0][:, 19:20], onesr[0:1, 0:128], onesr[0:1, 128:129],
                             start=True, stop=True)
                t.wait_ge(S["etp"].h, ev)
                for fc in range(NF):
                    ins = t.matmul(pbh[0][:, tc:tc + 1], A[:, fc, tc * 128:(tc + 1) * 128], wrc[:, fc:fc + 1],
                                   start=(fc == 0), stop=(fc == NF - 1))
                ins.then_inc(S["lgm"].h, 1)
            P.t(f_lg)

        if stop_ln:
            return
        lgall = S["lgm"].n
        lge1 = S["lge"].inc()

        def f_lev(v, lgall=lgall):
            v.wait_ge(S["lgm"].h, lgall)
            v.tensor_scalar(ltm[:, :], pbh[0][:, 0:16], pbh[0][:, 19:20], None, ALU.add).then_inc(S["lge"].h, 1)
        P.v(f_lev)

        rk1 = S["rkm"].inc()

        def f_lt(t, lge1=lge1):
            t.wait_ge(S["lge"].h, lge1)
            t.matmul(pb[4][0:16, 0:128], ltm[:, :], identf[:, :], start=True,
                     stop=True).then_inc(S["rkm"].h, 1)
        P.t(f_lt)

        lge2 = S["lge"].inc()

        def f_lte(v, rk1=rk1):
            v.wait_ge(S["rkm"].h, rk1)
            v.tensor_copy(lrow16[:, :], pb[4][0:16, 0:128]).then_inc(S["lge"].h, 1)
        P.v(f_lte)

        st1 = S["lrb"].inc(16)

        def f_lrd(y, lge2=lge2):
            y.wait_ge(S["lge"].h, lge2)
            y.dma_start(out=dlrow.rearrange("o (a b) -> (o a) b", a=16), in_=lrow16[:, :]).then_inc(S["lrb"].h, 16)
        P.y(f_lrd)
        st2 = S["lrb"].inc(16)

        def f_lrl(y, st1=st1):
            y.wait_ge(S["lrb"].h, st1)
            y.dma_start(out=lrow[:, :], in_=dlrow[:, :]).then_inc(S["lrb"].h, 16)
        P.y(f_lrl)

        rk2 = S["rkm"].inc()

        def f_lb(t, st2=st2):
            t.wait_ge(S["lrb"].h, st2)
            for tc in range(NT):
                bank = tc // 4
                col = tc % 4
                ins = t.matmul(pb[bank][:, col * 128:(col + 1) * 128], onesr[0:1, 0:128],
                               lrow[0:1, tc * 128:(tc + 1) * 128], start=True, stop=True)
            ins.then_inc(S["rkm"].h, 1)
        P.t(f_lb)

        lge3 = S["lge"].inc()

        def f_lbe(v, rk2=rk2):
            v.wait_ge(S["rkm"].h, rk2)
            for bank in range(4):
                ins = v.tensor_copy(lbuf[:, bank * 512:(bank + 1) * 512], pb[bank][:, :])
                edge(v, ins, S["vch"])
            edge(v, v.reduce_max(mcol[:, :], lbuf[:, :], axis=AXX), S["vch"])
            v.tensor_scalar(negmx[:, :], mcol[:, :], -1.0, None, ALU.mult).then_inc(S["lge"].h, 1)
        P.v(f_lbe)

        lge4 = S["lge"].inc()

        def f_rank(v, lge3=lge3):
            v.wait_ge(S["lge"].h, lge3)
            for tc in range(NT):
                ins = v.tensor_scalar(junkb[:, :], lbuf[:, :], ltm[:, tc:tc + 1], 0.0, ALU.is_gt,
                                      ALU.add, accum_out=rank[:, tc:tc + 1])
                edge(v, ins, S["vch"])
            edge(v, v.tensor_scalar(flag[:, :], rank[:, :], 255.5, None, ALU.is_lt), S["vch"])
            v.tensor_copy(flagb[:, :], flag[:, :]).then_inc(S["lge"].h, 1)
        P.v(f_rank)

        se1 = S["ssq"].inc()

        def f_e(s, lge4=lge4):
            s.wait_ge(S["lge"].h, lge4)
            s.activation(etm[:, :], ltm[:, :], AF.Exp, bias=negmx[:, :], scale=1.0).then_inc(S["ssq"].h, 1)
        P.s(f_e)

        rk3 = S["rkm"].inc()

        def f_pf(t, lge4=lge4):
            t.wait_ge(S["lge"].h, lge4)
            for tc in range(NT):
                for tp_ in range(tc):
                    t.matmul(pbh[1][:, tc:tc + 1], onesm[:, :], flagb[:, tp_:tp_ + 1],
                             start=(tp_ == 0), stop=False)
                ins = t.matmul(pbh[1][:, tc:tc + 1], tri[:, :], flagb[:, tc:tc + 1],
                               start=(tc == 0), stop=True)
            ins.then_inc(S["rkm"].h, 1)
        P.t(f_pf)

        lge5 = S["lge"].inc()

        def f_pp(v, rk3=rk3, se1=se1):
            v.wait_ge(S["rkm"].h, rk3)
            v.wait_ge(S["ssq"].h, se1)
            edge(v, v.scalar_tensor_tensor(posp[:, :], flag[:, :], -999.0, pbh[1][:, 0:16],
                                           ALU.mult, ALU.add), S["vch"])
            v.tensor_scalar(posp[:, :], posp[:, :], 999.0, None, ALU.add)
            edge(v, v.tensor_mul(etm[:, :], etm[:, :], flag[:, :]), S["vch"])
            v.reduce_sum(zcol[:, :], etm[:, :], axis=AXX).then_inc(S["lge"].h, 1)
        P.v(f_pp)

        rk4 = S["rkm"].inc()

        def f_z(t, lge5=lge5):
            t.wait_ge(S["lge"].h, lge5)
            t.matmul(pbh[0][0:1, 16:17], zcol[:, :], onescol[:, :], start=True, stop=True)
            t.matmul(pbh[0][:, 18:19], onesr[0:1, 0:128], hofs[:, :], start=True,
                     stop=True).then_inc(S["rkm"].h, 1)
        P.t(f_z)

        lge6 = S["lge"].inc()

        def f_iz(v, rk4=rk4):
            v.wait_ge(S["rkm"].h, rk4)
            v.reciprocal(izval[:, :], pbh[0][0:1, 16:17]).then_inc(S["lge"].h, 1)
        P.v(f_iz)

        rk5 = S["rkm"].inc()

        def f_izb(t, lge6=lge6):
            t.wait_ge(S["lge"].h, lge6)
            t.matmul(pbh[0][:, 17:18], onesr[0:1, 0:128], izval[:, :], start=True,
                     stop=True).then_inc(S["rkm"].h, 1)
        P.t(f_izb)

        if stop_router:
            return
        for sh in range(2):
            for tc in range(NT):
                par = tc % 2
                pv = S["rke"].inc()

                def f_ptile(v, tc=tc, par=par, sh=sh, lge5=lge5, pv=pv):
                    if tc == 0 and sh == 0:
                        v.wait_ge(S["lge"].h, lge5)
                        v.wait_ge(S["tp"].h, NT)
                        v.wait_ge(S["xnst0"].h, 16 * 8)
                        v.wait_ge(S["xnst1"].h, 16 * 8)
                    if sh * NT + tc >= 2:
                        v.wait_ge(S["scn"].h, sh * NT + tc - 1)
                    v.tensor_scalar(tokc[:, :], iop[:, :], float(128 * tc), None, ALU.add)
                    v.tensor_scalar(xbuf[par][:, 0:128], iob[:, sh * 128:(sh + 1) * 128],
                                    posp[:, tc:tc + 1], None, ALU.is_equal)
                    v.tensor_copy(xbuf[par][:, 256:257], tokc[:, :])
                    v.tensor_copy(xbuf[par][:, 257:258], etm[:, tc:tc + 1]).then_inc(S["rke"].h, 1)
                P.v(f_ptile)

                S["scn"].inc()

                def f_pmm(t, tc=tc, par=par, sh=sh, pv=pv):
                    t.wait_ge(S["rke"].h, pv)
                    ins = t.matmul(pbh[1][:, 18 + sh * 2:20 + sh * 2],
                                   xbuf[par][:, 0:128], xbuf[par][:, 256:258],
                                   start=(tc == 0), stop=(tc == NT - 1))
                    ins.then_inc(S["scn"].h, 1)
                P.t(f_pmm)

        lge7 = S["lge"].inc()

        def f_slote(v, rk5=rk5):
            v.wait_ge(S["scn"].h, 2 * NT)
            edge(v, v.tensor_copy(izb[:, :], pbh[0][:, 17:18]), S["vch"])
            edge(v, v.tensor_copy(mcol[:, :], pbh[0][:, 18:19]), S["vch"])
            for sh in range(2):
                v.tensor_copy(slott[:, sh:sh + 1], pbh[1][:, 18 + sh * 2:19 + sh * 2])
                ins = v.tensor_copy(slote[:, sh:sh + 1], pbh[1][:, 19 + sh * 2:20 + sh * 2])
            edge(v, ins, S["vch"])
            v.tensor_scalar(rw[:, :], slote[:, :], izb[:, 0:1], None, ALU.mult)
            v.tensor_copy(fxi_u[:, :], slott[:, :])
            edge(v, v.tensor_scalar(sci_f[:, :], slott[:, :], mcol[:, 0:1], None, ALU.subtract), S["vch"])
            edge(v, v.tensor_scalar(lnm[:, 0:2], sci_f[:, :], -0.5, None, ALU.is_gt), S["vch"])
            edge(v, v.tensor_scalar(lnr[:, 0:2], sci_f[:, :], 1023.5, None, ALU.is_lt), S["vch"])
            edge(v, v.tensor_mul(lnm[:, 0:2], lnm[:, 0:2], lnr[:, 0:2]), S["vch"])
            edge(v, v.tensor_scalar(lnr[:, 0:1], iop[:, :], 1024.0, None, ALU.add), S["vch"])
            edge(v, v.tensor_scalar(lnr[:, 1:2], iop[:, :], 1024.0 + 128.0, None, ALU.add), S["vch"])
            edge(v, v.tensor_sub(sci_f[:, :], sci_f[:, :], lnr[:, 0:2]), S["vch"])
            edge(v, v.tensor_mul(sci_f[:, :], sci_f[:, :], lnm[:, 0:2]), S["vch"])
            edge(v, v.tensor_add(sci_f[:, :], sci_f[:, :], lnr[:, 0:2]), S["vch"])
            v.tensor_copy(sci_u[:, :], sci_f[:, :]).then_inc(S["lge"].h, 1)
        P.v(f_slote)

        if stop_slots:
            return
        rke_fx0 = None
        for sh in range(2):
            fg = S["fxg%d" % sh].inc(16)

            def f_fxg(g, sh=sh, lge7=lge7):
                g.wait_ge(S["lge"].h, lge7)
                g.wait_ge(S["xnst0"].h, 16 * 8)
                g.wait_ge(S["xnst1"].h, 16 * 8)
                g.indirect_dma_start(
                    out=fxtm[sh][:, :], out_offset=None,
                    in_=dxn[:, :], in_offset=IndirectOffsetOnAxis(fxi_u[:, sh:sh + 1], 0),
                ).then_inc(S["fxg%d" % sh].h, 16)
            P.g(f_fxg)

            ft = S["fxt"].inc()

            def f_fxt(t, sh=sh, fg=fg, ft=ft, rke_fx0=rke_fx0):
                t.wait_ge(S["fxg%d" % sh].h, fg)
                if ft > 1:
                    t.wait_ge(S["rke"].h, rke_fx0)
                for fc in range(NF):
                    bank = 3 + fc // 4
                    col = fc % 4
                    ins = t.matmul(pb[bank][:, col * 128:(col + 1) * 128], fxtm[sh][:, fc * 128:(fc + 1) * 128],
                                   ident[:, :], start=True, stop=True)
                ins.then_inc(S["fxt"].h, 1)
            P.t(f_fxt)

            S["rke"].inc(4)

            def f_fxe(s, sh=sh, ft=ft):
                s.wait_ge(S["fxt"].h, ft)
                for bank in range(4):
                    ins = s.copy(fx_fm[:, bank * 4:(bank + 1) * 4, sh * 128:(sh + 1) * 128],
                                 pb[3 + bank][:, :].rearrange("p (f c) -> p f c", f=4))
                    if bank == 3:
                        ins.then_inc(S["rke"].h, 4)
            P.s(f_fxe)
            rke_fx0 = S["rke"].n

        if "xnfm" in debug and not stop_ln:
            etp_all = S["etp"].n

            def f_dbg1(g, etp_all=etp_all):
                g.wait_ge(S["etp"].h, etp_all)
                for fc in range(NF):
                    g.dma_start(out=dbg["d_xnfm"][:, fc * SEQ:(fc + 1) * SEQ],
                                in_=A[:, fc, :]).then_inc(S["fin"].h, 16)
            for fc in range(NF):
                S["fin"].inc(16)
            P.g(f_dbg1)
        if "router" in debug and not (stop_ln or stop_router or stop_slots):
            lge_all = S["lge"].n
            rke_all = S["rke"].n

            def f_dbg2(g, lge_all=lge_all, rke_all=rke_all):
                g.wait_ge(S["lge"].h, lge_all)
                g.wait_ge(S["rke"].h, rke_all)
                g.dma_start(out=dbg["d_ltm"][:, :], in_=ltm[:, :]).then_inc(S["fin"].h, 16)
                g.dma_start(out=dbg["d_rank"][:, :], in_=rank[:, :]).then_inc(S["fin"].h, 16)
                g.dma_start(out=dbg["d_slott"][:, :], in_=slott[:, :]).then_inc(S["fin"].h, 16)
                g.dma_start(out=dbg["d_rw"][:, :], in_=rw[:, :]).then_inc(S["fin"].h, 16)
                g.dma_start(out=dbg["d_sci"][:, :], in_=sci_f[:, :]).then_inc(S["fin"].h, 16)
                for fc in range(NF):
                    g.dma_start(out=dbg["d_fxfm"][:, fc * KSEL:(fc + 1) * KSEL],
                                in_=fx_fm[:, fc, :]).then_inc(S["fin"].h, 16)
            for _ in range(5 + NF):
                S["fin"].inc(16)
            P.g(f_dbg2)


    shared = {}

    def p2():
        stop_qkv = "stop_qkv" in debug
        S_bv = {}
        SCw = {k: {} for k in ("ke", "ve", "me", "om", "dm", "su", "ae")}
        qarena = [Bq, Bk, Bv]
        rke_all_p1 = S["rke"].n
        lge_all_p1 = S["lge"].n
        vch_p1 = S["vch"].n
        pd_pieces = []
        for ob in range(32):
            pd_pieces.append((dW1[ob, :, :, :], dW1bf[ob, :, :, :], 2048))
        for i in range(64):
            pd_pieces.append((dW2[i // 4, :, (i % 4) * 8:(i % 4 + 1) * 8, :], dW2bf[i, :, :, :], 1024))
        for i in range(16):
            pd_pieces.append((dwout[i // 4, :, (i % 4) * 4:(i % 4 + 1) * 4, :], dwoutbf[i, :, :, :], 2048))
        bnc = [asmb[:, j, :].bitcast(BF16) for j in range(4)]

        def emit_pd(i):
            sl = i % 4
            S["pd%d" % sl].inc(16)

            def f_pdl(g, i=i, sl=sl, psv=16 * (i // 4)):
                if 32 <= i < 96:
                    g.wait_ge(S["qkm"].h, 12)
                elif i >= 96:
                    g.wait_ge(S["qkm"].h, 18)
                if i >= 4:
                    g.wait_ge(S["ps%d" % sl].h, psv)
                src_ap, dst_ap, n = pd_pieces[i]
                g.dma_start(out=bnc[sl][:, 0:n], in_=src_ap).then_inc(S["pd%d" % sl].h, 16)
            P.g(f_pdl)
            S["ps%d" % sl].inc(16)

            def f_pds(y, i=i, sl=sl, pdv=16 * (i // 4 + 1)):
                y.wait_ge(S["pd%d" % sl].h, pdv)
                src_ap, dst_ap, n = pd_pieces[i]
                y.dma_start(out=dst_ap, in_=bnc[sl][:, 0:n]).then_inc(S["ps%d" % sl].h, 16)
            P.y(f_pds)

        pd_emitted = 0
        for gb in range(24):
            wq = S["wqld%d" % (gb % 2)].inc(16)

            def f_wdma(g, gb=gb):
                if gb >= 2:
                    g.wait_ge(S["qkm"].h, gb - 1)
                g.dma_start(out=wqkvb[:, gb % 2, :, :], in_=dwqkv[gb, :, :, :]).then_inc(S["wqld%d" % (gb % 2)].h, 16)
            P.g(f_wdma)
            if gb < 12:
                target = min(32, (gb + 1) * 3)
            else:
                target = 32 + (gb - 11) * 4
            while pd_emitted < target:
                emit_pd(pd_emitted)
                pd_emitted += 1
        while pd_emitted < len(pd_pieces):
            emit_pd(pd_emitted)
            pd_emitted += 1
        shared["pd_done"] = [(S["ps%d" % j].h, S["ps%d" % j].n) for j in range(4)]

        for hg in range(2):
            if hg == 1:
                for ob in range(4):
                    S["w1ld%d" % ob].inc(16)

                    def f_w1e(s, ob=ob):
                        s.wait_ge(S["ps%d" % ob].h, 16)
                        if ob >= 2:
                            s.wait_ge(S["qkm"].h, 24)
                        dst = w1b if ob < 2 else wqkvb
                        s.dma_start(out=dst[:, ob % 2, :, :],
                                    in_=dW1bf[ob, :, :, :]).then_inc(S["w1ld%d" % ob].h, 16)
                    P.s(f_w1e)
            for bl in range(12):
                gb = hg * 12 + bl
                i, o = bl // 3, bl % 3
                base = 0 if gb % 2 == 0 else 4
                qk = S["qkm"].inc()

                def f_qmm(t, gb=gb, i=i, o=o, base=base, rke_all_p1=rke_all_p1, lge_all_p1=lge_all_p1,
                          sce_pg=S["sce"].n, scs_pg=S["scs"].n):
                    t.wait_ge(S["wqld%d" % (gb % 2)].h, 16 * (gb // 2 + 1))
                    if gb in (12, 13):
                        t.wait_ge(S["sce"].h, sce_pg)
                        t.wait_ge(S["scs"].h, scs_pg)
                    if gb == 0:
                        t.wait_ge(S["rke"].h, rke_all_p1)
                    if gb == 1:
                        t.wait_ge(S["lge"].h, lge_all_p1)
                    if gb >= 2:
                        t.wait_ge(S["qke"].h, gb - 1)
                    for f in range(NF):
                        for tt in range(4):
                            ins = t.matmul(pb[base + tt][:, :], wqkvb[:, gb % 2, f, :],
                                           A[:, f, tt * 512:(tt + 1) * 512],
                                           start=(f == 0), stop=(f == NF - 1))
                    ins.then_inc(S["qkm"].h, 1)
                P.t(f_qmm)

                qe = S["qke"].inc()
                dest = qarena[o]

                def f_qev(v, gb=gb, i=i, o=o, base=base, dest=dest, qk=qk,
                          rke_all_p1=rke_all_p1, lge_all_p1=lge_all_p1, vch_p1=vch_p1):
                    v.wait_ge(S["qkm"].h, qk)
                    if gb <= 2:
                        v.wait_ge(S["tp"].h, NT)
                        v.wait_ge(S["fxt"].h, 2)
                        v.wait_ge(S["lge"].h, lge_all_p1)
                        v.wait_ge(S["vch"].h, vch_p1)
                        v.wait_ge(S["rke"].h, rke_all_p1)
                    for tt in range(4):
                        ins = v.tensor_scalar(dest[:, i, tt * 512:(tt + 1) * 512], pb[base + tt][:, :],
                                              bqkvc[:, gb:gb + 1], None, ALU.add)
                    ins.then_inc(S["qke"].h, 1)
                P.v(f_qev)

            if stop_qkv and hg == 0:
                break

            sv = S["sce"].inc()

            def f_srst(v, hg=hg, qe_all=S["qke"].n, pvsb=S_bv.get(hg * NCH - 1)):
                v.wait_ge(S["qke"].h, qe_all)
                if pvsb is not None:
                    v.wait_ge(S["scs"].h, pvsb)
                v.memset(S_f[:, :, :], 0.0).then_inc(S["sce"].h, 1)
            P.v(f_srst)

            for c in range(NCH):
                par = c % 2
                gc = hg * NCH + c
                bK = [0, 5][par]
                bV = [1, 6][par]
                bS = [2, 7][par]

                kt = S["scn"].inc()
                vt = S["scn"].inc()

                def f_kt(t, hg=hg, c=c, par=par, bK=bK, bV=bV, qe_all=S["qke"].n,
                         w_ke=SCw["ke"].get(hg * NCH + c - 2), w_ve=SCw["ve"].get(hg * NCH + c - 2)):
                    if c <= 1:
                        t.wait_ge(S["qke"].h, qe_all)
                    if w_ke is not None:
                        t.wait_ge(S["sce"].h, w_ke)
                    if w_ve is not None:
                        t.wait_ge(S["scs"].h, w_ve)
                    for i in range(4):
                        ins = t.matmul(pb[bK][:, i * 128:(i + 1) * 128], Bk[:, i, c * 128:(c + 1) * 128],
                                       ident[:, :], start=True, stop=True)
                    for i in range(4):
                        ins2 = t.matmul(pb[bV][:, i * 128:(i + 1) * 128], Bv[:, i, c * 128:(c + 1) * 128],
                                        ident[:, :], start=True, stop=True)
                    ins.then_inc(S["scn"].h, 1)
                    ins2.then_inc(S["scn"].h, 1)
                P.t(f_kt)

                ke = S["sce"].inc()
                SCw["ke"][gc] = ke

                def f_ke(v, hg=hg, c=c, par=par, bK=bK, kt=kt, w_dm=SCw["dm"].get(hg * NCH + c - 2)):
                    v.wait_ge(S["scn"].h, kt)
                    if w_dm is not None:
                        v.wait_ge(S["scn"].h, w_dm)
                    for i in range(4):
                        ins = v.tensor_scalar(kT_s[:, par, i, :], pb[bK][:, i * 128:(i + 1) * 128],
                                              dkc[:, 4 * hg + i:4 * hg + i + 1], None, ALU.mult)
                    ins.then_inc(S["sce"].h, 1)
                P.v(f_ke)

                ve = S["scs"].inc()
                SCw["ve"][gc] = ve

                def f_ve(s, c=c, par=par, bV=bV, vt=vt, w_dm=SCw["dm"].get(hg * NCH + c - 2),
                         w_om=SCw["om"].get(hg * NCH + c - 2)):
                    s.wait_ge(S["scn"].h, vt)
                    if w_dm is not None:
                        s.wait_ge(S["scn"].h, w_dm)
                    if w_om is not None:
                        s.wait_ge(S["scn"].h, w_om)
                    s.copy(vT_s[:, par, :, :], pb[bV][:, :].rearrange("p (i t) -> p i t", i=4)).then_inc(S["scs"].h, 1)
                P.s(f_ve)

                sc_mm = S["scn"].inc()

                def f_sc(t, hg=hg, c=c, bS=bS, w_me=SCw["me"].get(hg * NCH + c - 2)):
                    if w_me is not None:
                        t.wait_ge(S["sce"].h, w_me)
                    for i in range(4):
                        ins = t.matmul(pb[bS][:, i * 128:(i + 1) * 128], Bk[:, i, c * 128:(c + 1) * 128],
                                       Bq[:, i, c * 128:(c + 1) * 128], start=True, stop=True)
                    ins.then_inc(S["scn"].h, 1)
                P.t(f_sc)

                me = S["sce"].inc()
                SCw["me"][gc] = me

                def f_me(v, hg=hg, c=c, par=par, bS=bS, sc_mm=sc_mm, w_om=SCw["om"].get(hg * NCH + c - 2)):
                    v.wait_ge(S["scn"].h, sc_mm)
                    if w_om is not None:
                        v.wait_ge(S["scn"].h, w_om)
                    v.tensor_tensor(sc_s[:, par, :, :], pb[bS][:, :].rearrange("p (i t) -> p i t", i=4),
                                    dmask[:, 4 * hg:4 * hg + 4, :], ALU.mult).then_inc(S["sce"].h, 1)
                P.v(f_me)

                qs = S["sce"].inc()

                def f_qs(v, hg=hg, c=c, sc_mm=sc_mm):
                    v.wait_ge(S["scn"].h, sc_mm)
                    v.tensor_tensor(Bq[:, :, c * 128:(c + 1) * 128], Bq[:, :, c * 128:(c + 1) * 128],
                                    dqb[:, 4 * hg:4 * hg + 4, :], ALU.mult).then_inc(S["sce"].h, 1)
                P.v(f_qs)

                om = S["scn"].inc()
                SCw["om"][gc] = om

                def f_om(t, hg=hg, c=c, par=par, me=me, qs=qs, ve=ve, gc=gc,
                         w_ae=SCw["ae"].get(hg * NCH + c - 1)):
                    t.wait_ge(S["sce"].h, me)
                    t.wait_ge(S["scs"].h, ve)
                    if w_ae is not None:
                        t.wait_ge(S["scs"].h, w_ae)
                    if c > 0:
                        t.wait_ge(S["sce"].h, qs)
                        t.wait_ge(S["scs"].h, S_bv[gc - 1])
                    for i in range(4):
                        ins = t.matmul(pb[3][:, i * 128:(i + 1) * 128], vT_s[:, par, i, :], sc_s[:, par, i, :],
                                       start=True, stop=(c == 0))
                        if c > 0:
                            ins = t.matmul(pb[3][:, i * 128:(i + 1) * 128], S_b[:, i, :],
                                           Bq[:, i, c * 128:(c + 1) * 128], start=False, stop=True)
                    ins.then_inc(S["scn"].h, 1)
                P.t(f_om)

                dm = S["scn"].inc()
                SCw["dm"][gc] = dm

                def f_dm(t, hg=hg, c=c, par=par, ke=ke, ve=ve, w_su=SCw["su"].get(hg * NCH + c - 1)):
                    t.wait_ge(S["sce"].h, ke)
                    t.wait_ge(S["scs"].h, ve)
                    if w_su is not None:
                        t.wait_ge(S["sce"].h, w_su)
                    for i in range(4):
                        ins = t.matmul(pb[4][:, i * 128:(i + 1) * 128], kT_s[:, par, i, :], vT_s[:, par, i, :],
                                       start=True, stop=True)
                    ins.then_inc(S["scn"].h, 1)
                P.t(f_dm)

                su = S["sce"].inc()
                SCw["su"][gc] = su

                def f_su(v, hg=hg, c=c, dm=dm, pvsb=S_bv.get(hg * NCH + c - 1)):
                    v.wait_ge(S["scn"].h, dm)
                    if pvsb is not None:
                        v.wait_ge(S["scs"].h, pvsb)
                    for i in range(4):
                        ins = v.scalar_tensor_tensor(S_f[:, i, :], S_f[:, i, :],
                                                     lamc[:, 4 * hg + i:4 * hg + i + 1],
                                                     pb[4][:, i * 128:(i + 1) * 128], ALU.mult, ALU.add)
                    ins.then_inc(S["sce"].h, 1)
                P.v(f_su)

                sb = S["scs"].inc()
                S_bv[gc] = sb

                def f_sb(s, c=c, su=su):
                    s.wait_ge(S["sce"].h, su)
                    s.copy(S_b[:, :, :], S_f[:, :, :]).then_inc(S["scs"].h, 1)
                P.s(f_sb)

                ae = S["scs"].inc()
                SCw["ae"][gc] = ae

                def f_ae(s, c=c, par=par, om=om, gc=gc):
                    s.wait_ge(S["scn"].h, om)
                    if gc >= 2:
                        s.wait_ge(S["cc1st%d" % par].h, (gc // 2) * 16)
                    s.copy(pstage[:, par, :, :], pb[3][:, :].rearrange("p (i t) -> p i t", i=4)).then_inc(S["scs"].h, 1)
                P.s(f_ae)

                S["cc1st%d" % par].inc(16)

                def f_cst(s, hg=hg, c=c, par=par, ae=ae):
                    s.wait_ge(S["scs"].h, ae)
                    dst = [dcc1inA, dcc1inB][hg]
                    s.dma_start(
                        out=dst[0:512, c * 128:(c + 1) * 128].rearrange("(i d) t -> d i t", i=4),
                        in_=pstage[:, par, :, :]).then_inc(S["cc1st%d" % par].h, 16)
                P.s(f_cst)

            S["cc"].inc()

            def f_ccg(g, hg=hg, p0=S["cc1st0"].n, p1=S["cc1st1"].n):
                g.wait_ge(S["cc1st0"].h, p0)
                g.wait_ge(S["cc1st1"].h, p1)
                g.collective_compute(
                    "AllGather", ALU.bypass, replica_groups=[list(range(8))],
                    ins=[[dcc1inA, dcc1inB][hg][:, :]], outs=[[dcc1outA, dcc1outB][hg][:, :]],
                ).then_inc(S["cc"].h, 1)
            P.g(f_ccg)

        if "attn" in debug and not stop_qkv:
            cc1_all = S["cc1st0"].n
            qke_all = S["qke"].n

            def f_dbga(g, cc1_all=cc1_all, qke_all=qke_all):
                g.wait_ge(S["cc1st0"].h, cc1_all)
                g.wait_ge(S["cc1st1"].h, S["cc1st1"].n)
                g.wait_ge(S["qke"].h, qke_all)
                g.dma_start(out=dbg["d_cc1"][:, :], in_=dcc1in[:, :]).then_inc(S["fin"].h, 16)
                for i in range(4):
                    g.dma_start(out=dbg["d_qf"][:, i * SEQ:(i + 1) * SEQ], in_=Bq[:, i, :]).then_inc(S["fin"].h, 16)
                    g.dma_start(out=dbg["d_kf"][:, i * SEQ:(i + 1) * SEQ], in_=Bk[:, i, :]).then_inc(S["fin"].h, 16)
                    g.dma_start(out=dbg["d_vf"][:, i * SEQ:(i + 1) * SEQ], in_=Bv[:, i, :]).then_inc(S["fin"].h, 16)
            for _ in range(13):
                S["fin"].inc(16)
            P.g(f_dbga)

    def p3():
        rkm_p1 = S["rkm"].n
        qkm_all = S["qkm"].n
        attn_g = A[:, :, 0:1024]
        h_reg = A[:, :, 1024:1536]
        wout_reg = A[:, :, 1536:2048]

        def hslice(dc):
            return A[:, dc % 16, 1024 + (dc // 16) * 256: 1024 + (dc // 16) * 256 + 256]


        S["cst"].inc(16)

        def f_brow(g, rkm_p1=rkm_p1):
            g.wait_ge(S["rkm"].h, rkm_p1)
            g.dma_start(out=brow[:, :], in_=dbrow[:, :]).then_inc(S["cst"].h, 16)
        P.g(f_brow)

        viewA = dcc1outA.rearrange("r (a c) -> (r a) c", a=2)
        viewB = dcc1outB.rearrange("r (a c) -> (r a) c", a=2)
        order = [g_ for g_ in range(16) if g_ % 8 < 4] + [g_ for g_ in range(16) if g_ % 8 >= 4]
        first_b = True
        for idx, g_ in enumerate(order):
            S["ccrb"].inc(16)
            isB = g_ % 8 >= 4

            def f_rb1(g, g_=g_, idx=idx, isB=isB, qkm_all=qkm_all, first_b=first_b):
                if idx == 0:
                    g.wait_ge(S["cc"].h, 1)
                    g.wait_ge(S["qkm"].h, qkm_all)
                if isB and first_b:
                    g.wait_ge(S["cc"].h, 2)
                g.indirect_dma_start(
                    out=attn_g[:, g_, :], out_offset=None,
                    in_=viewB if isB else viewA, in_offset=IndirectOffsetOnAxis(cc1i_u[:, g_:g_ + 1], 0),
                ).then_inc(S["ccrb"].h, 16)
            if isB and first_b:
                first_b = False
            P.g(f_rb1)
        ccrb1 = S["ccrb"].n

        for w2i in range(64):
            S["w2ld%d" % (w2i % 2)].inc(16)

            def f_w2(y, w2i=w2i, pv=16 * ((32 + w2i) // 4 + 1)):
                y.wait_ge(S["ps%d" % ((32 + w2i) % 4)].h, pv)
                if w2i >= 2:
                    y.wait_ge(S["ym"].h, w2i - 1)
                y.dma_start(out=w2b[:, w2i % 2, :, :],
                            in_=dW2bf[w2i, :, :, :]).then_inc(S["w2ld%d" % (w2i % 2)].h, 16)
            P.y(f_w2)

        cc2v = []
        for half in range(2):
            cc2v.append(S["cc"].inc())

            def f_cc2(g, half=half, stv=S["cc2st%d" % half].n + 32):
                g.wait_ge(S["cc2st%d" % half].h, stv)
                g.collective_compute(
                    "AllGather", ALU.bypass, replica_groups=[list(range(8))],
                    ins=[dcc2inH[half][:, :]], outs=[dcc2outH[half][:, :]],
                ).then_inc(S["cc"].h, 1)
            P.g(f_cc2)

        fxe_all = S["rke"].n
        scee = S["sce"].n
        for ob in range(32):
            S["hm"].inc()

            def f_hmm(t, ob=ob, fxe_all=fxe_all, scee=scee):
                t.wait_ge(S["w1ld%d" % (ob % 4)].h, 16 * (ob // 4 + 1))
                if ob == 0:
                    t.wait_ge(S["sce"].h, scee)
                if ob >= 4:
                    t.wait_ge(S["he"].h, ob - 3)
                src = w1b if ob % 4 < 2 else wqkvb
                for f in range(NF):
                    ins = t.matmul(pb[4 + ob % 4][:, 0:256], src[:, ob % 2, f, :], fx_fm[:, f, :],
                                   start=(f == 0), stop=(f == NF - 1))
                ins.then_inc(S["hm"].h, 1)
            P.t(f_hmm)

            S["he"].inc()

            def f_gel(s, ob=ob, hv=S["hm"].n, qkm_all=qkm_all):
                s.wait_ge(S["hm"].h, hv)
                if ob == 0:
                    s.wait_ge(S["qkm"].h, qkm_all)
                s.activation(hslice(ob), pb[4 + ob % 4][:, 0:256], AF.Gelu,
                             bias=b1c[:, ob:ob + 1], scale=1.0).then_inc(S["he"].h, 1)
            P.s(f_gel)

            if ob + 4 < 32:
                ob2 = ob + 4
                S["w1ld%d" % (ob2 % 4)].inc(16)

                def f_w1(s, ob2=ob2, pv=16 * (ob2 // 4 + 1)):
                    s.wait_ge(S["ps%d" % (ob2 % 4)].h, pv)
                    s.wait_ge(S["hm"].h, ob2 - 3)
                    dst = w1b if ob2 % 4 < 2 else wqkvb
                    s.dma_start(out=dst[:, ob2 % 2, :, :],
                                in_=dW1bf[ob2, :, :, :]).then_inc(S["w1ld%d" % (ob2 % 4)].h, 16)
                P.s(f_w1)

        he_all = S["he"].n
        for mb in range(16):
            for sg in range(4):
                w2i = mb * 4 + sg
                S["ym"].inc()

                def f_ymm(t, mb=mb, sg=sg, w2i=w2i, he_all=he_all):
                    t.wait_ge(S["w2ld%d" % (w2i % 2)].h, 16 * (w2i // 2 + 1))
                    if mb == 0 and sg == 0:
                        t.wait_ge(S["he"].h, he_all)
                    if sg == 0 and mb >= 2:
                        t.wait_ge(S["ye"].h, mb - 1)
                    for d8 in range(8):
                        dc = sg * 8 + d8
                        ins = t.matmul(pb[mb % 2][:, 0:256], w2b[:, w2i % 2, d8, :],
                                       hslice(dc), start=(dc == 0), stop=(dc == 31))
                    ins.then_inc(S["ym"].h, 1)
                P.t(f_ymm)

            S["ye"].inc()

            def f_yfe(v, mb=mb, ymv=S["ym"].n, trv_prev=S["opm"].n):
                v.wait_ge(S["ym"].h, ymv)
                if mb >= 1:
                    v.wait_ge(S["opm"].h, trv_prev)
                ins = v.tensor_scalar(yfmt[:, :], pb[mb % 2][:, 0:256], b2c[:, mb:mb + 1], None, ALU.add)
                ins.then_inc(S["ye"].h, 1)
            P.v(f_yfe)

            S["opm"].inc()

            def f_ytr(t, mb=mb, yev=S["ye"].n):
                t.wait_ge(S["ye"].h, yev)
                if mb >= 2:
                    t.wait_ge(S["asml"].h, 2 * (mb - 1))
                for sh in range(2):
                    ins = t.matmul(pb[2 + mb % 2][:, sh * 128:(sh + 1) * 128],
                                   yfmt[:, sh * 128:(sh + 1) * 128], identf[:, :],
                                   start=True, stop=True)
                ins.then_inc(S["opm"].h, 1)
            P.t(f_ytr)

            S["asml"].inc(2)

            def f_yte(v, mb=mb, trv=S["opm"].n, scn_p1=S["scn"].n if False else NT, lge_p=S["lge"].n):
                v.wait_ge(S["opm"].h, trv)
                if mb == 0:
                    v.wait_ge(S["scn"].h, 2 * NT)
                    v.wait_ge(S["lge"].h, lge_p)
                for sh in range(2):
                    ins = v.tensor_copy([C0, C1][sh][:, mb * 128:(mb + 1) * 128],
                                        pb[2 + mb % 2][:, sh * 128:(sh + 1) * 128])
                ins.then_inc(S["asml"].h, 2)
            P.v(f_yte)

        asml_y = S["asml"].n
        for half in range(2):
            for sh in range(2):
                S["cc2st%d" % half].inc(16)

                def f_c2st(y, half=half, sh=sh, av=asml_y - 32 + 16 * (half + 1)):
                    y.wait_ge(S["asml"].h, av)
                    y.dma_start(out=dcc2inH[half][sh * 128:(sh + 1) * 128, :],
                                in_=[C0, C1][sh][:, half * 1024:(half + 1) * 1024]
                                ).then_inc(S["cc2st%d" % half].h, 16)
                P.y(f_c2st)

        xn_view = dxn.rearrange("t (a c) -> (t a) c", a=4)
        sce_scan = S["sce"].n
        scs_scan = S["scs"].n
        pd_done = shared["pd_done"]
        for mt in range(4):
            for q in range(4):
                pi = 96 + mt * 4 + q
                S["wold"].inc(16)

                def f_wo(s, mt=mt, q=q, pi=pi, qkm_all=qkm_all, op_prev=S["opm"].n,
                         ppv=16 * (pi // 4 + 1)):
                    s.wait_ge(S["ps%d" % (pi % 4)].h, ppv)
                    if q == 0:
                        if mt == 0:
                            s.wait_ge(S["qkm"].h, qkm_all)
                        else:
                            s.wait_ge(S["opm"].h, op_prev)
                    s.dma_start(out=wout_reg[:, q * 4:(q + 1) * 4, :],
                                in_=dwoutbf[mt * 4 + q, :, :, :]).then_inc(S["wold"].h, 16)
                P.s(f_wo)

            for tc in range(8):
                j = mt * 8 + tc
                pj = j % 2
                xj = pj * 2
                xv = S["hst%d" % pj].inc(16)

                def f_xl(y, mt=mt, tc=tc, xj=xj, j=j, pj=pj, stv=S["ast%d" % pj].n,
                         pdd=pd_done):
                    if j < 2:
                        for h_, n_ in pdd:
                            y.wait_ge(h_, n_)
                    if j >= 2:
                        y.wait_ge(S["ast%d" % pj].h, stv)
                    y.dma_start(out=asmb[:, xj, 0:512],
                                in_=dxown[tc * 128:(tc + 1) * 128, mt * 512:(mt + 1) * 512]
                                ).then_inc(S["hst%d" % pj].h, 16)
                P.y(f_xl)

                xnv = S["hld%d" % pj].inc(16)

                def f_xnl(g, mt=mt, tc=tc, xj=xj, j=j, pj=pj, stv=S["ast%d" % pj].n,
                          pdd=pd_done):
                    if j == 0:
                        g.wait_ge(S["xnst0"].h, 16 * 8)
                        g.wait_ge(S["xnst1"].h, 16 * 8)
                    if j < 2:
                        for h_, n_ in pdd:
                            g.wait_ge(h_, n_)
                    if j >= 2:
                        g.wait_ge(S["ast%d" % pj].h, stv)
                    g.indirect_dma_start(
                        out=asmb[:, xj + 1, 0:512], out_offset=None,
                        in_=xn_view, in_offset=IndirectOffsetOnAxis(xni_u[:, tc * 4 + mt:tc * 4 + mt + 1], 0),
                    ).then_inc(S["hld%d" % pj].h, 16)
                P.g(f_xnl)

                opv = S["opm"].inc()

                def f_op(t, mt=mt, tc=tc, pj=pj, cst_b=S["cst"].n, wo_v=S["wold"].n,
                         asml_prev=S["asml"].n, sce_scan=sce_scan, scs_scan=scs_scan):
                    t.wait_ge(S["wold"].h, wo_v)
                    if mt == 0 and tc == 0:
                        t.wait_ge(S["cst"].h, cst_b)
                        t.wait_ge(S["ccrb"].h, ccrb1)
                        t.wait_ge(S["sce"].h, sce_scan)
                        t.wait_ge(S["scs"].h, scs_scan)
                    if mt * 8 + tc >= 2:
                        t.wait_ge(S["asml"].h, asml_prev - 1)
                    bank = 4 + pj
                    for g_ in range(16):
                        ins = t.matmul(pb[bank][:, :], attn_g[:, g_, tc * 128:(tc + 1) * 128],
                                       wout_reg[:, g_, :], start=(g_ == 0), stop=False)
                    t.matmul(pb[bank][:, :], onesr[0:1, 0:128],
                             brow[0:1, mt * 512:(mt + 1) * 512],
                             start=False, stop=True).then_inc(S["opm"].h, 1)
                P.t(f_op)

                av = S["asml"].inc()

                def f_asm(v, mt=mt, tc=tc, xj=xj, pj=pj, xv=xv, xnv=xnv, opv2=S["opm"].n):
                    v.wait_ge(S["hst%d" % pj].h, xv)
                    v.wait_ge(S["hld%d" % pj].h, xnv)
                    v.wait_ge(S["opm"].h, opv2)
                    v.tensor_add(asmb[:, xj, 0:512], asmb[:, xj, 0:512], asmb[:, xj + 1, 0:512])
                    v.tensor_add(asmb[:, xj, 0:512], asmb[:, xj, 0:512],
                                 pb[4 + pj][:, :]).then_inc(S["asml"].h, 1)
                P.v(f_asm)

                S["ast%d" % pj].inc(16)

                def f_st(y, mt=mt, tc=tc, xj=xj, pj=pj, av=av):
                    y.wait_ge(S["asml"].h, av)
                    y.dma_start(out=dy[tc * 128:(tc + 1) * 128, mt * 512:(mt + 1) * 512],
                                in_=asmb[:, xj, 0:512]).then_inc(S["ast%d" % pj].h, 16)
                P.y(f_st)

        opm_y = S["opm"].n - 32
        for sh in range(2):
            for half in range(2):
                S["ccr2"].inc(16)

                def f_rb2(g, sh=sh, half=half, ypv=S["ypost"].n, lge_p=S["lge"].n,
                          ccv=cc2v[half], opm_y=opm_y):
                    if sh == 0:
                        g.wait_ge(S["lge"].h, lge_p)
                        if half == 0:
                            g.wait_ge(S["opm"].h, opm_y)
                    elif half == 0:
                        g.wait_ge(S["ypost"].h, ypv)
                    g.wait_ge(S["cc"].h, ccv)
                    g.indirect_dma_start(
                        out=C2[:, half * 1024:(half + 1) * 1024], out_offset=None,
                        in_=dcc2outH[half][:, :],
                        in_offset=IndirectOffsetOnAxis(cc2i_u[:, 2 + sh:3 + sh], 0),
                    ).then_inc(S["ccr2"].h, 16)
                P.g(f_rb2)

            S["ypost"].inc()

            def f_yadd(v, sh=sh, rbv=S["ccr2"].n):
                v.wait_ge(S["ccr2"].h, rbv)
                Csh = [C0, C1][sh]
                edge(v, v.tensor_add(Csh[:, :], Csh[:, :], C2[:, :]), S["vch"])
                v.tensor_scalar(Csh[:, :], Csh[:, :], rw[:, sh:sh + 1], None, ALU.mult).then_inc(S["ypost"].h, 1)
            P.v(f_yadd)


        ast_all0 = S["ast0"].n
        ast_all1 = S["ast1"].n
        ypost_all = S["ypost"].n
        for sh in range(2):
            S["sca"].inc(16)

            def f_scat(g, sh=sh, ast_all0=ast_all0, ast_all1=ast_all1, ypost_all=ypost_all):
                if sh == 0:
                    g.wait_ge(S["ast0"].h, ast_all0)
                    g.wait_ge(S["ast1"].h, ast_all1)
                    g.wait_ge(S["ypost"].h, ypost_all)
                g.indirect_dma_start(
                    out=dy[:, :], out_offset=IndirectOffsetOnAxis(sci_u[:, sh:sh + 1], 0),
                    in_=[C0, C1][sh][:, :], in_offset=None,
                    compute_op=ALU.add,
                ).then_inc(S["sca"].h, 16)
            P.g(f_scat)

        if "ffn" in debug:
            he_a = S["he"].n
            ysp = S["yst"].n
            rb_a = S["ccrb"].n

            def f_dbgf(g, he_a=he_a, ysp=ysp, rb_a=rb_a):
                g.wait_ge(S["he"].h, he_a)
                g.wait_ge(S["yst"].h, ysp)
                g.wait_ge(S["ccrb"].h, rb_a)
                for dc in range(32):
                    g.dma_start(out=dbg["d_h"][:, dc * KSEL:(dc + 1) * KSEL], in_=hslice(dc)).then_inc(S["fin"].h, 16)
                g.dma_start(out=dbg["d_ysc"][:, :], in_=dysc[:, :]).then_inc(S["fin"].h, 16)
                for g_ in range(16):
                    g.dma_start(out=dbg["d_aog"][:, g_ * 1024:(g_ + 1) * 1024], in_=attn_g[:, g_, :]).then_inc(S["fin"].h, 16)
            for _ in range(32 + 1 + 16):
                S["fin"].inc(16)
            P.g(f_dbgf)

    p0()
    if upto >= 1:
        p1()
    if upto >= 2:
        p2()
    if upto >= 3:
        p3()

    sem_totals = [(sem.h, sem.n) for sem in S.values() if sem.n > 0]

    def f_fin(g, sem_totals=sem_totals):
        for h, n in sem_totals:
            g.wait_ge(h, n)
    P.g(f_fin)

    with nc.Block() as block:
        @block.tensor
        def _(t):
            for fn in P.ops["tensor"]:
                fn(t)

        @block.vector
        def _(v):
            vv = Chained(v, S["vch"])
            for fn in P.ops["vector"]:
                fn(vv)

        @block.scalar
        def _(s):
            ss = Chained(s, S["sch"])
            for fn in P.ops["scalar"]:
                fn(ss)

        @block.sync
        def _(y):
            for fn in P.ops["sync"]:
                fn(y)

        @block.gpsimd
        def _(g):
            for fn in P.ops["gpsimd"]:
                fn(g)

    return nc, dict(S=S)



def _prep_in_maps(L0_attn_x, L0_attn_w_qkv, L0_attn_b_qkv, L0_attn_w_out, L0_attn_b_out,
                  L0_ffn_Wr, L0_ffn_br, L0_ffn_W1, L0_ffn_b1, L0_ffn_W2, L0_ffn_b2):
    f32 = np.float32
    x = np.asarray(L0_attn_x, f32)
    wqkv = np.asarray(L0_attn_w_qkv, f32)
    bqkv = np.asarray(L0_attn_b_qkv, f32)
    wout = np.asarray(L0_attn_w_out, f32)
    bout = np.asarray(L0_attn_b_out, f32)
    Wr = np.asarray(L0_ffn_Wr, f32)
    br = np.asarray(L0_ffn_br, f32)
    W1 = np.asarray(L0_ffn_W1, f32)
    b1 = np.asarray(L0_ffn_b1, f32)
    W2 = np.asarray(L0_ffn_W2, f32)
    b2 = np.asarray(L0_ffn_b2, f32)

    lam = np.exp(-_slopes(16))
    pos = np.arange(128, dtype=np.float64)
    ident = np.eye(128, dtype=f32)
    tri = (np.arange(128)[:, None] < np.arange(128)[None, :]).astype(f32)
    onesm = np.ones((128, 128), f32)
    iob = np.broadcast_to(np.arange(256, dtype=f32)[None, :], (128, 256)).copy()
    iop = np.arange(128, dtype=f32).reshape(128, 1).copy()
    browv = bout.reshape(1, DM).copy()
    onesrv = np.zeros((1, 256), f32)
    onesrv[0, 0:128] = 1.0
    onesrv[0, 128] = br[0]

    wT = wout.T.copy()
    woutT = np.ascontiguousarray(wT.reshape(16, 128, 4, 512).transpose(2, 1, 0, 3))

    in_maps = []
    for c in range(8):
        s, h = c // 2, c % 2
        heads = np.arange(8 * h, 8 * h + 8)
        blocks = np.empty((24, 128, NF, 128), f32)
        bq = np.empty((128, 24), f32)
        for hh in range(8):
            g = heads[hh]
            for o, base in enumerate((0, DM, 2 * DM)):
                Wsub = wqkv[base + g * 128: base + (g + 1) * 128]
                blocks[hh * 3 + o] = Wsub.reshape(128, NF, 128).transpose(2, 1, 0)
                bq[:, hh * 3 + o] = bqkv[base + g * 128: base + (g + 1) * 128]
        W1sub = W1[:, h * DFFH:(h + 1) * DFFH]
        W1t = np.empty((32, 128, NF, 128), f32)
        for b in range(32):
            W1t[b] = W1sub[:, b * 128:(b + 1) * 128].reshape(NF, 128, 128).transpose(1, 0, 2)
        b1h = np.ascontiguousarray(b1[h * DFFH:(h + 1) * DFFH].reshape(32, 128).T)
        W2sub = W2[h * DFFH:(h + 1) * DFFH, :]
        W2cm = np.ascontiguousarray(W2sub.reshape(32, 128, 16, 128).transpose(2, 1, 0, 3))
        b2cv = np.ascontiguousarray(b2.reshape(16, 128).T) if h == 0 else np.zeros((128, 16), f32)
        wrcv = np.ascontiguousarray(Wr[:, 0].reshape(NF, 128).T)

        lam_h = lam[heads]
        lamcv = np.broadcast_to((lam_h ** 128.0).astype(f32)[None, :], (128, HLOC)).copy()
        dmaskT = np.zeros((128, HLOC, 128), f32)
        dqbv = np.zeros((128, HLOC, 128), f32)
        dkv = np.zeros((128, HLOC), f32)
        for hh in range(8):
            lh = lam_h[hh]
            diff = pos[None, :] - pos[:, None]
            m = np.where(diff >= 0, lh ** np.maximum(diff, 0.0), 0.0)
            dmaskT[:, hh, :] = m.astype(f32)
            dqbv[:, hh, :] = np.broadcast_to((lh ** (pos + 1.0)).astype(f32)[None, :], (128, 128))
            dkv[:, hh] = (lh ** (127.0 - pos)).astype(f32)

        pair = s * 2 + (1 - h)
        cc1idx = np.zeros((128, 16), f32)
        for g in range(16):
            owner = s * 2 + (g // 8)
            lh = g % 8
            r = owner * 512 + (lh % 4) * 128 + np.arange(128)
            cc1idx[:, g] = r * 2 + h
        cc2idx = np.zeros((128, 4), f32)
        for sh in range(2):
            cc2idx[:, sh] = c * KSEL + sh * 128 + np.arange(128)
            cc2idx[:, 2 + sh] = pair * KSEL + sh * 128 + np.arange(128)

        in_maps.append(dict(
            x=np.ascontiguousarray(x[s]),
            wqkvT=blocks, bqkv=bq, woutT=woutT,
            W1t=W1t, b1h=b1h, W2cm=W2cm, b2c=b2cv, Wrc=wrcv,
            boutr=browv, onesr=onesrv,
            identc=ident, tric=tri, onesmc=onesm, iobc=iob, iopc=iop,
            onescolc=np.ones((128, 1), f32),
            epscolc=np.full((128, 1), EPS, f32),
            dmaskc=dmaskT, dqbc=dqbv, dkc=dkv, lamcc=lamcv,
            cc1idx=cc1idx, cc2idx=cc2idx,
            hofsc=np.array([[h * 1024.0]], f32),
            xown=np.ascontiguousarray(x[s, h * OWN:(h + 1) * OWN]),
            xnidx=np.ascontiguousarray(
                ((h * OWN + (np.arange(32)[None, :] // 4) * 128 + np.arange(128)[:, None]) * 4
                 + (np.arange(32)[None, :] % 4)).astype(f32)),
        ))
    return in_maps


_BUILD_CACHE = {}


def _get_program(upto=99, debug=()):
    key = (upto, tuple(debug))
    if key not in _BUILD_CACHE:
        _BUILD_CACHE[key] = build_program(upto=upto, debug=debug)
    return _BUILD_CACHE[key]


def kernel(**inputs):
    in_maps = _prep_in_maps(**inputs)
    nc, meta = _get_program()
    res = run_bass_kernel_spmd(nc, in_maps, core_ids=list(range(8)))
    out = np.empty((4, SEQ, DM), np.float32)
    for c in range(8):
        s, h = c // 2, c % 2
        out[s, h * OWN:(h + 1) * OWN] = res.results[c]["y"][0:OWN]
    return out


# version 12 (final)
# speedup vs baseline: 1.0781x; 1.0533x over previous
import math
import numpy as np
import ml_dtypes
import concourse.bass as bass
import concourse.mybir as mybir
from concourse.bass_core_rust import IndirectOffsetOnAxis
from concourse.bass_utils import run_bass_kernel_spmd

F32 = mybir.dt.float32
BF16 = mybir.dt.bfloat16
U32 = mybir.dt.uint32
AF = mybir.ActivationFunctionType
ALU = mybir.AluOpType
AXX = mybir.AxisListType.X

SEQ = 2048
DM = 2048
NF = 16
NT = 16
HLOC = 8
HD = 128
NCH = 16
DFFH = 4096
KSEL = 256
OWN = 1024
TRASH = 256
EPS = 1e-5

ENGINES = ("tensor", "vector", "scalar", "sync", "gpsimd")


def _slopes(n_heads):
    start = 2.0 ** (-(2.0 ** (-(math.log2(n_heads) - 3))))
    return np.array([start ** (i + 1) for i in range(n_heads)], dtype=np.float64)


class Prog:
    def __init__(self):
        self.ops = {e: [] for e in ENGINES}

    def t(self, fn):
        self.ops["tensor"].append(fn)

    def v(self, fn):
        self.ops["vector"].append(fn)

    def s(self, fn):
        self.ops["scalar"].append(fn)

    def y(self, fn):
        self.ops["sync"].append(fn)

    def g(self, fn):
        self.ops["gpsimd"].append(fn)


class Sem:
    def __init__(self, nc, name):
        self.h = nc.alloc_semaphore(name=name)
        self.n = 0

    def inc(self, k=1):
        self.n += k
        return self.n


_CHAIN_OPS = frozenset((
    "tensor_copy", "tensor_tensor", "tensor_scalar", "tensor_single_scalar",
    "tensor_add", "tensor_sub", "tensor_mul", "tensor_max", "tensor_relu",
    "tensor_scalar_add", "tensor_scalar_sub", "tensor_scalar_mul",
    "tensor_reduce", "reduce_sum", "reduce_max", "scalar_tensor_tensor",
    "tensor_tensor_scan", "memset", "memzero", "reciprocal", "select",
    "copy_predicated", "activation", "copy", "add", "mul", "sqrt", "square", "sign",
))


class ChainIns:

    def __init__(self, chain, n):
        self._c = chain
        self._n = n

    def then_inc(self, sem, val=1):
        self._c._e.wait_ge(self._c._sem.h, self._n)
        self._c._e.sem_inc(sem, val)
        return self


class Chained:

    def __init__(self, eng, sem):
        self._e = eng
        self._sem = sem

    def __getattr__(self, name):
        attr = getattr(self._e, name)
        if name in _CHAIN_OPS:
            def wrapped(*a, **k):
                self._e.wait_ge(self._sem.h, self._sem.n)
                ins = attr(*a, **k)
                self._sem.inc()
                ins.then_inc(self._sem.h, 1)
                return ChainIns(self, self._sem.n)
            return wrapped
        return attr


def build_program(upto=99, debug=()):
    nc = bass.Bass("TRN2", target_bir_lowering=False)
    P = Prog()

    def edge(eng, ins, sem):
        k = sem.inc()
        ins.then_inc(sem.h, 1)
        eng.wait_ge(sem.h, k)

    dx = nc.dram_tensor("x", [SEQ, DM], F32, kind="ExternalInput").ap()
    dwqkv = nc.dram_tensor("wqkvT", [24, 128, NF, 128], F32, kind="ExternalInput").ap()
    dbqkv = nc.dram_tensor("bqkv", [128, 24], F32, kind="ExternalInput").ap()
    dwout = nc.dram_tensor("woutT", [4, 128, NF, 512], F32, kind="ExternalInput").ap()
    dW1 = nc.dram_tensor("W1t", [32, 128, NF, 128], F32, kind="ExternalInput").ap()
    db1 = nc.dram_tensor("b1h", [128, 32], F32, kind="ExternalInput").ap()
    dW2 = nc.dram_tensor("W2cm", [16, 128, 32, 128], F32, kind="ExternalInput").ap()
    db2 = nc.dram_tensor("b2c", [128, 16], F32, kind="ExternalInput").ap()
    dWr = nc.dram_tensor("Wrc", [128, NF], F32, kind="ExternalInput").ap()
    dbrow = nc.dram_tensor("boutr", [1, DM], F32, kind="ExternalInput").ap()
    donesr = nc.dram_tensor("onesr", [1, 256], F32, kind="ExternalInput").ap()
    dident = nc.dram_tensor("identc", [128, 128], F32, kind="ExternalInput").ap()
    dtri = nc.dram_tensor("tric", [128, 128], F32, kind="ExternalInput").ap()
    donesm = nc.dram_tensor("onesmc", [128, 128], F32, kind="ExternalInput").ap()
    diob = nc.dram_tensor("iobc", [128, 256], F32, kind="ExternalInput").ap()
    diop = nc.dram_tensor("iopc", [128, 1], F32, kind="ExternalInput").ap()
    donescol = nc.dram_tensor("onescolc", [128, 1], F32, kind="ExternalInput").ap()
    deps = nc.dram_tensor("epscolc", [128, 1], F32, kind="ExternalInput").ap()
    ddmask = nc.dram_tensor("dmaskc", [128, HLOC, 128], F32, kind="ExternalInput").ap()
    ddqb = nc.dram_tensor("dqbc", [128, HLOC, 128], F32, kind="ExternalInput").ap()
    ddk = nc.dram_tensor("dkc", [128, HLOC], F32, kind="ExternalInput").ap()
    dlamc = nc.dram_tensor("lamcc", [128, HLOC], F32, kind="ExternalInput").ap()
    dcc1i = nc.dram_tensor("cc1idx", [128, 16], F32, kind="ExternalInput").ap()
    dcc2i = nc.dram_tensor("cc2idx", [128, 4], F32, kind="ExternalInput").ap()
    dhofs = nc.dram_tensor("hofsc", [1, 1], F32, kind="ExternalInput").ap()
    dxown = nc.dram_tensor("xown", [OWN, DM], F32, kind="ExternalInput").ap()
    dxni = nc.dram_tensor("xnidx", [128, 32], F32, kind="ExternalInput").ap()

    dy = nc.dram_tensor("y", [OWN + TRASH, DM], F32, kind="ExternalOutput").ap()

    dxn = nc.dram_tensor("xn_dram", [SEQ, DM], F32).ap()
    dlrow = nc.dram_tensor("lrow_dram", [1, SEQ], F32).ap()
    dhbuf = nc.dram_tensor("h_dram", [32, 128, KSEL], BF16).ap()
    dysc = nc.dram_tensor("ysc_dram", [KSEL, DM], F32).ap()
    dW1bf = nc.dram_tensor("w1bf", [32, 128, NF, 128], BF16).ap()
    dW2bf = nc.dram_tensor("w2bf", [64, 128, 8, 128], BF16).ap()
    dwoutbf = nc.dram_tensor("woutbf", [16, 128, 4, 512], BF16).ap()
    dcc1inA = nc.dram_tensor("cc1_inA", [512, SEQ], BF16).ap()
    dcc1inB = nc.dram_tensor("cc1_inB", [512, SEQ], BF16).ap()
    dcc1outA = nc.dram_tensor("cc1_outA", [4096, SEQ], BF16).ap()
    dcc1outB = nc.dram_tensor("cc1_outB", [4096, SEQ], BF16).ap()
    dcc2inH = [nc.dram_tensor("cc2_in%d" % i, [KSEL, DM // 2], F32).ap() for i in range(2)]
    dcc2outH = [nc.dram_tensor("cc2_out%d" % i, [8 * KSEL, DM // 2], F32).ap() for i in range(2)]

    dbg = {}
    if "xnfm" in debug:
        dbg["d_xnfm"] = nc.dram_tensor("d_xnfm", [128, NF * SEQ], F32, kind="ExternalOutput").ap()
    if "router" in debug:
        dbg["d_ltm"] = nc.dram_tensor("d_ltm", [128, 16], F32, kind="ExternalOutput").ap()
        dbg["d_rank"] = nc.dram_tensor("d_rank", [128, 16], F32, kind="ExternalOutput").ap()
        dbg["d_slott"] = nc.dram_tensor("d_slott", [128, 2], F32, kind="ExternalOutput").ap()
        dbg["d_rw"] = nc.dram_tensor("d_rw", [128, 2], F32, kind="ExternalOutput").ap()
        dbg["d_sci"] = nc.dram_tensor("d_sci", [128, 2], F32, kind="ExternalOutput").ap()
        dbg["d_fxfm"] = nc.dram_tensor("d_fxfm", [128, NF * KSEL], F32, kind="ExternalOutput").ap()
    if "ffn" in debug:
        dbg["d_h"] = nc.dram_tensor("d_h", [128, 32 * KSEL], BF16, kind="ExternalOutput").ap()
        dbg["d_ysc"] = nc.dram_tensor("d_ysc", [KSEL, DM], F32, kind="ExternalOutput").ap()
        dbg["d_aog"] = nc.dram_tensor("d_aog", [128, 16 * 1024], BF16, kind="ExternalOutput").ap()
    if "attn" in debug:
        dbg["d_attn"] = nc.dram_tensor("d_attn", [128, HLOC * 1024], BF16, kind="ExternalOutput").ap()
        dbg["d_cc1"] = nc.dram_tensor("d_cc1", [1024, SEQ], BF16, kind="ExternalOutput").ap()
        dbg["d_qf"] = nc.dram_tensor("d_qf", [128, 4 * SEQ], BF16, kind="ExternalOutput").ap()
        dbg["d_kf"] = nc.dram_tensor("d_kf", [128, 4 * SEQ], BF16, kind="ExternalOutput").ap()
        dbg["d_vf"] = nc.dram_tensor("d_vf", [128, 4 * SEQ], BF16, kind="ExternalOutput").ap()

    S = {}
    for name in (
        "cst", "xld", "vmean", "ssq", "vxn", "xnst", "tp", "etp",
        "lgm", "lge", "rkm", "rke", "fxg", "fxt",
        "wqld", "qkm", "qke", "w1ld", "hm", "he", "hst",
        "scn", "sce", "scs", "cc1st", "cc",
        "w2ld", "hld", "ym", "ye", "yst", "cc2st", "ccrb", "ypost",
        "wold", "opm", "asml", "ast", "sca", "fin", "vch", "sch",
        "xld0", "xld1", "xnst0", "xnst1", "wqld0", "wqld1", "w1ld0", "w1ld1", "ccr2",
        "w2ld0", "w2ld1", "cc1st0", "cc1st1", "hst0", "hst1", "hld0", "hld1",
        "ast0", "ast1", "fxg0", "fxg1", "lrb", "yrl",
        "pd0", "pd1", "pd2", "pd3", "ps0", "ps1", "ps2", "ps3",
        "cc2st0", "cc2st1", "w1ld2", "w1ld3", "w2ld2", "w2ld3",
    ):
        S[name] = Sem(nc, "s_" + name)

    ctxs = []

    def alloc(name, shape, dtype):
        cm = nc.sbuf_tensor(name, shape, dtype)
        t = cm.__enter__()
        ctxs.append(cm)
        return t

    A = alloc("arenaA", [128, NF, SEQ], BF16)
    Bq = alloc("arenaBq", [128, 4, SEQ], BF16)
    Bk = alloc("arenaBk", [128, 4, SEQ], BF16)
    Bv = alloc("arenaBv", [128, 4, SEQ], BF16)
    C0 = alloc("arenaC0", [128, DM], F32)
    C1 = alloc("arenaC1", [128, DM], F32)
    C2 = alloc("arenaC2", [128, DM], F32)

    asmb = alloc("asmb", [128, 4, 1024], F32)
    fx_fm = alloc("fx_fm", [128, NF, KSEL], BF16)
    pstage = alloc("pstage", [128, 2, 4, 128], BF16)

    wqkvb = alloc("wqkvb", [128, 2, NF, 128], BF16)
    w1b = alloc("w1b", [128, 2, NF, 128], BF16)
    w2b = alloc("w2b", [128, 2, 8, 128], BF16)

    brow = alloc("brows", [1, DM], F32)
    onesr = alloc("onesrs", [1, 256], F32)
    ident = alloc("idents", [128, 128], BF16)
    identf = alloc("identfs", [128, 128], F32)
    tri = alloc("tris", [128, 128], BF16)
    onesm = alloc("onesms", [128, 128], BF16)
    iob = alloc("iobs", [128, 256], BF16)
    iop = alloc("iops", [128, 1], F32)
    onescol = alloc("onescols", [128, 1], F32)
    epscol = alloc("epscols", [128, 1], F32)
    dmask = alloc("dmasks", [128, HLOC, 128], BF16)
    dqb = alloc("dqbs", [128, HLOC, 128], BF16)
    dkc = alloc("dks", [128, HLOC], F32)
    lamc = alloc("lamcs", [128, HLOC], F32)
    bqkvc = alloc("bqkvs", [128, 24], F32)
    b1c = alloc("b1s", [128, 32], F32)
    b2c = alloc("b2s", [128, 16], F32)
    wrc = alloc("wrs", [128, NF], BF16)
    cc1i_u = alloc("cc1i_u", [128, 16], U32)
    cc2i_u = alloc("cc2i_u", [128, 4], U32)
    xni_u = alloc("xni_u", [128, 32], U32)
    hofs = alloc("hofss", [1, 1], F32)

    ltm = alloc("ltm", [128, 16], F32)
    lrow = brow
    lrow16 = alloc("lrow16", [16, 128], F32)
    rank = alloc("ranks", [128, 16], F32)
    flag = alloc("flags", [128, 16], F32)
    flagb = alloc("flagbs", [128, 16], BF16)
    etm = alloc("etms", [128, 16], F32)
    posp = alloc("posps", [128, 16], F32)
    mcol = alloc("mcols", [128, 1], F32)
    negmx = alloc("negms", [128, 1], F32)
    zcol = alloc("zcols", [128, 1], F32)
    izval = alloc("izvals", [1, 1], F32)
    izb = alloc("izbs", [128, 1], F32)
    slott = alloc("slotts", [128, 2], F32)
    slote = alloc("slotes", [128, 2], F32)
    rw = alloc("rws", [128, 2], F32)
    tokc = alloc("tokcs", [128, 1], F32)
    fxi_u = alloc("fxi_u", [128, 2], U32)
    sci_f = alloc("sci_f", [128, 2], F32)
    sci_u = alloc("sci_u", [128, 2], U32)

    S_f = alloc("S_f", [128, 4, 128], F32)
    S_b = alloc("S_b", [128, 4, 128], BF16)
    kT_s = alloc("kT_s", [128, 2, 4, 128], BF16)
    vT_s = alloc("vT_s", [128, 2, 4, 128], BF16)
    sc_s = alloc("sc_s", [128, 2, 4, 128], BF16)
    lnm = alloc("lnm", [128, 2], F32)
    lnr = alloc("lnr", [128, 2], F32)

    xbuf = [C0, C1]
    lbuf = C2
    yfmt = C2[:, 0:256]
    junkb = Bv[:, 0, :]
    xnbf = [Bq[:, 0, :], Bq[:, 1, :]]
    fxtm = [Bk[:, 0, :], Bk[:, 1, :]]

    pb = []
    for i in range(8):
        cm = nc.psum_tensor(f"pb{i}", [128, 512], F32)
        pb.append(cm.__enter__())
        ctxs.append(cm)
    pbh = [pb[7][:, 0:32], pb[7][:, 32:64]]

    def p0():
        pairs = [
            (onesr, donesr), (ident, dident), (identf, dident),
            (tri, dtri), (onesm, donesm), (iob, diob), (iop, diop), (onescol, donescol), (epscol, deps), (lamc, dlamc),
            (dmask, ddmask), (dqb, ddqb), (dkc, ddk), (bqkvc, dbqkv),
            (b1c, db1), (b2c, db2), (wrc, dWr), (hofs, dhofs),
        ]

        def fn(g, pairs=pairs):
            for dst, src in pairs:
                g.dma_start(out=dst[...], in_=src[...]).then_inc(S["cst"].h, 16)
            g.dma_start(out=cc1i_u[:, :], in_=dcc1i[:, :]).then_inc(S["cst"].h, 16)
            g.dma_start(out=cc2i_u[:, :], in_=dcc2i[:, :]).then_inc(S["cst"].h, 16)
            g.dma_start(out=xni_u[:, :], in_=dxni[:, :]).then_inc(S["cst"].h, 16)
        for _ in range(len(pairs) + 3):
            S["cst"].inc(16)
        P.g(fn)

    def p1():
        stop_ln = "stop_ln" in debug
        stop_router = "stop_router" in debug
        stop_slots = "stop_slots" in debug
        cst_all = S["cst"].n
        for tc in range(NT):
            par = tc % 2
            xw = S["xld%d" % par].inc(16)

            def f_x(y, tc=tc, par=par):
                if tc >= 2:
                    y.wait_ge(S["xnst%d" % par].h, (tc // 2) * 16)
                y.dma_start(out=xbuf[par][:, :], in_=dx[tc * 128:(tc + 1) * 128, :]).then_inc(S["xld%d" % par].h, 16)
            P.y(f_x)

            vm = S["vmean"].inc()

            def f_mean(v, par=par, xw=xw, cst_all=cst_all):
                v.wait_ge(S["cst"].h, cst_all)
                v.wait_ge(S["xld%d" % par].h, xw)
                edge(v, v.reduce_sum(lnm[:, par:par + 1], xbuf[par][:, :], axis=AXX), S["vch"])
                v.tensor_scalar(lnm[:, par:par + 1], lnm[:, par:par + 1], -1.0 / DM, None,
                                ALU.mult).then_inc(S["vmean"].h, 1)
            P.v(f_mean)

            sq = S["ssq"].inc()

            def f_sq(s, par=par, vm=vm):
                s.wait_ge(S["vmean"].h, vm)
                edge(s, s.activation(junkb[:, :], xbuf[par][:, :], AF.Square, bias=lnm[:, par:par + 1],
                                     scale=1.0, accum_out=lnr[:, par:par + 1]), S["sch"])
                edge(s, s.activation(lnr[:, par:par + 1], lnr[:, par:par + 1], AF.Ln,
                                     bias=epscol[:, :], scale=1.0 / DM), S["sch"])
                s.activation(lnr[:, par:par + 1], lnr[:, par:par + 1], AF.Exp,
                             scale=-0.5).then_inc(S["ssq"].h, 1)
            P.s(f_sq)

            vx = S["vxn"].inc()

            def f_norm(v, par=par, sq=sq, tc=tc):
                v.wait_ge(S["ssq"].h, sq)
                edge(v, v.tensor_scalar(xbuf[par][:, :], xbuf[par][:, :], lnm[:, par:par + 1],
                                        lnr[:, par:par + 1], ALU.add, ALU.mult), S["vch"])
                if tc >= 2:
                    v.wait_ge(S["tp"].h, tc - 1)
                v.tensor_copy(xnbf[par][:, :], xbuf[par][:, :]).then_inc(S["vxn"].h, 1)
            P.v(f_norm)

            S["xnst%d" % par].inc(16)

            def f_store(s, tc=tc, par=par, vx=vx):
                s.wait_ge(S["vxn"].h, vx)
                s.dma_start(out=dxn[tc * 128:(tc + 1) * 128, :], in_=xbuf[par][:, :]).then_inc(S["xnst%d" % par].h, 16)
            P.s(f_store)

            tpv = S["tp"].inc()

            def f_tp(t, tc=tc, par=par, vx=vx, tpv=tpv, cst_all=cst_all):
                if tc == 0:
                    t.wait_ge(S["cst"].h, cst_all)
                t.wait_ge(S["vxn"].h, vx)
                if tpv > 1:
                    t.wait_ge(S["etp"].h, tpv - 1)
                for fc in range(NF):
                    bank = fc // 4
                    col = fc % 4
                    ins = t.matmul(pb[bank][:, col * 128:(col + 1) * 128], xnbf[par][:, fc * 128:(fc + 1) * 128],
                                   ident[:, :], start=True, stop=True)
                ins.then_inc(S["tp"].h, 1)
            P.t(f_tp)

            ev = S["etp"].inc()

            def f_etp(s, tc=tc, tpv=tpv):
                s.wait_ge(S["tp"].h, tpv)
                for bank in range(4):
                    ins = s.copy(A[:, bank * 4:(bank + 1) * 4, tc * 128:(tc + 1) * 128],
                                 pb[bank][:, :].rearrange("p (f c) -> p f c", f=4))
                    if bank == 3:
                        ins.then_inc(S["etp"].h, 1)
            P.s(f_etp)

            S["lgm"].inc()

            def f_lg(t, tc=tc, ev=ev, cst_all=cst_all):
                if tc == 0:
                    t.matmul(pbh[0][:, 19:20], onesr[0:1, 0:128], onesr[0:1, 128:129],
                             start=True, stop=True)
                t.wait_ge(S["etp"].h, ev)
                for fc in range(NF):
                    ins = t.matmul(pbh[0][:, tc:tc + 1], A[:, fc, tc * 128:(tc + 1) * 128], wrc[:, fc:fc + 1],
                                   start=(fc == 0), stop=(fc == NF - 1))
                ins.then_inc(S["lgm"].h, 1)
            P.t(f_lg)

        if stop_ln:
            return
        lgall = S["lgm"].n
        lge1 = S["lge"].inc()

        def f_lev(v, lgall=lgall):
            v.wait_ge(S["lgm"].h, lgall)
            v.tensor_scalar(ltm[:, :], pbh[0][:, 0:16], pbh[0][:, 19:20], None, ALU.add).then_inc(S["lge"].h, 1)
        P.v(f_lev)

        rk1 = S["rkm"].inc()

        def f_lt(t, lge1=lge1):
            t.wait_ge(S["lge"].h, lge1)
            t.matmul(pb[4][0:16, 0:128], ltm[:, :], identf[:, :], start=True,
                     stop=True).then_inc(S["rkm"].h, 1)
        P.t(f_lt)

        lge2 = S["lge"].inc()

        def f_lte(v, rk1=rk1):
            v.wait_ge(S["rkm"].h, rk1)
            v.tensor_copy(lrow16[:, :], pb[4][0:16, 0:128]).then_inc(S["lge"].h, 1)
        P.v(f_lte)

        st1 = S["lrb"].inc(16)

        def f_lrd(y, lge2=lge2):
            y.wait_ge(S["lge"].h, lge2)
            y.dma_start(out=dlrow.rearrange("o (a b) -> (o a) b", a=16), in_=lrow16[:, :]).then_inc(S["lrb"].h, 16)
        P.y(f_lrd)
        st2 = S["lrb"].inc(16)

        def f_lrl(y, st1=st1):
            y.wait_ge(S["lrb"].h, st1)
            y.dma_start(out=lrow[:, :], in_=dlrow[:, :]).then_inc(S["lrb"].h, 16)
        P.y(f_lrl)

        rk2 = S["rkm"].inc()

        def f_lb(t, st2=st2):
            t.wait_ge(S["lrb"].h, st2)
            for tc in range(NT):
                bank = tc // 4
                col = tc % 4
                ins = t.matmul(pb[bank][:, col * 128:(col + 1) * 128], onesr[0:1, 0:128],
                               lrow[0:1, tc * 128:(tc + 1) * 128], start=True, stop=True)
            ins.then_inc(S["rkm"].h, 1)
        P.t(f_lb)

        lge3 = S["lge"].inc()

        def f_lbe(v, rk2=rk2):
            v.wait_ge(S["rkm"].h, rk2)
            for bank in range(4):
                ins = v.tensor_copy(lbuf[:, bank * 512:(bank + 1) * 512], pb[bank][:, :])
                edge(v, ins, S["vch"])
            edge(v, v.reduce_max(mcol[:, :], lbuf[:, :], axis=AXX), S["vch"])
            v.tensor_scalar(negmx[:, :], mcol[:, :], -1.0, None, ALU.mult).then_inc(S["lge"].h, 1)
        P.v(f_lbe)

        lge4 = S["lge"].inc()

        def f_rank(v, lge3=lge3):
            v.wait_ge(S["lge"].h, lge3)
            for tc in range(NT):
                ins = v.tensor_scalar(junkb[:, :], lbuf[:, :], ltm[:, tc:tc + 1], 0.0, ALU.is_gt,
                                      ALU.add, accum_out=rank[:, tc:tc + 1])
                edge(v, ins, S["vch"])
            edge(v, v.tensor_scalar(flag[:, :], rank[:, :], 255.5, None, ALU.is_lt), S["vch"])
            v.tensor_copy(flagb[:, :], flag[:, :]).then_inc(S["lge"].h, 1)
        P.v(f_rank)

        se1 = S["ssq"].inc()

        def f_e(s, lge4=lge4):
            s.wait_ge(S["lge"].h, lge4)
            s.activation(etm[:, :], ltm[:, :], AF.Exp, bias=negmx[:, :], scale=1.0).then_inc(S["ssq"].h, 1)
        P.s(f_e)

        rk3 = S["rkm"].inc()

        def f_pf(t, lge4=lge4):
            t.wait_ge(S["lge"].h, lge4)
            for tc in range(NT):
                for tp_ in range(tc):
                    t.matmul(pbh[1][:, tc:tc + 1], onesm[:, :], flagb[:, tp_:tp_ + 1],
                             start=(tp_ == 0), stop=False)
                ins = t.matmul(pbh[1][:, tc:tc + 1], tri[:, :], flagb[:, tc:tc + 1],
                               start=(tc == 0), stop=True)
            ins.then_inc(S["rkm"].h, 1)
        P.t(f_pf)

        lge5 = S["lge"].inc()

        def f_pp(v, rk3=rk3, se1=se1):
            v.wait_ge(S["rkm"].h, rk3)
            v.wait_ge(S["ssq"].h, se1)
            edge(v, v.scalar_tensor_tensor(posp[:, :], flag[:, :], -999.0, pbh[1][:, 0:16],
                                           ALU.mult, ALU.add), S["vch"])
            v.tensor_scalar(posp[:, :], posp[:, :], 999.0, None, ALU.add)
            edge(v, v.tensor_mul(etm[:, :], etm[:, :], flag[:, :]), S["vch"])
            v.reduce_sum(zcol[:, :], etm[:, :], axis=AXX).then_inc(S["lge"].h, 1)
        P.v(f_pp)

        rk4 = S["rkm"].inc()

        def f_z(t, lge5=lge5):
            t.wait_ge(S["lge"].h, lge5)
            t.matmul(pbh[0][0:1, 16:17], zcol[:, :], onescol[:, :], start=True, stop=True)
            t.matmul(pbh[0][:, 18:19], onesr[0:1, 0:128], hofs[:, :], start=True,
                     stop=True).then_inc(S["rkm"].h, 1)
        P.t(f_z)

        lge6 = S["lge"].inc()

        def f_iz(v, rk4=rk4):
            v.wait_ge(S["rkm"].h, rk4)
            v.reciprocal(izval[:, :], pbh[0][0:1, 16:17]).then_inc(S["lge"].h, 1)
        P.v(f_iz)

        rk5 = S["rkm"].inc()

        def f_izb(t, lge6=lge6):
            t.wait_ge(S["lge"].h, lge6)
            t.matmul(pbh[0][:, 17:18], onesr[0:1, 0:128], izval[:, :], start=True,
                     stop=True).then_inc(S["rkm"].h, 1)
        P.t(f_izb)

        if stop_router:
            return
        for sh in range(2):
            for tc in range(NT):
                par = tc % 2
                pv = S["rke"].inc()

                def f_ptile(v, tc=tc, par=par, sh=sh, lge5=lge5, pv=pv):
                    if tc == 0 and sh == 0:
                        v.wait_ge(S["lge"].h, lge5)
                        v.wait_ge(S["tp"].h, NT)
                        v.wait_ge(S["xnst0"].h, 16 * 8)
                        v.wait_ge(S["xnst1"].h, 16 * 8)
                    if sh * NT + tc >= 2:
                        v.wait_ge(S["scn"].h, sh * NT + tc - 1)
                    v.tensor_scalar(tokc[:, :], iop[:, :], float(128 * tc), None, ALU.add)
                    v.tensor_scalar(xbuf[par][:, 0:128], iob[:, sh * 128:(sh + 1) * 128],
                                    posp[:, tc:tc + 1], None, ALU.is_equal)
                    v.tensor_copy(xbuf[par][:, 256:257], tokc[:, :])
                    v.tensor_copy(xbuf[par][:, 257:258], etm[:, tc:tc + 1]).then_inc(S["rke"].h, 1)
                P.v(f_ptile)

                S["scn"].inc()

                def f_pmm(t, tc=tc, par=par, sh=sh, pv=pv):
                    t.wait_ge(S["rke"].h, pv)
                    ins = t.matmul(pbh[1][:, 18 + sh * 2:20 + sh * 2],
                                   xbuf[par][:, 0:128], xbuf[par][:, 256:258],
                                   start=(tc == 0), stop=(tc == NT - 1))
                    ins.then_inc(S["scn"].h, 1)
                P.t(f_pmm)

        lge7 = S["lge"].inc()

        def f_slote(v, rk5=rk5):
            v.wait_ge(S["scn"].h, 2 * NT)
            edge(v, v.tensor_copy(izb[:, :], pbh[0][:, 17:18]), S["vch"])
            edge(v, v.tensor_copy(mcol[:, :], pbh[0][:, 18:19]), S["vch"])
            for sh in range(2):
                v.tensor_copy(slott[:, sh:sh + 1], pbh[1][:, 18 + sh * 2:19 + sh * 2])
                ins = v.tensor_copy(slote[:, sh:sh + 1], pbh[1][:, 19 + sh * 2:20 + sh * 2])
            edge(v, ins, S["vch"])
            v.tensor_scalar(rw[:, :], slote[:, :], izb[:, 0:1], None, ALU.mult)
            v.tensor_copy(fxi_u[:, :], slott[:, :])
            edge(v, v.tensor_scalar(sci_f[:, :], slott[:, :], mcol[:, 0:1], None, ALU.subtract), S["vch"])
            edge(v, v.tensor_scalar(lnm[:, 0:2], sci_f[:, :], -0.5, None, ALU.is_gt), S["vch"])
            edge(v, v.tensor_scalar(lnr[:, 0:2], sci_f[:, :], 1023.5, None, ALU.is_lt), S["vch"])
            edge(v, v.tensor_mul(lnm[:, 0:2], lnm[:, 0:2], lnr[:, 0:2]), S["vch"])
            edge(v, v.tensor_scalar(lnr[:, 0:1], iop[:, :], 1024.0, None, ALU.add), S["vch"])
            edge(v, v.tensor_scalar(lnr[:, 1:2], iop[:, :], 1024.0 + 128.0, None, ALU.add), S["vch"])
            edge(v, v.tensor_sub(sci_f[:, :], sci_f[:, :], lnr[:, 0:2]), S["vch"])
            edge(v, v.tensor_mul(sci_f[:, :], sci_f[:, :], lnm[:, 0:2]), S["vch"])
            edge(v, v.tensor_add(sci_f[:, :], sci_f[:, :], lnr[:, 0:2]), S["vch"])
            v.tensor_copy(sci_u[:, :], sci_f[:, :]).then_inc(S["lge"].h, 1)
        P.v(f_slote)

        if stop_slots:
            return
        rke_fx0 = None
        for sh in range(2):
            fg = S["fxg%d" % sh].inc(16)

            def f_fxg(g, sh=sh, lge7=lge7):
                g.wait_ge(S["lge"].h, lge7)
                g.wait_ge(S["xnst0"].h, 16 * 8)
                g.wait_ge(S["xnst1"].h, 16 * 8)
                g.indirect_dma_start(
                    out=fxtm[sh][:, :], out_offset=None,
                    in_=dxn[:, :], in_offset=IndirectOffsetOnAxis(fxi_u[:, sh:sh + 1], 0),
                ).then_inc(S["fxg%d" % sh].h, 16)
            P.g(f_fxg)

            ft = S["fxt"].inc()

            def f_fxt(t, sh=sh, fg=fg, ft=ft, rke_fx0=rke_fx0):
                t.wait_ge(S["fxg%d" % sh].h, fg)
                if ft > 1:
                    t.wait_ge(S["rke"].h, rke_fx0)
                for fc in range(NF):
                    bank = 3 + fc // 4
                    col = fc % 4
                    ins = t.matmul(pb[bank][:, col * 128:(col + 1) * 128], fxtm[sh][:, fc * 128:(fc + 1) * 128],
                                   ident[:, :], start=True, stop=True)
                ins.then_inc(S["fxt"].h, 1)
            P.t(f_fxt)

            S["rke"].inc(4)

            def f_fxe(s, sh=sh, ft=ft):
                s.wait_ge(S["fxt"].h, ft)
                for bank in range(4):
                    ins = s.copy(fx_fm[:, bank * 4:(bank + 1) * 4, sh * 128:(sh + 1) * 128],
                                 pb[3 + bank][:, :].rearrange("p (f c) -> p f c", f=4))
                    if bank == 3:
                        ins.then_inc(S["rke"].h, 4)
            P.s(f_fxe)
            rke_fx0 = S["rke"].n

        if "xnfm" in debug and not stop_ln:
            etp_all = S["etp"].n

            def f_dbg1(g, etp_all=etp_all):
                g.wait_ge(S["etp"].h, etp_all)
                for fc in range(NF):
                    g.dma_start(out=dbg["d_xnfm"][:, fc * SEQ:(fc + 1) * SEQ],
                                in_=A[:, fc, :]).then_inc(S["fin"].h, 16)
            for fc in range(NF):
                S["fin"].inc(16)
            P.g(f_dbg1)
        if "router" in debug and not (stop_ln or stop_router or stop_slots):
            lge_all = S["lge"].n
            rke_all = S["rke"].n

            def f_dbg2(g, lge_all=lge_all, rke_all=rke_all):
                g.wait_ge(S["lge"].h, lge_all)
                g.wait_ge(S["rke"].h, rke_all)
                g.dma_start(out=dbg["d_ltm"][:, :], in_=ltm[:, :]).then_inc(S["fin"].h, 16)
                g.dma_start(out=dbg["d_rank"][:, :], in_=rank[:, :]).then_inc(S["fin"].h, 16)
                g.dma_start(out=dbg["d_slott"][:, :], in_=slott[:, :]).then_inc(S["fin"].h, 16)
                g.dma_start(out=dbg["d_rw"][:, :], in_=rw[:, :]).then_inc(S["fin"].h, 16)
                g.dma_start(out=dbg["d_sci"][:, :], in_=sci_f[:, :]).then_inc(S["fin"].h, 16)
                for fc in range(NF):
                    g.dma_start(out=dbg["d_fxfm"][:, fc * KSEL:(fc + 1) * KSEL],
                                in_=fx_fm[:, fc, :]).then_inc(S["fin"].h, 16)
            for _ in range(5 + NF):
                S["fin"].inc(16)
            P.g(f_dbg2)


    shared = {}

    def p2():
        stop_qkv = "stop_qkv" in debug
        S_bv = {}
        SCw = {k: {} for k in ("ke", "ve", "me", "om", "dm", "su", "ae")}
        qarena = [Bq, Bk, Bv]
        rke_all_p1 = S["rke"].n
        lge_all_p1 = S["lge"].n
        vch_p1 = S["vch"].n
        pd_pieces = []
        for ob in range(32):
            pd_pieces.append((dW1[ob, :, :, :], dW1bf[ob, :, :, :], 2048))
        for i in range(64):
            pd_pieces.append((dW2[i // 4, :, (i % 4) * 8:(i % 4 + 1) * 8, :], dW2bf[i, :, :, :], 1024))
        for i in range(16):
            pd_pieces.append((dwout[i // 4, :, (i % 4) * 4:(i % 4 + 1) * 4, :], dwoutbf[i, :, :, :], 2048))
        bnc = [asmb[:, j, :].bitcast(BF16) for j in range(4)]

        def emit_pd(i):
            sl = i % 4
            S["pd%d" % sl].inc(16)

            def f_pdl(g, i=i, sl=sl, psv=16 * (i // 4)):
                if 32 <= i < 96:
                    g.wait_ge(S["qkm"].h, 12)
                elif i >= 96:
                    g.wait_ge(S["qkm"].h, 18)
                if i >= 4:
                    g.wait_ge(S["ps%d" % sl].h, psv)
                src_ap, dst_ap, n = pd_pieces[i]
                g.dma_start(out=bnc[sl][:, 0:n], in_=src_ap).then_inc(S["pd%d" % sl].h, 16)
            P.g(f_pdl)
            S["ps%d" % sl].inc(16)

            def f_pds(y, i=i, sl=sl, pdv=16 * (i // 4 + 1)):
                y.wait_ge(S["pd%d" % sl].h, pdv)
                src_ap, dst_ap, n = pd_pieces[i]
                y.dma_start(out=dst_ap, in_=bnc[sl][:, 0:n]).then_inc(S["ps%d" % sl].h, 16)
            P.y(f_pds)

        pd_emitted = 0
        for gb in range(24):
            wq = S["wqld%d" % (gb % 2)].inc(16)

            def f_wdma(g, gb=gb):
                if gb >= 2:
                    g.wait_ge(S["qkm"].h, gb - 1)
                g.dma_start(out=wqkvb[:, gb % 2, :, :], in_=dwqkv[gb, :, :, :]).then_inc(S["wqld%d" % (gb % 2)].h, 16)
            P.g(f_wdma)
            if gb < 12:
                target = min(32, (gb + 1) * 3)
            else:
                target = 32 + (gb - 11) * 4
            while pd_emitted < target:
                emit_pd(pd_emitted)
                pd_emitted += 1
        while pd_emitted < len(pd_pieces):
            emit_pd(pd_emitted)
            pd_emitted += 1
        shared["pd_done"] = [(S["ps%d" % j].h, S["ps%d" % j].n) for j in range(4)]

        for hg in range(2):
            if hg == 1:
                for ob in range(4):
                    S["w1ld%d" % ob].inc(16)

                    def f_w1e(s, ob=ob):
                        s.wait_ge(S["ps%d" % ob].h, 16)
                        if ob >= 2:
                            s.wait_ge(S["qkm"].h, 24)
                        dst = w1b if ob < 2 else wqkvb
                        s.dma_start(out=dst[:, ob % 2, :, :],
                                    in_=dW1bf[ob, :, :, :]).then_inc(S["w1ld%d" % ob].h, 16)
                    P.s(f_w1e)
            for bl in range(12):
                gb = hg * 12 + bl
                i, o = bl // 3, bl % 3
                base = 0 if gb % 2 == 0 else 4
                qk = S["qkm"].inc()

                def f_qmm(t, gb=gb, i=i, o=o, base=base, rke_all_p1=rke_all_p1, lge_all_p1=lge_all_p1,
                          sce_pg=S["sce"].n, scs_pg=S["scs"].n):
                    t.wait_ge(S["wqld%d" % (gb % 2)].h, 16 * (gb // 2 + 1))
                    if gb in (12, 13):
                        t.wait_ge(S["sce"].h, sce_pg)
                        t.wait_ge(S["scs"].h, scs_pg)
                    if gb == 0:
                        t.wait_ge(S["rke"].h, rke_all_p1)
                    if gb == 1:
                        t.wait_ge(S["lge"].h, lge_all_p1)
                    if gb >= 2:
                        t.wait_ge(S["qke"].h, gb - 1)
                    for f in range(NF):
                        for tt in range(4):
                            ins = t.matmul(pb[base + tt][:, :], wqkvb[:, gb % 2, f, :],
                                           A[:, f, tt * 512:(tt + 1) * 512],
                                           start=(f == 0), stop=(f == NF - 1))
                    ins.then_inc(S["qkm"].h, 1)
                P.t(f_qmm)

                qe = S["qke"].inc()
                dest = qarena[o]

                def f_qev(v, gb=gb, i=i, o=o, base=base, dest=dest, qk=qk,
                          rke_all_p1=rke_all_p1, lge_all_p1=lge_all_p1, vch_p1=vch_p1):
                    v.wait_ge(S["qkm"].h, qk)
                    if gb <= 2:
                        v.wait_ge(S["tp"].h, NT)
                        v.wait_ge(S["fxt"].h, 2)
                        v.wait_ge(S["lge"].h, lge_all_p1)
                        v.wait_ge(S["vch"].h, vch_p1)
                        v.wait_ge(S["rke"].h, rke_all_p1)
                    for tt in range(4):
                        ins = v.tensor_scalar(dest[:, i, tt * 512:(tt + 1) * 512], pb[base + tt][:, :],
                                              bqkvc[:, gb:gb + 1], None, ALU.add)
                    ins.then_inc(S["qke"].h, 1)
                P.v(f_qev)

            if stop_qkv and hg == 0:
                break

            sv = S["sce"].inc()

            def f_srst(v, hg=hg, qe_all=S["qke"].n, pvsb=S_bv.get(hg * NCH - 1)):
                v.wait_ge(S["qke"].h, qe_all)
                if pvsb is not None:
                    v.wait_ge(S["scs"].h, pvsb)
                v.memset(S_f[:, :, :], 0.0).then_inc(S["sce"].h, 1)
            P.v(f_srst)

            for c in range(NCH):
                par = c % 2
                gc = hg * NCH + c
                bK = [0, 5][par]
                bV = [1, 6][par]
                bS = [2, 7][par]

                kt = S["scn"].inc()
                vt = S["scn"].inc()

                def f_kt(t, hg=hg, c=c, par=par, bK=bK, bV=bV, qe_all=S["qke"].n,
                         w_ke=SCw["ke"].get(hg * NCH + c - 2), w_ve=SCw["ve"].get(hg * NCH + c - 2)):
                    if c <= 1:
                        t.wait_ge(S["qke"].h, qe_all)
                    if w_ke is not None:
                        t.wait_ge(S["sce"].h, w_ke)
                    if w_ve is not None:
                        t.wait_ge(S["scs"].h, w_ve)
                    for i in range(4):
                        ins = t.matmul(pb[bK][:, i * 128:(i + 1) * 128], Bk[:, i, c * 128:(c + 1) * 128],
                                       ident[:, :], start=True, stop=True)
                    for i in range(4):
                        ins2 = t.matmul(pb[bV][:, i * 128:(i + 1) * 128], Bv[:, i, c * 128:(c + 1) * 128],
                                        ident[:, :], start=True, stop=True)
                    ins.then_inc(S["scn"].h, 1)
                    ins2.then_inc(S["scn"].h, 1)
                P.t(f_kt)

                ke = S["sce"].inc()
                SCw["ke"][gc] = ke

                def f_ke(v, hg=hg, c=c, par=par, bK=bK, kt=kt, w_dm=SCw["dm"].get(hg * NCH + c - 2)):
                    v.wait_ge(S["scn"].h, kt)
                    if w_dm is not None:
                        v.wait_ge(S["scn"].h, w_dm)
                    for i in range(4):
                        ins = v.tensor_scalar(kT_s[:, par, i, :], pb[bK][:, i * 128:(i + 1) * 128],
                                              dkc[:, 4 * hg + i:4 * hg + i + 1], None, ALU.mult)
                    ins.then_inc(S["sce"].h, 1)
                P.v(f_ke)

                ve = S["scs"].inc()
                SCw["ve"][gc] = ve

                def f_ve(s, c=c, par=par, bV=bV, vt=vt, w_dm=SCw["dm"].get(hg * NCH + c - 2),
                         w_om=SCw["om"].get(hg * NCH + c - 2)):
                    s.wait_ge(S["scn"].h, vt)
                    if w_dm is not None:
                        s.wait_ge(S["scn"].h, w_dm)
                    if w_om is not None:
                        s.wait_ge(S["scn"].h, w_om)
                    s.copy(vT_s[:, par, :, :], pb[bV][:, :].rearrange("p (i t) -> p i t", i=4)).then_inc(S["scs"].h, 1)
                P.s(f_ve)

                sc_mm = S["scn"].inc()

                def f_sc(t, hg=hg, c=c, bS=bS, w_me=SCw["me"].get(hg * NCH + c - 2)):
                    if w_me is not None:
                        t.wait_ge(S["sce"].h, w_me)
                    for i in range(4):
                        ins = t.matmul(pb[bS][:, i * 128:(i + 1) * 128], Bk[:, i, c * 128:(c + 1) * 128],
                                       Bq[:, i, c * 128:(c + 1) * 128], start=True, stop=True)
                    ins.then_inc(S["scn"].h, 1)
                P.t(f_sc)

                me = S["sce"].inc()
                SCw["me"][gc] = me

                def f_me(v, hg=hg, c=c, par=par, bS=bS, sc_mm=sc_mm, w_om=SCw["om"].get(hg * NCH + c - 2)):
                    v.wait_ge(S["scn"].h, sc_mm)
                    if w_om is not None:
                        v.wait_ge(S["scn"].h, w_om)
                    v.tensor_tensor(sc_s[:, par, :, :], pb[bS][:, :].rearrange("p (i t) -> p i t", i=4),
                                    dmask[:, 4 * hg:4 * hg + 4, :], ALU.mult).then_inc(S["sce"].h, 1)
                P.v(f_me)

                qs = S["sce"].inc()

                def f_qs(v, hg=hg, c=c, sc_mm=sc_mm):
                    v.wait_ge(S["scn"].h, sc_mm)
                    v.tensor_tensor(Bq[:, :, c * 128:(c + 1) * 128], Bq[:, :, c * 128:(c + 1) * 128],
                                    dqb[:, 4 * hg:4 * hg + 4, :], ALU.mult).then_inc(S["sce"].h, 1)
                P.v(f_qs)

                om = S["scn"].inc()
                SCw["om"][gc] = om

                def f_om(t, hg=hg, c=c, par=par, me=me, qs=qs, ve=ve, gc=gc,
                         w_ae=SCw["ae"].get(hg * NCH + c - 1)):
                    t.wait_ge(S["sce"].h, me)
                    t.wait_ge(S["scs"].h, ve)
                    if w_ae is not None:
                        t.wait_ge(S["scs"].h, w_ae)
                    if c > 0:
                        t.wait_ge(S["sce"].h, qs)
                        t.wait_ge(S["scs"].h, S_bv[gc - 1])
                    for i in range(4):
                        ins = t.matmul(pb[3][:, i * 128:(i + 1) * 128], vT_s[:, par, i, :], sc_s[:, par, i, :],
                                       start=True, stop=(c == 0))
                        if c > 0:
                            ins = t.matmul(pb[3][:, i * 128:(i + 1) * 128], S_b[:, i, :],
                                           Bq[:, i, c * 128:(c + 1) * 128], start=False, stop=True)
                    ins.then_inc(S["scn"].h, 1)
                P.t(f_om)

                dm = S["scn"].inc()
                SCw["dm"][gc] = dm

                def f_dm(t, hg=hg, c=c, par=par, ke=ke, ve=ve, w_su=SCw["su"].get(hg * NCH + c - 1)):
                    t.wait_ge(S["sce"].h, ke)
                    t.wait_ge(S["scs"].h, ve)
                    if w_su is not None:
                        t.wait_ge(S["sce"].h, w_su)
                    for i in range(4):
                        ins = t.matmul(pb[4][:, i * 128:(i + 1) * 128], kT_s[:, par, i, :], vT_s[:, par, i, :],
                                       start=True, stop=True)
                    ins.then_inc(S["scn"].h, 1)
                P.t(f_dm)

                su = S["sce"].inc()
                SCw["su"][gc] = su

                def f_su(v, hg=hg, c=c, dm=dm, pvsb=S_bv.get(hg * NCH + c - 1)):
                    v.wait_ge(S["scn"].h, dm)
                    if pvsb is not None:
                        v.wait_ge(S["scs"].h, pvsb)
                    for i in range(4):
                        ins = v.scalar_tensor_tensor(S_f[:, i, :], S_f[:, i, :],
                                                     lamc[:, 4 * hg + i:4 * hg + i + 1],
                                                     pb[4][:, i * 128:(i + 1) * 128], ALU.mult, ALU.add)
                    ins.then_inc(S["sce"].h, 1)
                P.v(f_su)

                sb = S["scs"].inc()
                S_bv[gc] = sb

                def f_sb(s, c=c, su=su):
                    s.wait_ge(S["sce"].h, su)
                    s.copy(S_b[:, :, :], S_f[:, :, :]).then_inc(S["scs"].h, 1)
                P.s(f_sb)

                ae = S["scs"].inc()
                SCw["ae"][gc] = ae

                def f_ae(s, c=c, par=par, om=om, gc=gc):
                    s.wait_ge(S["scn"].h, om)
                    if gc >= 2:
                        s.wait_ge(S["cc1st%d" % par].h, (gc // 2) * 16)
                    s.copy(pstage[:, par, :, :], pb[3][:, :].rearrange("p (i t) -> p i t", i=4)).then_inc(S["scs"].h, 1)
                P.s(f_ae)

                S["cc1st%d" % par].inc(16)

                def f_cst(s, hg=hg, c=c, par=par, ae=ae):
                    s.wait_ge(S["scs"].h, ae)
                    dst = [dcc1inA, dcc1inB][hg]
                    s.dma_start(
                        out=dst[0:512, c * 128:(c + 1) * 128].rearrange("(i d) t -> d i t", i=4),
                        in_=pstage[:, par, :, :]).then_inc(S["cc1st%d" % par].h, 16)
                P.s(f_cst)

            S["cc"].inc()

            def f_ccg(g, hg=hg, p0=S["cc1st0"].n, p1=S["cc1st1"].n):
                g.wait_ge(S["cc1st0"].h, p0)
                g.wait_ge(S["cc1st1"].h, p1)
                g.collective_compute(
                    "AllGather", ALU.bypass, replica_groups=[list(range(8))],
                    ins=[[dcc1inA, dcc1inB][hg][:, :]], outs=[[dcc1outA, dcc1outB][hg][:, :]],
                ).then_inc(S["cc"].h, 1)
            P.g(f_ccg)

        if "attn" in debug and not stop_qkv:
            cc1_all = S["cc1st0"].n
            qke_all = S["qke"].n

            def f_dbga(g, cc1_all=cc1_all, qke_all=qke_all):
                g.wait_ge(S["cc1st0"].h, cc1_all)
                g.wait_ge(S["cc1st1"].h, S["cc1st1"].n)
                g.wait_ge(S["qke"].h, qke_all)
                g.dma_start(out=dbg["d_cc1"][:, :], in_=dcc1in[:, :]).then_inc(S["fin"].h, 16)
                for i in range(4):
                    g.dma_start(out=dbg["d_qf"][:, i * SEQ:(i + 1) * SEQ], in_=Bq[:, i, :]).then_inc(S["fin"].h, 16)
                    g.dma_start(out=dbg["d_kf"][:, i * SEQ:(i + 1) * SEQ], in_=Bk[:, i, :]).then_inc(S["fin"].h, 16)
                    g.dma_start(out=dbg["d_vf"][:, i * SEQ:(i + 1) * SEQ], in_=Bv[:, i, :]).then_inc(S["fin"].h, 16)
            for _ in range(13):
                S["fin"].inc(16)
            P.g(f_dbga)

    def p3():
        rkm_p1 = S["rkm"].n
        qkm_all = S["qkm"].n
        attn_g = A[:, :, 0:1024]
        h_reg = A[:, :, 1024:1536]
        wout_reg = A[:, :, 1536:2048]

        def hslice(dc):
            return A[:, dc % 16, 1024 + (dc // 16) * 256: 1024 + (dc // 16) * 256 + 256]


        S["cst"].inc(16)

        def f_brow(g, rkm_p1=rkm_p1):
            g.wait_ge(S["rkm"].h, rkm_p1)
            g.dma_start(out=brow[:, :], in_=dbrow[:, :]).then_inc(S["cst"].h, 16)
        P.g(f_brow)

        viewA = dcc1outA.rearrange("r (a c) -> (r a) c", a=2)
        viewB = dcc1outB.rearrange("r (a c) -> (r a) c", a=2)
        order = [g_ for g_ in range(16) if g_ % 8 < 4] + [g_ for g_ in range(16) if g_ % 8 >= 4]
        first_b = True
        for idx, g_ in enumerate(order):
            S["ccrb"].inc(16)
            isB = g_ % 8 >= 4

            def f_rb1(g, g_=g_, idx=idx, isB=isB, qkm_all=qkm_all, first_b=first_b):
                if idx == 0:
                    g.wait_ge(S["cc"].h, 1)
                    g.wait_ge(S["qkm"].h, qkm_all)
                if isB and first_b:
                    g.wait_ge(S["cc"].h, 2)
                g.indirect_dma_start(
                    out=attn_g[:, g_, :], out_offset=None,
                    in_=viewB if isB else viewA, in_offset=IndirectOffsetOnAxis(cc1i_u[:, g_:g_ + 1], 0),
                ).then_inc(S["ccrb"].h, 16)
            if isB and first_b:
                first_b = False
            P.g(f_rb1)
        ccrb1 = S["ccrb"].n

        def w2slot(i):
            return w2b[:, i % 2, :, :] if i % 4 < 2 else wqkvb[:, i % 2, 0:8, :]
        for w2i in range(64):
            S["w2ld%d" % (w2i % 4)].inc(16)

            def f_w2(y, w2i=w2i, pv=16 * ((32 + w2i) // 4 + 1)):
                y.wait_ge(S["ps%d" % ((32 + w2i) % 4)].h, pv)
                if w2i in (2, 3):
                    y.wait_ge(S["hm"].h, 32)
                if w2i >= 4:
                    y.wait_ge(S["ym"].h, w2i - 3)
                y.dma_start(out=w2slot(w2i),
                            in_=dW2bf[w2i, :, :, :]).then_inc(S["w2ld%d" % (w2i % 4)].h, 16)
            P.y(f_w2)

        xn_view0 = dxn.rearrange("t (a c) -> (t a) c", a=4)
        pd_done0 = shared["pd_done"]
        xnv_pre = {}
        for j0_ in range(2):
            xnv_pre[j0_] = S["hld%d" % (j0_ % 2)].inc(16)

            def f_xnl0(g, tc=j0_, xj=(j0_ % 2) * 2, j=j0_, pdd=pd_done0):
                if j == 0:
                    g.wait_ge(S["xnst0"].h, 16 * 8)
                    g.wait_ge(S["xnst1"].h, 16 * 8)
                for h_, n_ in pdd:
                    g.wait_ge(h_, n_)
                g.indirect_dma_start(
                    out=asmb[:, xj + 1, 0:512], out_offset=None,
                    in_=xn_view0, in_offset=IndirectOffsetOnAxis(xni_u[:, tc * 4:tc * 4 + 1], 0),
                ).then_inc(S["hld%d" % (j % 2)].h, 16)
            P.g(f_xnl0)

        asml_b = S["asml"].n
        cc2v = []
        for half in range(2):
            for sh in range(2):
                S["cc2st%d" % half].inc(16)

                def f_c2st(g, half=half, sh=sh, av=asml_b + 16 * (half + 1)):
                    g.wait_ge(S["asml"].h, av)
                    g.dma_start(out=dcc2inH[half][sh * 128:(sh + 1) * 128, :],
                                in_=[C0, C1][sh][:, half * 1024:(half + 1) * 1024]
                                ).then_inc(S["cc2st%d" % half].h, 16)
                P.g(f_c2st)
            cc2v.append(S["cc"].inc())

            def f_cc2(g, half=half, stv=S["cc2st%d" % half].n):
                g.wait_ge(S["cc2st%d" % half].h, stv)
                g.collective_compute(
                    "AllGather", ALU.bypass, replica_groups=[list(range(8))],
                    ins=[dcc2inH[half][:, :]], outs=[dcc2outH[half][:, :]],
                ).then_inc(S["cc"].h, 1)
            P.g(f_cc2)

        fxe_all = S["rke"].n
        scee = S["sce"].n
        for ob in range(32):
            S["hm"].inc()

            def f_hmm(t, ob=ob, fxe_all=fxe_all, scee=scee):
                t.wait_ge(S["w1ld%d" % (ob % 4)].h, 16 * (ob // 4 + 1))
                if ob == 0:
                    t.wait_ge(S["sce"].h, scee)
                if ob >= 4:
                    t.wait_ge(S["he"].h, ob - 3)
                src = w1b if ob % 4 < 2 else wqkvb
                for f in range(NF):
                    ins = t.matmul(pb[4 + ob % 4][:, 0:256], src[:, ob % 2, f, :], fx_fm[:, f, :],
                                   start=(f == 0), stop=(f == NF - 1))
                ins.then_inc(S["hm"].h, 1)
            P.t(f_hmm)

            S["he"].inc()

            def f_gel(s, ob=ob, hv=S["hm"].n, qkm_all=qkm_all):
                s.wait_ge(S["hm"].h, hv)
                if ob == 0:
                    s.wait_ge(S["qkm"].h, qkm_all)
                s.activation(hslice(ob), pb[4 + ob % 4][:, 0:256], AF.Gelu,
                             bias=b1c[:, ob:ob + 1], scale=1.0).then_inc(S["he"].h, 1)
            P.s(f_gel)

            if ob + 4 < 32:
                ob2 = ob + 4
                S["w1ld%d" % (ob2 % 4)].inc(16)

                def f_w1(s, ob2=ob2, pv=16 * (ob2 // 4 + 1)):
                    s.wait_ge(S["ps%d" % (ob2 % 4)].h, pv)
                    s.wait_ge(S["hm"].h, ob2 - 3)
                    dst = w1b if ob2 % 4 < 2 else wqkvb
                    s.dma_start(out=dst[:, ob2 % 2, :, :],
                                in_=dW1bf[ob2, :, :, :]).then_inc(S["w1ld%d" % (ob2 % 4)].h, 16)
                P.s(f_w1)

        he_all = S["he"].n
        for mb in range(16):
            for sg in range(4):
                w2i = mb * 4 + sg
                S["ym"].inc()

                def f_ymm(t, mb=mb, sg=sg, w2i=w2i, he_all=he_all):
                    t.wait_ge(S["w2ld%d" % (w2i % 4)].h, 16 * (w2i // 4 + 1))
                    if mb == 0 and sg == 0:
                        t.wait_ge(S["he"].h, he_all)
                    if sg == 0 and mb >= 2:
                        t.wait_ge(S["ye"].h, mb - 1)
                    w2s = w2slot(w2i)
                    for d8 in range(8):
                        dc = sg * 8 + d8
                        ins = t.matmul(pb[mb % 2][:, 0:256], w2s[:, d8, :],
                                       hslice(dc), start=(dc == 0), stop=(dc == 31))
                    ins.then_inc(S["ym"].h, 1)
                P.t(f_ymm)

            S["ye"].inc()

            def f_yfe(v, mb=mb, ymv=S["ym"].n, trv_prev=S["opm"].n):
                v.wait_ge(S["ym"].h, ymv)
                if mb >= 1:
                    v.wait_ge(S["opm"].h, trv_prev)
                ins = v.tensor_scalar(yfmt[:, :], pb[mb % 2][:, 0:256], b2c[:, mb:mb + 1], None, ALU.add)
                ins.then_inc(S["ye"].h, 1)
            P.v(f_yfe)

            S["opm"].inc()

            def f_ytr(t, mb=mb, yev=S["ye"].n):
                t.wait_ge(S["ye"].h, yev)
                if mb >= 2:
                    t.wait_ge(S["asml"].h, 2 * (mb - 1))
                for sh in range(2):
                    ins = t.matmul(pb[2 + mb % 2][:, sh * 128:(sh + 1) * 128],
                                   yfmt[:, sh * 128:(sh + 1) * 128], identf[:, :],
                                   start=True, stop=True)
                ins.then_inc(S["opm"].h, 1)
            P.t(f_ytr)

            S["asml"].inc(2)

            def f_yte(v, mb=mb, trv=S["opm"].n, scn_p1=S["scn"].n if False else NT, lge_p=S["lge"].n):
                v.wait_ge(S["opm"].h, trv)
                if mb == 0:
                    v.wait_ge(S["scn"].h, 2 * NT)
                    v.wait_ge(S["lge"].h, lge_p)
                for sh in range(2):
                    ins = v.tensor_copy([C0, C1][sh][:, mb * 128:(mb + 1) * 128],
                                        pb[2 + mb % 2][:, sh * 128:(sh + 1) * 128])
                ins.then_inc(S["asml"].h, 2)
            P.v(f_yte)

        xn_view = dxn.rearrange("t (a c) -> (t a) c", a=4)
        sce_scan = S["sce"].n
        scs_scan = S["scs"].n
        pd_done = shared["pd_done"]
        for mt in range(4):
            for q in range(4):
                pi = 96 + mt * 4 + q
                S["wold"].inc(16)

                def f_wo(s, mt=mt, q=q, pi=pi, qkm_all=qkm_all, op_prev=S["opm"].n,
                         ppv=16 * (pi // 4 + 1)):
                    s.wait_ge(S["ps%d" % (pi % 4)].h, ppv)
                    if q == 0:
                        if mt == 0:
                            s.wait_ge(S["qkm"].h, qkm_all)
                        else:
                            s.wait_ge(S["opm"].h, op_prev)
                    s.dma_start(out=wout_reg[:, q * 4:(q + 1) * 4, :],
                                in_=dwoutbf[mt * 4 + q, :, :, :]).then_inc(S["wold"].h, 16)
                P.s(f_wo)

            for tc in range(8):
                j = mt * 8 + tc
                pj = j % 2
                xj = pj * 2
                xv = S["hst%d" % pj].inc(16)

                def f_xl(y, mt=mt, tc=tc, xj=xj, j=j, pj=pj, stv=S["ast%d" % pj].n,
                         pdd=pd_done):
                    if j < 2:
                        for h_, n_ in pdd:
                            y.wait_ge(h_, n_)
                    if j >= 2:
                        y.wait_ge(S["ast%d" % pj].h, stv)
                    y.dma_start(out=asmb[:, xj, 0:512],
                                in_=dxown[tc * 128:(tc + 1) * 128, mt * 512:(mt + 1) * 512]
                                ).then_inc(S["hst%d" % pj].h, 16)
                P.y(f_xl)

                if j < 2:
                    xnv = xnv_pre[j]
                else:
                    xnv = S["hld%d" % pj].inc(16)

                    def f_xnl(g, mt=mt, tc=tc, xj=xj, j=j, pj=pj, stv=S["ast%d" % pj].n):
                        g.wait_ge(S["ast%d" % pj].h, stv)
                        g.indirect_dma_start(
                            out=asmb[:, xj + 1, 0:512], out_offset=None,
                            in_=xn_view, in_offset=IndirectOffsetOnAxis(xni_u[:, tc * 4 + mt:tc * 4 + mt + 1], 0),
                        ).then_inc(S["hld%d" % pj].h, 16)
                    P.g(f_xnl)

                opv = S["opm"].inc()

                def f_op(t, mt=mt, tc=tc, pj=pj, cst_b=S["cst"].n, wo_v=S["wold"].n,
                         asml_prev=S["asml"].n, sce_scan=sce_scan, scs_scan=scs_scan):
                    t.wait_ge(S["wold"].h, wo_v)
                    if mt == 0 and tc == 0:
                        t.wait_ge(S["cst"].h, cst_b)
                        t.wait_ge(S["ccrb"].h, ccrb1)
                        t.wait_ge(S["sce"].h, sce_scan)
                        t.wait_ge(S["scs"].h, scs_scan)
                    if mt * 8 + tc >= 2:
                        t.wait_ge(S["asml"].h, asml_prev - 1)
                    bank = 4 + pj
                    for g_ in range(16):
                        ins = t.matmul(pb[bank][:, :], attn_g[:, g_, tc * 128:(tc + 1) * 128],
                                       wout_reg[:, g_, :], start=(g_ == 0), stop=False)
                    t.matmul(pb[bank][:, :], onesr[0:1, 0:128],
                             brow[0:1, mt * 512:(mt + 1) * 512],
                             start=False, stop=True).then_inc(S["opm"].h, 1)
                P.t(f_op)

                av = S["asml"].inc()

                def f_asm(v, mt=mt, tc=tc, xj=xj, pj=pj, xv=xv, xnv=xnv, opv2=S["opm"].n):
                    v.wait_ge(S["hst%d" % pj].h, xv)
                    v.wait_ge(S["hld%d" % pj].h, xnv)
                    v.wait_ge(S["opm"].h, opv2)
                    v.tensor_add(asmb[:, xj, 0:512], asmb[:, xj, 0:512], asmb[:, xj + 1, 0:512])
                    v.tensor_add(asmb[:, xj, 0:512], asmb[:, xj, 0:512],
                                 pb[4 + pj][:, :]).then_inc(S["asml"].h, 1)
                P.v(f_asm)

                S["ast%d" % pj].inc(16)

                def f_st(y, mt=mt, tc=tc, xj=xj, pj=pj, av=av):
                    y.wait_ge(S["asml"].h, av)
                    y.dma_start(out=dy[tc * 128:(tc + 1) * 128, mt * 512:(mt + 1) * 512],
                                in_=asmb[:, xj, 0:512]).then_inc(S["ast%d" % pj].h, 16)
                P.y(f_st)

        opm_y = S["opm"].n - 32
        for sh in range(2):
            for half in range(2):
                S["ccr2"].inc(16)

                def f_rb2(g, sh=sh, half=half, ypv=S["ypost"].n, lge_p=S["lge"].n,
                          ccv=cc2v[half], opm_y=opm_y):
                    if sh == 0:
                        g.wait_ge(S["lge"].h, lge_p)
                        if half == 0:
                            g.wait_ge(S["opm"].h, opm_y)
                    elif half == 0:
                        g.wait_ge(S["ypost"].h, ypv)
                    g.wait_ge(S["cc"].h, ccv)
                    g.indirect_dma_start(
                        out=C2[:, half * 1024:(half + 1) * 1024], out_offset=None,
                        in_=dcc2outH[half][:, :],
                        in_offset=IndirectOffsetOnAxis(cc2i_u[:, 2 + sh:3 + sh], 0),
                    ).then_inc(S["ccr2"].h, 16)
                P.g(f_rb2)

            S["ypost"].inc()

            def f_yadd(v, sh=sh, rbv=S["ccr2"].n):
                v.wait_ge(S["ccr2"].h, rbv)
                Csh = [C0, C1][sh]
                edge(v, v.tensor_add(Csh[:, :], Csh[:, :], C2[:, :]), S["vch"])
                v.tensor_scalar(Csh[:, :], Csh[:, :], rw[:, sh:sh + 1], None, ALU.mult).then_inc(S["ypost"].h, 1)
            P.v(f_yadd)


        ast_all0 = S["ast0"].n
        ast_all1 = S["ast1"].n
        ypost_all = S["ypost"].n
        for sh in range(2):
            S["sca"].inc(16)

            def f_scat(g, sh=sh, ast_all0=ast_all0, ast_all1=ast_all1, ypost_all=ypost_all):
                if sh == 0:
                    g.wait_ge(S["ast0"].h, ast_all0)
                    g.wait_ge(S["ast1"].h, ast_all1)
                    g.wait_ge(S["ypost"].h, ypost_all)
                g.indirect_dma_start(
                    out=dy[:, :], out_offset=IndirectOffsetOnAxis(sci_u[:, sh:sh + 1], 0),
                    in_=[C0, C1][sh][:, :], in_offset=None,
                    compute_op=ALU.add,
                ).then_inc(S["sca"].h, 16)
            P.g(f_scat)

        if "ffn" in debug:
            he_a = S["he"].n
            ysp = S["yst"].n
            rb_a = S["ccrb"].n

            def f_dbgf(g, he_a=he_a, ysp=ysp, rb_a=rb_a):
                g.wait_ge(S["he"].h, he_a)
                g.wait_ge(S["yst"].h, ysp)
                g.wait_ge(S["ccrb"].h, rb_a)
                for dc in range(32):
                    g.dma_start(out=dbg["d_h"][:, dc * KSEL:(dc + 1) * KSEL], in_=hslice(dc)).then_inc(S["fin"].h, 16)
                g.dma_start(out=dbg["d_ysc"][:, :], in_=dysc[:, :]).then_inc(S["fin"].h, 16)
                for g_ in range(16):
                    g.dma_start(out=dbg["d_aog"][:, g_ * 1024:(g_ + 1) * 1024], in_=attn_g[:, g_, :]).then_inc(S["fin"].h, 16)
            for _ in range(32 + 1 + 16):
                S["fin"].inc(16)
            P.g(f_dbgf)

    p0()
    if upto >= 1:
        p1()
    if upto >= 2:
        p2()
    if upto >= 3:
        p3()

    sem_totals = [(sem.h, sem.n) for sem in S.values() if sem.n > 0]

    def f_fin(g, sem_totals=sem_totals):
        for h, n in sem_totals:
            g.wait_ge(h, n)
    P.g(f_fin)

    with nc.Block() as block:
        @block.tensor
        def _(t):
            for fn in P.ops["tensor"]:
                fn(t)

        @block.vector
        def _(v):
            vv = Chained(v, S["vch"])
            for fn in P.ops["vector"]:
                fn(vv)

        @block.scalar
        def _(s):
            ss = Chained(s, S["sch"])
            for fn in P.ops["scalar"]:
                fn(ss)

        @block.sync
        def _(y):
            for fn in P.ops["sync"]:
                fn(y)

        @block.gpsimd
        def _(g):
            for fn in P.ops["gpsimd"]:
                fn(g)

    return nc, dict(S=S)



def _prep_in_maps(L0_attn_x, L0_attn_w_qkv, L0_attn_b_qkv, L0_attn_w_out, L0_attn_b_out,
                  L0_ffn_Wr, L0_ffn_br, L0_ffn_W1, L0_ffn_b1, L0_ffn_W2, L0_ffn_b2):
    f32 = np.float32
    x = np.asarray(L0_attn_x, f32)
    wqkv = np.asarray(L0_attn_w_qkv, f32)
    bqkv = np.asarray(L0_attn_b_qkv, f32)
    wout = np.asarray(L0_attn_w_out, f32)
    bout = np.asarray(L0_attn_b_out, f32)
    Wr = np.asarray(L0_ffn_Wr, f32)
    br = np.asarray(L0_ffn_br, f32)
    W1 = np.asarray(L0_ffn_W1, f32)
    b1 = np.asarray(L0_ffn_b1, f32)
    W2 = np.asarray(L0_ffn_W2, f32)
    b2 = np.asarray(L0_ffn_b2, f32)

    lam = np.exp(-_slopes(16))
    pos = np.arange(128, dtype=np.float64)
    ident = np.eye(128, dtype=f32)
    tri = (np.arange(128)[:, None] < np.arange(128)[None, :]).astype(f32)
    onesm = np.ones((128, 128), f32)
    iob = np.broadcast_to(np.arange(256, dtype=f32)[None, :], (128, 256)).copy()
    iop = np.arange(128, dtype=f32).reshape(128, 1).copy()
    browv = bout.reshape(1, DM).copy()
    onesrv = np.zeros((1, 256), f32)
    onesrv[0, 0:128] = 1.0
    onesrv[0, 128] = br[0]

    wT = wout.T.copy()
    woutT = np.ascontiguousarray(wT.reshape(16, 128, 4, 512).transpose(2, 1, 0, 3))

    in_maps = []
    for c in range(8):
        s, h = c // 2, c % 2
        heads = np.arange(8 * h, 8 * h + 8)
        blocks = np.empty((24, 128, NF, 128), f32)
        bq = np.empty((128, 24), f32)
        for hh in range(8):
            g = heads[hh]
            for o, base in enumerate((0, DM, 2 * DM)):
                Wsub = wqkv[base + g * 128: base + (g + 1) * 128]
                blocks[hh * 3 + o] = Wsub.reshape(128, NF, 128).transpose(2, 1, 0)
                bq[:, hh * 3 + o] = bqkv[base + g * 128: base + (g + 1) * 128]
        W1sub = W1[:, h * DFFH:(h + 1) * DFFH]
        W1t = np.empty((32, 128, NF, 128), f32)
        for b in range(32):
            W1t[b] = W1sub[:, b * 128:(b + 1) * 128].reshape(NF, 128, 128).transpose(1, 0, 2)
        b1h = np.ascontiguousarray(b1[h * DFFH:(h + 1) * DFFH].reshape(32, 128).T)
        W2sub = W2[h * DFFH:(h + 1) * DFFH, :]
        W2cm = np.ascontiguousarray(W2sub.reshape(32, 128, 16, 128).transpose(2, 1, 0, 3))
        b2cv = np.ascontiguousarray(b2.reshape(16, 128).T) if h == 0 else np.zeros((128, 16), f32)
        wrcv = np.ascontiguousarray(Wr[:, 0].reshape(NF, 128).T)

        lam_h = lam[heads]
        lamcv = np.broadcast_to((lam_h ** 128.0).astype(f32)[None, :], (128, HLOC)).copy()
        dmaskT = np.zeros((128, HLOC, 128), f32)
        dqbv = np.zeros((128, HLOC, 128), f32)
        dkv = np.zeros((128, HLOC), f32)
        for hh in range(8):
            lh = lam_h[hh]
            diff = pos[None, :] - pos[:, None]
            m = np.where(diff >= 0, lh ** np.maximum(diff, 0.0), 0.0)
            dmaskT[:, hh, :] = m.astype(f32)
            dqbv[:, hh, :] = np.broadcast_to((lh ** (pos + 1.0)).astype(f32)[None, :], (128, 128))
            dkv[:, hh] = (lh ** (127.0 - pos)).astype(f32)

        pair = s * 2 + (1 - h)
        cc1idx = np.zeros((128, 16), f32)
        for g in range(16):
            owner = s * 2 + (g // 8)
            lh = g % 8
            r = owner * 512 + (lh % 4) * 128 + np.arange(128)
            cc1idx[:, g] = r * 2 + h
        cc2idx = np.zeros((128, 4), f32)
        for sh in range(2):
            cc2idx[:, sh] = c * KSEL + sh * 128 + np.arange(128)
            cc2idx[:, 2 + sh] = pair * KSEL + sh * 128 + np.arange(128)

        in_maps.append(dict(
            x=np.ascontiguousarray(x[s]),
            wqkvT=blocks, bqkv=bq, woutT=woutT,
            W1t=W1t, b1h=b1h, W2cm=W2cm, b2c=b2cv, Wrc=wrcv,
            boutr=browv, onesr=onesrv,
            identc=ident, tric=tri, onesmc=onesm, iobc=iob, iopc=iop,
            onescolc=np.ones((128, 1), f32),
            epscolc=np.full((128, 1), EPS, f32),
            dmaskc=dmaskT, dqbc=dqbv, dkc=dkv, lamcc=lamcv,
            cc1idx=cc1idx, cc2idx=cc2idx,
            hofsc=np.array([[h * 1024.0]], f32),
            xown=np.ascontiguousarray(x[s, h * OWN:(h + 1) * OWN]),
            xnidx=np.ascontiguousarray(
                ((h * OWN + (np.arange(32)[None, :] // 4) * 128 + np.arange(128)[:, None]) * 4
                 + (np.arange(32)[None, :] % 4)).astype(f32)),
        ))
    return in_maps


_BUILD_CACHE = {}


def _get_program(upto=99, debug=()):
    key = (upto, tuple(debug))
    if key not in _BUILD_CACHE:
        _BUILD_CACHE[key] = build_program(upto=upto, debug=debug)
    return _BUILD_CACHE[key]


def kernel(**inputs):
    in_maps = _prep_in_maps(**inputs)
    nc, meta = _get_program()
    res = run_bass_kernel_spmd(nc, in_maps, core_ids=list(range(8)))
    out = np.empty((4, SEQ, DM), np.float32)
    for c in range(8):
        s, h = c // 2, c % 2
        out[s, h * OWN:(h + 1) * OWN] = res.results[c]["y"][0:OWN]
    return out
